# version 13
# baseline (speedup 1.0000x reference)
import contextlib
import numpy as np
import ml_dtypes
import concourse.bass as bass
import concourse.mybir as mybir
from concourse.bass_utils import run_bass_kernel_spmd

F32 = mybir.dt.float32
BF16 = mybir.dt.bfloat16
AF = mybir.ActivationFunctionType
ALU = mybir.AluOpType
AX = mybir.AxisListType.X

D = 1024
DIN = 2048
DST = 128
NH = 32
PH = 64
CONVD = DIN + 2 * DST
DINPROJ = 2 * DIN + 2 * DST + NH
EXPT_SERIAL = False
FF = 4096
NKD = 8
NFIN = 16
NFCV = 18
NFF = 32

TOK = 2048
BLK = 512
NBLK = 4
C = 128
NCB = 4
NTT = 4

EPS_LN = 1e-5
EPS_OUT = 1.1920929e-07
EPS_GATE = 1e-05
BETA = 0.9
MASKVAL = -1e30


def build_program(dbg=(), stop_after=None):
    dbg = set(dbg)
    nc = bass.Bass("TRN2", target_bir_lowering=False, debug=False)

    xm = nc.dram_tensor("xm", [TOK, D], F32, kind="ExternalInput").ap()
    xp = nc.dram_tensor("xp", [TOK, D], F32, kind="ExternalInput").ap()
    w_in = nc.dram_tensor("w_in", [D, DINPROJ], F32, kind="ExternalInput").ap()
    conv_w = nc.dram_tensor("conv_w", [CONVD, 4], F32, kind="ExternalInput").ap()
    conv_b = nc.dram_tensor("conv_b", [CONVD, 1], F32, kind="ExternalInput").ap()
    dt_bias = nc.dram_tensor("dt_bias", [NH, 1], F32, kind="ExternalInput").ap()
    a_log = nc.dram_tensor("a_log", [NH, 1], F32, kind="ExternalInput").ap()
    d_skip_diag = nc.dram_tensor("d_skip_diag", [128, NFIN * 128], F32, kind="ExternalInput").ap()
    norm_w = nc.dram_tensor("norm_w", [DIN, 1], F32, kind="ExternalInput").ap()
    w_out = nc.dram_tensor("w_out", [DIN, D], F32, kind="ExternalInput").ap()
    w_fc = nc.dram_tensor("w_fc", [D, FF], F32, kind="ExternalInput").ap()
    w_proj = nc.dram_tensor("w_proj", [FF, D], F32, kind="ExternalInput").ap()
    Wfc = nc.dram_tensor("Wfc", [FF, D], F32, kind="ExternalInput").ap()
    Wproj = nc.dram_tensor("Wproj", [D, FF], F32, kind="ExternalInput").ap()
    triu_f = nc.dram_tensor("triu_f", [128, 128], F32, kind="ExternalInput").ap()
    triu_bf = nc.dram_tensor("triu_bf", [128, 128], BF16, kind="ExternalInput").ap()
    maskneg = nc.dram_tensor("maskneg", [128, 128], BF16, kind="ExternalInput").ap()
    ident_in = nc.dram_tensor("ident_in", [128, 128], BF16, kind="ExternalInput").ap()
    onesrow_in = nc.dram_tensor("onesrow_in", [1, 128], BF16, kind="ExternalInput").ap()
    eind_in = nc.dram_tensor("eind_in", [32, 2048], BF16, kind="ExternalInput").ap()
    eind2_in = nc.dram_tensor("eind2_in", [NH, DIN], BF16, kind="ExternalInput").ap()
    flag_in = nc.dram_tensor("flag_in", [128, 1], F32, kind="ExternalInput").ap()

    out = nc.dram_tensor("out", [TOK, D], F32, kind="ExternalOutput").ap()
    dr_h = nc.dram_tensor("dr_h", [TOK, D], F32, kind="Internal").ap()
    dr_x1 = nc.dram_tensor("dr_x1", [TOK, D], F32, kind="Internal").ap()
    dr_hn = nc.dram_tensor("dr_hn", [D, TOK], BF16, kind="Internal").ap()
    dr_woutbf = nc.dram_tensor("dr_woutbf", [DIN, D], BF16, kind="Internal").ap()
    dr_winzbf = nc.dram_tensor("dr_winzbf", [D, 2048], BF16, kind="Internal").ap()
    dr_wfcbf = nc.dram_tensor("dr_wfcbf", [D, FF], BF16, kind="Internal").ap()
    dr_wprojbf = nc.dram_tensor("dr_wprojbf", [FF, D], BF16, kind="Internal").ap()
    dr_Wfcbf = nc.dram_tensor("dr_Wfcbf", [FF, D], BF16, kind="Internal").ap()
    dr_Wprojbf = nc.dram_tensor("dr_Wprojbf", [D, FF], BF16, kind="Internal").ap()
    dr_flat = nc.dram_tensor("dr_flat", [2, NH * C], BF16, kind="Internal").ap()

    dbg_outs = {}
    def dbgt(name, shape, dtype=F32):
        if name in dbg:
            dbg_outs[name] = nc.dram_tensor("dbg_" + name, shape, dtype, kind="ExternalOutput").ap()
        return dbg_outs.get(name)

    d_u = dbgt("u", [NBLK, 128, NKD * BLK], BF16)
    d_xcv = dbgt("xcv", [NBLK, 128, NFCV * BLK], BF16)
    d_z = dbgt("z", [NBLK, 128, NTT * DIN], BF16)
    d_dt = dbgt("dt", [NBLK, NH, BLK])
    d_acum = dbgt("acum", [NBLK * NCB, NH, C])
    d_y = dbgt("y", [NBLK * NCB, 128, DIN])
    d_g = dbgt("g", [NBLK, 128, NFIN * BLK], BF16)
    d_h = dbgt("h", [TOK, D])
    d_st0 = dbgt("st0", [128, 6 * NTT])
    d_x00 = dbgt("x00", [128, 2 * D])
    d_eind = dbgt("eind", [33, 2048], BF16)
    d_state = dbgt("state", [NBLK * NCB, 128, DIN])
    d_pstate = dbgt("pstate", [128, DIN])
    d_x1 = dbgt("x1", [TOK, D])
    d_hn = dbgt("hn", [NKD * 128, TOK], BF16)
    d_g1 = dbgt("g1", [NBLK, 128, NFF * BLK], BF16)
    d_pm2 = dbgt("pm2", [NBLK, 2, 2, 128, D])

    es = contextlib.ExitStack()
    es1 = contextlib.ExitStack()
    sbt = lambda name, shape, dtype: es.enter_context(nc.sbuf_tensor(name, shape, dtype))
    sb1 = lambda name, shape, dtype: es1.enter_context(nc.sbuf_tensor(name, shape, dtype))
    pst = lambda name, shape, dtype: es.enter_context(nc.psum_tensor(name, shape, dtype))

    x0_rot = sbt("x0_rot", [128, 2 * D], F32)
    h_rot = sbt("h_rot", [128, 2 * D], F32)
    hn_rot = sbt("hn_rot", [128, 2 * D], BF16)
    hnf_rot = sbt("hnf_rot", [128, 2 * D], BF16)
    wo_rot = sbt("wo_rot", [128, 2 * D], BF16)
    stats = sbt("stats", [128, 6 * NTT], F32)
    smalls = sbt("smalls", [128, 28], F32)
    consts = sbt("consts", [128, 4], BF16)
    onesrow = sbt("onesrow", [1, 128], BF16)
    ident_s = sbt("ident_s", [128, 128], BF16)
    sc128 = sbt("sc128", [128, DIN], F32)
    epsc = sbt("epsc", [128, 4], F32)

    WXB = DINPROJ - 2048
    w_in_bf = sb1("w_in_bf", [128, NKD * WXB], BF16)
    xcv_bf = sb1("xcv_bf", [128, NFCV * BLK], BF16)
    z_bf = sb1("z_bf", [128, NTT * DIN], BF16)
    state = sb1("state", [128, DIN], F32)
    state_bf = sb1("state_bf", [128, DIN], BF16)
    egt = sb1("egt", [128, NH * C // 2], BF16)
    ea_bf = sb1("ea_bf", [128, DIN], BF16)
    uh_bf = sb1("uh_bf", [128, DIN], BF16)
    uh2_bf = sb1("uh2_bf", [128, DIN], BF16)
    xh_tm = sb1("xh_tm", [128, DIN], BF16)
    b_tm = sb1("b_tm", [128, DST], BF16)
    mskst = sb1("mskst", [128, C], BF16)
    y_f32 = sb1("y_f32", [128, DIN], F32)
    g_blk = sb1("g_blk", [128, NFIN * BLK], BF16)
    u_tm = sb1("u_tm", [128, NTT * D], BF16)
    u_fm = sb1("u_fm", [128, NKD * BLK], BF16)
    tails = sb1("tails", [128, NFCV * 3], F32)
    dt_blk = sb1("dt_blk", [NH, BLK], F32)
    dtb_bf = sb1("dtb_bf", [NH, BLK], BF16)
    a_bf = sb1("a_bf", [NH, BLK], BF16)
    a_tmb = sb1("a_tmb", [128, NH], BF16)
    acum_f = sb1("acum_f", [NH, C], F32)
    acf_h = sb1("acf_h", [16, C], F32)
    ab_hi = sb1("ab_hi", [64, C], BF16)
    ab_lo = sb1("ab_lo", [64, C], BF16)
    hi_t = sb1("hi_t", [16, 2 * C], BF16)
    lo_t = sb1("lo_t", [16, 2 * C], BF16)
    eind_hi = sb1("eind_hi", [33, NH * C // 2], BF16)
    eind_lo = sb1("eind_lo", [33, NH * C // 2], BF16)
    eind2_s = sb1("eind2_s", [NH, DIN], BF16)
    dtdec_bf = sb1("dtdec_bf", [NH, C], BF16)
    expac_f = sb1("expac_f", [NH, C], F32)
    expac_bf = sb1("expac_bf", [NH, C], BF16)
    dec_bf = sb1("dec_bf", [NH, C], BF16)
    triu_fs = sb1("triu_fs", [128, 128], F32)
    triu_bfs = sb1("triu_bfs", [128, 128], BF16)
    maskneg_s = sb1("maskneg_s", [128, 128], BF16)
    dskd_s = sb1("dskd_s", [128, NFIN * 128], BF16)
    convw_s = sb1("convw_s", [128, NFCV * 4], F32)
    convb_s = sb1("convb_s", [128, NFCV], F32)
    cdrep = sb1("cdrep", [NH, 128], BF16)

    SM_DTB, SM_NA, SM_TMP, SM_FLAG, SM_TMP2 = 0, 1, 2, 3, 4

    P0 = pst("P0", [128, 1024], F32)
    P1 = pst("P1", [128, 1024], F32)
    P2 = pst("P2", [128, 1024], F32)
    P3 = pst("P3", [128, 1024], F32)
    PS = [P0, P1, P2, P3]

    class Sem:
        def __init__(self, name):
            self.h = es.enter_context(nc.semaphore(name))
            self.n = 0
        def inc(self, instr, k=1):
            instr.then_inc(self.h, k)
            self.n += k
    s_ld = Sem("s_ld")
    s_ax = Sem("s_ax")
    s_st = Sem("s_st")
    s_pe = Sem("s_pe")
    s_v = Sem("s_v")
    s_s = Sem("s_s")
    s_g = Sem("s_g")
    s_wl = Sem("s_wl")
    s_ws = Sem("s_ws")
    s_wp = Sem("s_wp")
    s_px = Sem("s_px")
    ALLS = None

    block = es.enter_context(nc.Block())

    _OPNAMES = {"tensor_copy", "tensor_tensor", "tensor_add", "tensor_sub", "tensor_mul",
                "tensor_max", "tensor_scalar", "tensor_single_scalar", "tensor_scalar_add",
                "tensor_scalar_sub", "tensor_scalar_mul", "tensor_scalar_max", "tensor_scalar_min",
                "tensor_relu", "tensor_reduce", "reduce_max", "reduce_sum", "scalar_tensor_tensor",
                "tensor_tensor_scan", "memset", "memzero", "reciprocal", "select", "activation",
                "copy", "add", "mul", "square", "sqrt", "sign", "transpose"}

    class _DW:
        def __init__(self, eng):
            self._e = eng
        def __getattr__(self, name):
            attr = getattr(self._e, name)
            if name not in _OPNAMES:
                return attr
            def wrapped(*args, **kw):
                r = attr(*args, **kw)
                need = kw.get("accum_out") is not None
                if not need and args:
                    try:
                        need = args[0].free_size() < 512
                    except Exception:
                        need = True
                if need:
                    self._e.drain()
                    return self._e.nop()
                return r
            return wrapped

    def T(f): block.tensor(f)
    def V(f): block.vector(lambda e: f(_DW(e)))
    def SC(f): block.scalar(lambda e: f(_DW(e)))
    def G(f): block.gpsimd(lambda e: f(_DW(e)))
    def SY(f): block.sync(f)

    def wait_upto(eng, *sems):
        for s, v in [(s, s.n) for s in sems]:
            if v:
                eng.wait_ge(s.h, v)

    def barrier():
        sems = [s_pe, s_v, s_s, s_g, s_ld, s_ax, s_st]
        vals = [(s.h, s.n) for s in sems if s.n]
        def mk(eng, vals=vals):
            for h, v in vals:
                eng.wait_ge(h, v)
        T(mk); V(mk); SC(mk); G(mk); SY(mk)

    from concourse.hw_specs import get_activation_tables
    _TABSETS = list(get_activation_tables(nc.m.arch))
    _NLE = _TABSETS.index("natural_log_exp_and_others")
    _SILU = _TABSETS.index("silu_and_others")

    def load_tab(sc, idx):
        sc.add_instruction(mybir.InstLoadActFuncSet(
            name=nc.get_next_instruction_name(), act_func_set_id=idx, ins=[], outs=[]))

    def pe_done(t):
        t.drain()
        s_pe.inc(t.nop())

    def preload():
        def ld(sync):
            s_ld.inc(sync.dma_start(out=triu_fs[:], in_=triu_f), 16)
            s_ld.inc(sync.dma_start(out=triu_bfs[:], in_=triu_bf), 16)
            s_ld.inc(sync.dma_start(out=maskneg_s[:], in_=maskneg), 16)
            s_ld.inc(sync.dma_start(out=ident_s[:], in_=ident_in), 16)
            s_ld.inc(sync.dma_start(out=onesrow[:], in_=onesrow_in), 16)
            s_ld.inc(sync.dma_start(out=eind_hi[0:32, :], in_=eind_in), 16)
            s_ld.inc(sync.dma_start(out=eind_lo[0:32, :], in_=eind_in), 16)
            s_ld.inc(sync.dma_start(out=eind2_s[:], in_=eind2_in), 16)
            s_ld.inc(sync.dma_start(out=sc128[:, 0:2048], in_=d_skip_diag), 16)
            s_ld.inc(sync.dma_start(out=smalls[0:NH, SM_DTB:SM_DTB + 1], in_=dt_bias), 16)
            s_ld.inc(sync.dma_start(out=smalls[0:NH, SM_NA:SM_NA + 1], in_=a_log), 16)
            s_ld.inc(sync.dma_start(out=smalls[:, SM_FLAG:SM_FLAG + 1], in_=flag_in), 16)
            for t in range(NFCV):
                s_ld.inc(sync.dma_start(out=convw_s[:, 4 * t:4 * t + 4], in_=conv_w[128 * t:128 * (t + 1), :]), 16)
                s_ld.inc(sync.dma_start(out=convb_s[:, t:t + 1], in_=conv_b[128 * t:128 * (t + 1), :]), 16)
            for t in range(NFIN):
                s_ld.inc(sync.dma_start(out=smalls[:, 8 + t:9 + t], in_=norm_w[128 * t:128 * (t + 1), :]), 16)
        SY(ld)
        ldn = s_ld.n
        def pre_s(s, ldn=ldn):
            s.wait_ge(s_ld.h, ldn)
            s_s.inc(s.activation(smalls[0:NH, SM_NA:SM_NA + 1], smalls[0:NH, SM_NA:SM_NA + 1], AF.Exp))
        SC(pre_s)
        sn = s_s.n
        def pre_v(v, sn=sn, ldn=ldn):
            v.wait_ge(s_s.h, sn)
            v.tensor_scalar_mul(smalls[0:NH, SM_NA:SM_NA + 1], smalls[0:NH, SM_NA:SM_NA + 1], -1.0)
            v.memset(consts[:, 0:1], 1.0)
            v.memset(epsc[:, 0:1], EPS_LN)
            v.memset(epsc[:, 1:2], EPS_OUT)
            v.memset(epsc[:, 2:3], EPS_GATE)
            v.memset(epsc[:, 3:4], EPS_GATE)
            v.memset(ab_hi[0:32, :], 0.0)
            v.memset(ab_lo[0:32, :], 0.0)
            v.memset(ab_hi[32:64, :], 1.0)
            v.memset(ab_lo[32:64, :], 1.0)
            v.memset(state[:], 0.0)
            v.memset(tails[:], 0.0)
            s_v.inc(v.tensor_copy(dskd_s[:], sc128[:, 0:2048]))
        V(pre_v)
        for k in range(NKD):
            for c0 in range(0, WXB, 2048):
                cw = min(2048, WXB - c0)
                vn = s_v.n
                def ldk(sync, k=k, c0=c0, cw=cw, vn=vn):
                    sync.wait_ge(s_v.h, vn)
                    s_ld.inc(sync.dma_start(out=sc128[:, 0:cw], in_=w_in[128 * k:128 * (k + 1), 2048 + c0:2048 + c0 + cw]), 16)
                ln = s_ld.n + 16
                def cstk(v, k=k, c0=c0, cw=cw, ln=ln):
                    v.wait_ge(s_ld.h, ln)
                    s_v.inc(v.tensor_copy(w_in_bf[:, WXB * k + c0:WXB * k + c0 + cw], sc128[:, 0:cw]))
                SY(ldk); V(cstk)
        barrier()
        zf32 = z_bf[:].bitcast(F32)
        wready = {}
        pieces = []
        def add_stream(name, src, dst, rows, cols):
            for r0 in range(0, rows, 128):
                for c0 in range(0, cols, 2048):
                    pieces.append((name, src, dst, r0, c0, min(2048, cols - c0)))
        add_stream("winz", w_in[:, 0:2048], dr_winzbf, D, 2048)
        add_stream("wout", w_out, dr_woutbf, DIN, D)
        add_stream("wfc", w_fc, dr_wfcbf, D, FF)
        add_stream("wproj", w_proj, dr_wprojbf, FF, D)
        add_stream("Wfc", Wfc, dr_Wfcbf, FF, D)
        add_stream("Wproj", Wproj, dr_Wprojbf, D, FF)
        NPC = len(pieces)
        ld_at = [0] * NPC
        st_at = [0] * NPC
        def emit_gstream(g):
            for i in range(NPC + 1):
                if i < NPC:
                    name, src, dst, r0, c0, cw = pieces[i]
                    par = i % 2
                    g.dma_start(out=zf32[:, par * 2048:par * 2048 + cw],
                                in_=src[r0:r0 + 128, c0:c0 + cw]).then_inc(s_wl.h, 16)
                    s_wl.n += 16
                    ld_at[i] = s_wl.n
                if i >= 1:
                    j = i - 1
                    name, src, dst, r0, c0, cw = pieces[j]
                    par = j % 2
                    g.wait_ge(s_wl.h, ld_at[j])
                    if j >= 2:
                        g.wait_ge(s_ws.h, 16 * (j - 1))
                    g.tensor_copy(g_blk[:, par * 2048:par * 2048 + cw],
                                  zf32[:, par * 2048:par * 2048 + cw])
                    g.dma_start(out=dst[r0:r0 + 128, c0:c0 + cw],
                                in_=g_blk[:, par * 2048:par * 2048 + cw]).then_inc(s_ws.h, 16)
                    s_ws.n += 16
        G(emit_gstream)
        for i, p in enumerate(pieces):
            wready[p[0]] = 16 * (i + 1)
        preload.wready = wready
        preload.stream_done = 16 * NPC

    def emit_stats_u(tt, par, rms=True, ldwait=0):
        s0 = 6 * tt
        xa = x0_rot[:, par * D:(par + 1) * D]
        c_mean = stats[:, s0:s0 + 1]
        c_var = stats[:, s0 + 1:s0 + 2]
        c_rln = stats[:, s0 + 2:s0 + 3]
        c_R = stats[:, s0 + 3:s0 + 4]
        c_ir2 = stats[:, s0 + 4:s0 + 5]
        def sq(s, ldwait=ldwait):
            if ldwait:
                s.wait_ge(s_ld.h, ldwait)
            s_s.inc(s.activation(sc128[:, 0:D], xa, AF.Square, accum_out=c_var))
        SC(sq)
        sn = s_s.n
        def v1(v, sn=sn):
            v.wait_ge(s_s.h, sn)
            v.reduce_sum(c_mean, xa, axis=AX)
            v.tensor_scalar_mul(c_mean, c_mean, 1.0 / D)
            v.tensor_scalar_mul(c_var, c_var, 1.0 / D)
            v.scalar_tensor_tensor(c_var, c_mean, c_mean, c_var, ALU.mult, ALU.subtract)
            s_v.inc(v.tensor_scalar_mul(c_var, c_var, -1.0))
        V(v1)
        vn = s_v.n
        def s1(s, vn=vn):
            s.wait_ge(s_v.h, vn)
            s_s.inc(s.activation(c_rln, c_var, AF.Sqrt, bias=epsc[:, 0:1]))
        SC(s1)
        if not rms:
            sn2 = s_s.n
            def v3(v, sn2=sn2, tt=tt):
                v.wait_ge(s_s.h, sn2)
                v.reciprocal(c_R, c_rln)
                v.tensor_copy(c_rln, c_R)
                s_v.inc(v.tensor_scalar(u_tm[:, tt * D:(tt + 1) * D], xa, c_mean, c_rln,
                                        ALU.subtract, ALU.mult))
            V(v3)
            return
        sn2 = s_s.n
        def v2(v, sn2=sn2):
            v.wait_ge(s_s.h, sn2)
            v.reciprocal(c_R, c_rln)
            v.tensor_copy(c_rln, c_R)
            v.tensor_mul(c_R, c_var, c_rln)
            s_v.inc(v.tensor_mul(c_R, c_R, c_rln))
        V(v2)
        vn2 = s_v.n
        def s2(s, vn2=vn2):
            s.wait_ge(s_v.h, vn2)
            s_s.inc(s.activation(c_ir2, c_R, AF.Sqrt, bias=epsc[:, 1:2]))
        SC(s2)
        sn3 = s_s.n
        def v3(v, sn3=sn3, tt=tt):
            v.wait_ge(s_s.h, sn3)
            v.reciprocal(c_R, c_ir2)
            v.tensor_mul(c_R, c_R, c_rln)
            s_v.inc(v.tensor_scalar(u_tm[:, tt * D:(tt + 1) * D], xa, c_mean, c_R,
                                    ALU.subtract, ALU.mult))
        V(v3)

    def emit_u_transpose(tt):
        vn = s_v.n
        def tr(t, tt=tt, vn=vn):
            t.wait_ge(s_v.h, vn)
            P3b = P3[:].bitcast(BF16)
            for k in range(NKD):
                t.matmul(P3b[:, k * 128:(k + 1) * 128],
                         u_tm[:, tt * D + k * 128: tt * D + (k + 1) * 128],
                         ident_s[:], is_transpose=True, start=True, stop=True)
            pe_done(t)
        T(tr)
        pn = s_pe.n
        def ev(v, tt=tt, pn=pn):
            v.wait_ge(s_pe.h, pn)
            P3b = P3[:].bitcast(BF16)
            dst = u_fm[:].rearrange("p (k b) -> p k b", k=NKD)[:, :, tt * 128:(tt + 1) * 128]
            src = P3b[:, 0:1024].rearrange("p (k l) -> p k l", k=NKD)
            s_v.inc(v.tensor_copy(dst, src))
        V(ev)

    def emit_block_front(xsrc, b, is_pre):
        for tt in range(NTT):
            par = tt % 2
            vv = s_v.n
            sv = s_s.n
            def ldx(sync, tt=tt, par=par, vv=vv, sv=sv, xsrc=xsrc, b=b):
                if vv:
                    sync.wait_ge(s_v.h, vv)
                if sv:
                    sync.wait_ge(s_s.h, sv)
                s_ld.inc(sync.dma_start(out=x0_rot[:, par * D:(par + 1) * D],
                                        in_=xsrc[b * BLK + tt * 128: b * BLK + (tt + 1) * 128, :]), 16)
            ln = s_ld.n + 16
            SY(ldx)
            def wx(v, ln=ln):
                v.wait_ge(s_ld.h, ln)
            V(wx)
            emit_stats_u(tt, par, rms=True, ldwait=ln)
            emit_u_transpose(tt)
        if d_u is not None and not is_pre:
            vn = s_v.n
            def dbg_st(sync, b=b, vn=vn):
                sync.wait_ge(s_v.h, vn)
                s_st.inc(sync.dma_start(out=d_u[b], in_=u_fm[:]), 16)
            SY(dbg_st)
        if d_st0 is not None and not is_pre and b == 0:
            vn = s_v.n
            def dbg_s0(sync, vn=vn):
                sync.wait_ge(s_v.h, vn)
                s_st.inc(sync.dma_start(out=d_st0, in_=stats[:]), 16)
                if d_x00 is not None:
                    s_st.inc(sync.dma_start(out=d_x00, in_=x0_rot[:]), 16)
            SY(dbg_s0)

        uvn = s_v.n
        if not is_pre:
            for zs in range(4):
                half = zs % 2
                sv = s_s.n
                for k in range(NKD):
                    par = k % 4
                    pvw = s_pe.n
                    def ldwz(sync, zs=zs, k=k, par=par, pvw=pvw):
                        if pvw:
                            sync.wait_ge(s_pe.h, pvw)
                        sync.wait_ge(s_ws.h, preload.wready["winz"])
                        s_ld.inc(sync.dma_start(out=wo_rot[:, par * 512:(par + 1) * 512],
                                                in_=dr_winzbf[128 * k:128 * (k + 1), zs * 512:(zs + 1) * 512]), 16)
                    SY(ldwz)
                    ln = s_ld.n
                    def mmz(t, zs=zs, k=k, par=par, ln=ln, sv=sv, uvn=uvn, half=half):
                        t.wait_ge(s_ld.h, ln)
                        if k == 0:
                            t.wait_ge(s_v.h, uvn)
                            if sv:
                                t.wait_ge(s_s.h, sv)
                        for tt in range(NTT):
                            PZ = PS[tt]
                            t.matmul(PZ[:, half * 512:(half + 1) * 512],
                                     u_fm[:, k * BLK + tt * 128:k * BLK + (tt + 1) * 128],
                                     wo_rot[:, par * 512:(par + 1) * 512],
                                     start=(k == 0), stop=(k == NKD - 1))
                        if k == NKD - 1:
                            pe_done(t)
                        else:
                            s_pe.inc(t.nop())
                    T(mmz)
                pn = s_pe.n
                def evz(s, zs=zs, pn=pn, half=half, b=b):
                    s.wait_ge(s_pe.h, pn)
                    if b == 0:
                        s.wait_ge(s_ws.h, preload.stream_done)
                    for tt in range(NTT):
                        i = s.activation(z_bf[:, tt * DIN + zs * 512: tt * DIN + (zs + 1) * 512],
                                         PS[tt][:, half * 512:(half + 1) * 512], AF.Silu)
                    s_s.inc(i)
                SC(evz)
            if d_z is not None:
                sn = s_s.n
                def dbg_z(sync, b=b, sn=sn):
                    sync.wait_ge(s_s.h, sn)
                    s_st.inc(sync.dma_start(out=d_z[b], in_=z_bf[:]), 16)
                SY(dbg_z)

        mstart = 16
        for m in range(mstart, 35):
            ft = m - 16
            pslot = 2 + (ft % 2)
            PZ = PS[pslot]
            rows = 128 if m < 34 else 32
            vv = s_v.n
            sv = s_s.n
            def mmx(t, m=m, PZ=PZ, vv=vv, sv=sv, uvn=uvn, rows=rows):
                t.wait_ge(s_v.h, vv)
                for k in range(NKD):
                    t.matmul(PZ[0:rows, 0:512],
                             w_in_bf[:, WXB * k + (m - 16) * 128: WXB * k + (m - 16) * 128 + rows],
                             u_fm[:, k * BLK:(k + 1) * BLK],
                             start=(k == 0), stop=(k == NKD - 1))
                pe_done(t)
            T(mmx)
            pn = s_pe.n
            if m < 34:
                def cv(v, ft=ft, PZ=PZ, pn=pn):
                    v.wait_ge(s_pe.h, pn)
                    wc = lambda k: convw_s[:, 4 * ft + k:4 * ft + k + 1]
                    acc = sc128[:, 0:512]
                    v.tensor_scalar(acc, PZ[:, 0:512], wc(3), convb_s[:, ft:ft + 1], ALU.mult, ALU.add)
                    for k in (2, 1, 0):
                        sh = 3 - k
                        v.scalar_tensor_tensor(acc[:, sh:512], PZ[:, 0:512 - sh], wc(k),
                                               acc[:, sh:512], ALU.mult, ALU.add)
                    tl = tails[:, 3 * ft:3 * ft + 3]
                    v.scalar_tensor_tensor(acc[:, 0:3], tl[:, 0:3], wc(0), acc[:, 0:3], ALU.mult, ALU.add)
                    v.scalar_tensor_tensor(acc[:, 0:2], tl[:, 1:3], wc(1), acc[:, 0:2], ALU.mult, ALU.add)
                    v.scalar_tensor_tensor(acc[:, 0:1], tl[:, 2:3], wc(2), acc[:, 0:1], ALU.mult, ALU.add)
                    s_v.inc(v.tensor_copy(tl, PZ[:, 509:512]))
                V(cv)
                vn = s_v.n
                def cvs(s, ft=ft, vn=vn):
                    s.wait_ge(s_v.h, vn)
                    s_s.inc(s.activation(xcv_bf[:, ft * BLK:(ft + 1) * BLK], sc128[:, 0:512], AF.Silu))
                SC(cvs)
            else:
                def dtev(s, PZ=PZ, pn=pn):
                    s.wait_ge(s_pe.h, pn)
                    s.activation(dt_blk[:, :], PZ[0:NH, 0:512], AF.Relu, bias=smalls[0:NH, SM_DTB:SM_DTB + 1])
                    s.activation(sc128[0:NH, 512:1024], PZ[0:NH, 0:512], AF.Abs, bias=smalls[0:NH, SM_DTB:SM_DTB + 1])
                    s_s.inc(s.activation(sc128[0:NH, 512:1024], sc128[0:NH, 512:1024], AF.Exp, scale=-1.0))
                SC(dtev)
                sn = s_s.n
                def dtv(v, sn=sn):
                    v.wait_ge(s_s.h, sn)
                    s_v.inc(v.tensor_scalar_add(sc128[0:NH, 512:1024], sc128[0:NH, 512:1024], 1.0))
                V(dtv)
                vn = s_v.n
                def dtl(s, vn=vn):
                    s.wait_ge(s_v.h, vn)
                    s_s.inc(s.activation(sc128[0:NH, 512:1024], sc128[0:NH, 512:1024], AF.Ln))
                SC(dtl)
                sn2 = s_s.n
                def dtf(v, sn2=sn2, b=b):
                    v.wait_ge(s_s.h, sn2)
                    v.tensor_add(dt_blk[:, :], dt_blk[:, :], sc128[0:NH, 512:1024])
                    v.tensor_copy(dtb_bf[:, :], dt_blk[:, :])
                    s_v.inc(v.tensor_scalar_mul(a_bf[:, :], dt_blk[:, :], smalls[0:NH, SM_NA:SM_NA + 1]))
                V(dtf)
        if d_xcv is not None and not is_pre:
            sn = s_s.n
            def dbg_xc(sync, b=b, sn=sn):
                sync.wait_ge(s_s.h, sn)
                s_st.inc(sync.dma_start(out=d_xcv[b], in_=xcv_bf[:]), 16)
            SY(dbg_xc)
        if d_dt is not None and not is_pre:
            vn = s_v.n
            def dbg_dt(sync, b=b, vn=vn):
                sync.wait_ge(s_v.h, vn)
                s_st.inc(sync.dma_start(out=d_dt[b], in_=dt_blk[:, :]), 16)
            SY(dbg_dt)

    def emit_acum(b, c, is_pre):
        vv0 = s_v.n
        def tra(t, c=c, vv0=vv0):
            if vv0:
                t.wait_ge(s_v.h, vv0)
            P2b = P2[:].bitcast(BF16)
            t.matmul(P2b[:, 0:NH], a_bf[:, c * C:(c + 1) * C], ident_s[0:NH, 0:NH],
                     is_transpose=True, start=True, stop=True)
            pe_done(t)
        T(tra)
        pn = s_pe.n
        def eva(v, pn=pn):
            v.wait_ge(s_pe.h, pn)
            P2b = P2[:].bitcast(BF16)
            s_v.inc(v.tensor_copy(a_tmb[:], P2b[:, 0:NH]))
        V(eva)
        vn = s_v.n
        def mmac(t, vn=vn, is_pre=is_pre):
            t.wait_ge(s_v.h, vn)
            t.matmul(P2[0:NH, 0:C], a_tmb[:], triu_bfs[:], start=True, stop=True)
            if not is_pre:
                t.matmul(P2[0:16, 256:256 + C], a_tmb[:, 0:16], triu_bfs[:], start=True, stop=True)
                t.matmul(P2[0:16, 512:512 + C], a_tmb[:, 16:32], triu_bfs[:], start=True, stop=True)
            pe_done(t)
        T(mmac)
        pn2 = s_pe.n
        def evac(v, pn2=pn2):
            v.wait_ge(s_pe.h, pn2)
            s_v.inc(v.tensor_copy(acum_f[:, :], P2[0:NH, 0:C]))
        V(evac)
        if not is_pre:
            pvh = s_pe.n
            axh = s_ax.n
            def evhh(v, pvh=pvh, axh=axh):
                v.wait_ge(s_pe.h, pvh)
                if axh:
                    v.wait_ge(s_ax.h, axh)
                for half in range(2):
                    src = P2[0:16, 256 + 256 * half:256 + 256 * half + C]
                    hs = hi_t[:, half * C:(half + 1) * C]
                    ls = lo_t[:, half * C:(half + 1) * C]
                    v.tensor_copy(acf_h[:, :], src)
                    v.tensor_copy(hs, src)
                    v.tensor_tensor(acf_h[:, :], acf_h[:, :], hs, ALU.subtract)
                    v.tensor_copy(ls, acf_h[:, :])
                v.tensor_scalar_mul(ab_hi[0:16, :], hi_t[:, 0:C], -1.0)
                s_v.inc(v.tensor_scalar_mul(ab_lo[0:16, :], lo_t[:, 0:C], -1.0))
            V(evhh)
            vnh = s_v.n
            def flh(sync, vnh=vnh):
                sync.wait_ge(s_v.h, vnh)
                s_ax.inc(sync.dma_start(out=dr_flat[0:1, 0:2048], in_=hi_t[:, 0:C]), 16)
                s_ax.inc(sync.dma_start(out=dr_flat[0:1, 2048:4096], in_=hi_t[:, C:2 * C]), 16)
                s_ax.inc(sync.dma_start(out=dr_flat[1:2, 0:2048], in_=lo_t[:, 0:C]), 16)
                s_ax.inc(sync.dma_start(out=dr_flat[1:2, 2048:4096], in_=lo_t[:, C:2 * C]), 16)
            SC(flh)
            axn2 = s_ax.n
            def flr0(sync, axn2=axn2):
                sync.wait_ge(s_ax.h, axn2)
                s_ax.inc(sync.dma_start(out=eind_hi[32:33, :], in_=dr_flat[0:1, 0:2048]), 16)
                s_ax.inc(sync.dma_start(out=eind_lo[32:33, :], in_=dr_flat[1:2, 0:2048]), 16)
            SC(flr0)
            def prep_half1():
                pv1 = s_pe.n
                def ab1(v, pv1=pv1):
                    v.wait_ge(s_pe.h, pv1)
                    v.tensor_scalar_mul(ab_hi[0:16, :], hi_t[:, C:2 * C], -1.0)
                    s_v.inc(v.tensor_scalar_mul(ab_lo[0:16, :], lo_t[:, C:2 * C], -1.0))
                V(ab1)
                def flr1(sync, pv1=pv1):
                    sync.wait_ge(s_pe.h, pv1)
                    s_ax.inc(sync.dma_start(out=eind_hi[32:33, :], in_=dr_flat[0:1, 2048:4096]), 16)
                    s_ax.inc(sync.dma_start(out=eind_lo[32:33, :], in_=dr_flat[1:2, 2048:4096]), 16)
                SC(flr1)
            emit_acum.prep_half1 = prep_half1
        vn2 = s_v.n
        def sdec(s, vn2=vn2, is_pre=is_pre):
            s.wait_ge(s_v.h, vn2)
            s.activation(dec_bf[:, :], acum_f[:, :], AF.Exp, bias=acum_f[:, C - 1:C], scale=-1.0)
            i = s.activation(expac_f[:, :], acum_f[:, :], AF.Exp)
            s_s.inc(i)
        SC(sdec)
        sn = s_s.n
        def vdec(v, sn=sn, is_pre=is_pre, c=c):
            v.wait_ge(s_s.h, sn)
            if is_pre:
                i = v.tensor_tensor(dtdec_bf[:, :], dtb_bf[:, c * C:(c + 1) * C], dec_bf[:, :], ALU.mult)
            else:
                i = v.tensor_copy(expac_bf[:, :], expac_f[:, :])
            s_v.inc(i)
        V(vdec)
        if d_acum is not None and not is_pre:
            vn3 = s_v.n
            def dbg_ac(sync, b=b, c=c, vn3=vn3):
                sync.wait_ge(s_v.h, vn3)
                s_st.inc(sync.dma_start(out=d_acum[b * NCB + c], in_=acum_f[:, :]), 16)
            SY(dbg_ac)

    def emit_xh_tm(b, c):
        sv = s_s.n
        vv = s_v.n
        def trx(t, c=c, sv=sv, vv=vv):
            t.wait_ge(s_s.h, sv)
            t.wait_ge(s_v.h, vv)
            P3b = P3[:].bitcast(BF16)
            for ftile in range(NFIN):
                t.matmul(P3b[:, ftile * 128:(ftile + 1) * 128],
                         xcv_bf[:, ftile * BLK + c * C: ftile * BLK + (c + 1) * C],
                         ident_s[:], is_transpose=True, start=True, stop=True)
            P2b = P2[:].bitcast(BF16)
            t.matmul(P2b[:, 256:256 + 128],
                     xcv_bf[:, 16 * BLK + c * C: 16 * BLK + (c + 1) * C],
                     ident_s[:], is_transpose=True, start=True, stop=True)
            pe_done(t)
        T(trx)
        pn = s_pe.n
        def evx(v, pn=pn):
            v.wait_ge(s_pe.h, pn)
            P3b = P3[:].bitcast(BF16)
            P2b = P2[:].bitcast(BF16)
            v.tensor_copy(xh_tm[:], P3b[:, 0:DIN])
            s_v.inc(v.tensor_copy(b_tm[:], P2b[:, 256:256 + 128]))
        V(evx)

    def emit_scores(b, c):
        sv = s_s.n
        vv0 = s_v.n
        def mms(t, c=c, sv=sv, vv0=vv0):
            t.wait_ge(s_s.h, sv)
            t.wait_ge(s_v.h, vv0)
            t.matmul(P2[:, 512:512 + 128],
                     xcv_bf[:, 16 * BLK + c * C:16 * BLK + (c + 1) * C],
                     xcv_bf[:, 17 * BLK + c * C:17 * BLK + (c + 1) * C],
                     start=True, stop=True)
            pe_done(t)
        T(mms)
        pn = s_pe.n
        def evs(v, pn=pn):
            v.wait_ge(s_pe.h, pn)
            s_v.inc(v.tensor_tensor(mskst[:], P2[:, 512:512 + 128], triu_fs[:], ALU.mult))
        V(evs)

    def emit_uh(b, c, is_pre):
        vv = s_v.n
        def mmdt(t, c=c, vv=vv, is_pre=is_pre):
            t.wait_ge(s_v.h, vv)
            lhs = dtdec_bf[:, :] if is_pre else dtb_bf[:, c * C:(c + 1) * C]
            for half in range(2):
                PZ = PS[half]
                for q in range(2):
                    t.matmul(PZ[:, q * 512:(q + 1) * 512], lhs,
                             eind2_s[:, half * 1024 + q * 512: half * 1024 + (q + 1) * 512],
                             start=True, stop=True)
            pe_done(t)
        T(mmdt)
        pn = s_pe.n
        def evuh(v, pn=pn, is_pre=is_pre):
            v.wait_ge(s_pe.h, pn)
            dst = uh2_bf if is_pre else uh_bf
            v.tensor_tensor(dst[:, 0:1024], xh_tm[:, 0:1024], P0[:], ALU.mult)
            s_v.inc(v.tensor_tensor(dst[:, 1024:2048], xh_tm[:, 1024:2048], P1[:], ALU.mult))
        V(evuh)
        if not is_pre:
            vv2 = s_v.n
            def mmdec(t, c=c, vv2=vv2):
                t.wait_ge(s_v.h, vv2)
                for half in range(2):
                    PZ = PS[half]
                    for q in range(2):
                        t.matmul(PZ[:, q * 512:(q + 1) * 512], dec_bf[:, :],
                                 eind2_s[:, half * 1024 + q * 512: half * 1024 + (q + 1) * 512],
                                 start=True, stop=True)
                pe_done(t)
            T(mmdec)
            pn2 = s_pe.n
            def evuh2(v, pn2=pn2):
                v.wait_ge(s_pe.h, pn2)
                v.tensor_tensor(uh2_bf[:, 0:1024], uh_bf[:, 0:1024], P0[:], ALU.mult)
                s_v.inc(v.tensor_tensor(uh2_bf[:, 1024:2048], uh_bf[:, 1024:2048], P1[:], ALU.mult))
            V(evuh2)

    def emit_ea(b, c):
        vv = s_v.n
        def mmea(t, vv=vv):
            t.wait_ge(s_v.h, vv)
            for half in range(2):
                PZ = PS[half]
                for q in range(2):
                    t.matmul(PZ[:, q * 512:(q + 1) * 512], expac_bf[:, :],
                             eind2_s[:, half * 1024 + q * 512: half * 1024 + (q + 1) * 512],
                             start=True, stop=True)
            pe_done(t)
        T(mmea)
        pn = s_pe.n
        def evea(s, pn=pn):
            s.wait_ge(s_pe.h, pn)
            s.activation(ea_bf[:, 0:1024], P0[:], AF.Copy)
            s_s.inc(s.activation(ea_bf[:, 1024:2048], P1[:], AF.Copy))
        SC(evea)

    def emit_ssd_quadratic(b, c):
        vv = s_v.n
        sv = s_s.n
        def mmdsk(t, c=c, sv=sv, vv=vv):
            t.wait_ge(s_s.h, sv)
            t.wait_ge(s_v.h, vv)
            for ftile in range(NFIN):
                PZ = P2 if ftile < 8 else P3
                off = (ftile % 8) * 128
                t.matmul(PZ[:, off:off + 128],
                         xcv_bf[:, ftile * BLK + c * C: ftile * BLK + (c + 1) * C],
                         dskd_s[:, ftile * 128:(ftile + 1) * 128],
                         start=((ftile % 8) % 4 == 0), stop=False)
            pe_done(t)
        T(mmdsk)
        if d_eind is not None and b == 0 and c == 0:
            axq = s_ax.n
            def dbg_ei(sync, axq=axq):
                sync.wait_ge(s_ax.h, axq)
                s_st.inc(sync.dma_start(out=d_eind, in_=eind_hi[:]), 16)
            SY(dbg_ei)
        for half in range(2):
            axn = s_ax.n
            vv2 = s_v.n
            sv2 = s_s.n
            def mmr(t, half=half, axn=axn, vv2=vv2, sv2=sv2):
                t.wait_ge(s_ax.h, axn)
                t.wait_ge(s_v.h, vv2)
                t.wait_ge(s_s.h, sv2)
                for q in range(4):
                    PZ = PS[q // 2]
                    off = (q % 2) * 512
                    t.matmul(PZ[:, off:off + 512], ab_hi[0:33, :],
                             eind_hi[:, q * 512:(q + 1) * 512], start=True, stop=False)
                    t.matmul(PZ[:, off:off + 512], ab_lo[0:33, :],
                             eind_lo[:, q * 512:(q + 1) * 512], start=False, stop=False)
                for hl in range(16):
                    PZ = PS[hl // 8]
                    off = (hl % 8) * 128
                    t.matmul(PZ[:, off:off + 128], ident_s[:], maskneg_s[:],
                             start=False, stop=True)
                pe_done(t)
            T(mmr)
            if half == 0:
                emit_acum.prep_half1()
            pn = s_pe.n
            def eve(s, pn=pn):
                s.wait_ge(s_pe.h, pn)
                s.activation(egt[:, 0:1024], P0[:], AF.Exp)
                s_s.inc(s.activation(egt[:, 1024:2048], P1[:], AF.Exp))
            SC(eve)
            sn = s_s.n
            def gtm(v, sn=sn):
                v.wait_ge(s_s.h, sn)
                src = mskst[:].unsqueeze(1).broadcast_to([128, 16, C])
                s_v.inc(v.tensor_tensor(egt[:].rearrange("p (h l) -> p h l", h=16),
                                        egt[:].rearrange("p (h l) -> p h l", h=16), src, ALU.mult))
            V(gtm)
            vn = s_v.n
            def mmyd(t, half=half, vn=vn):
                t.wait_ge(s_v.h, vn)
                PZ = P2 if half == 0 else P3
                for hl in range(16):
                    h = half * 16 + hl
                    t.matmul(PZ[:, hl * 64:hl * 64 + 64],
                             egt[:, hl * C:(hl + 1) * C],
                             uh_bf[:, h * PH:(h + 1) * PH],
                             start=False, stop=(hl == 15))
                pe_done(t)
            T(mmyd)

    def emit_yoff_and_y(b, c):
        vv = s_v.n
        sn0 = s_s.n
        def cpsb(v, vv=vv):
            s_v.inc(v.tensor_copy(state_bf[:], state[:]))
        V(cpsb)
        vn0 = s_v.n
        def mmyo(t, c=c, vn0=vn0, sn0=sn0):
            t.wait_ge(s_v.h, vn0)
            t.wait_ge(s_s.h, sn0)
            for h in range(NH):
                PZ = P0 if h < 16 else P1
                t.matmul(PZ[:, (h % 16) * 64:(h % 16) * 64 + 64],
                         xcv_bf[:, 17 * BLK + c * C:17 * BLK + (c + 1) * C],
                         state_bf[:, h * PH:(h + 1) * PH],
                         start=(h % 8 == 0), stop=(h % 16 == 15))
            pe_done(t)
        T(mmyo)
        pn = s_pe.n
        def ya(v, pn=pn, c=c):
            v.wait_ge(s_pe.h, pn)
            v.tensor_tensor(y_f32[:, 0:1024], P0[:], ea_bf[:, 0:1024], ALU.mult)
            v.tensor_tensor(y_f32[:, 1024:2048], P1[:], ea_bf[:, 1024:2048], ALU.mult)
            v.tensor_tensor(y_f32[:, 0:1024], y_f32[:, 0:1024], P2[:], ALU.add)
            s_v.inc(v.tensor_tensor(y_f32[:, 1024:2048], y_f32[:, 1024:2048], P3[:], ALU.add))
        V(ya)
        if d_y is not None:
            vn = s_v.n
            def dbg_y(sync, b=b, c=c, vn=vn):
                sync.wait_ge(s_v.h, vn)
                s_st.inc(sync.dma_start(out=d_y[b * NCB + c], in_=y_f32[:]), 16)
            SY(dbg_y)
        stn_y = s_st.n
        def yz(v, c=c, stn_y=stn_y):
            if stn_y:
                v.wait_ge(s_st.h, stn_y)
            s_v.inc(v.tensor_tensor(y_f32[:], y_f32[:], z_bf[:, c * DIN:(c + 1) * DIN], ALU.mult))
        V(yz)
        vn2 = s_v.n
        def gn(s, vn2=vn2):
            s.wait_ge(s_v.h, vn2)
            s.activation(sc128[:, 0:DIN], y_f32[:], AF.Square, accum_out=smalls[:, SM_TMP:SM_TMP + 1])
            s_s.inc(s.activation(smalls[:, SM_TMP:SM_TMP + 1], smalls[:, SM_TMP:SM_TMP + 1], AF.Sqrt,
                                 bias=epsc[:, 3:4], scale=1.0 / DIN))
        SC(gn)
        sn = s_s.n
        def gv(v, sn=sn):
            v.wait_ge(s_s.h, sn)
            v.reciprocal(smalls[:, SM_TMP2:SM_TMP2 + 1], smalls[:, SM_TMP:SM_TMP + 1])
            s_v.inc(v.tensor_scalar_mul(uh_bf[:], y_f32[:], smalls[:, SM_TMP2:SM_TMP2 + 1]))
        V(gv)
        vn3 = s_v.n
        def trg(t, c=c, vn3=vn3):
            t.wait_ge(s_v.h, vn3)
            P3b = P3[:].bitcast(BF16)
            for ftile in range(NFIN):
                t.matmul(P3b[:, ftile * 128:(ftile + 1) * 128],
                         uh_bf[:, ftile * 128:(ftile + 1) * 128],
                         ident_s[:], is_transpose=True, start=True, stop=True)
            pe_done(t)
        T(trg)
        pn2 = s_pe.n
        def evg(v, c=c, pn2=pn2, b=b):
            v.wait_ge(s_pe.h, pn2)
            if b == 0 and c == 0:
                v.wait_ge(s_ws.h, preload.stream_done)
            P3b = P3[:].bitcast(BF16)
            dst = g_blk[:].rearrange("p (f b) -> p f b", f=NFIN)[:, :, c * C:(c + 1) * C]
            s_v.inc(v.tensor_copy(dst, P3b.rearrange("p (f l) -> p f l", f=NFIN)))
        V(evg)

    def emit_scan2(b, c, is_pre, last_pre=False, first_main=False):
        vv = s_v.n
        def mmst(t, vv=vv):
            t.wait_ge(s_v.h, vv)
            for h in range(NH):
                PZ = P0 if h < 16 else P1
                t.matmul(PZ[:, (h % 16) * 64:(h % 16) * 64 + 64],
                         b_tm[:],
                         uh2_bf[:, h * PH:(h + 1) * PH],
                         start=(h % 8 == 0), stop=(h % 16 == 15))
            pe_done(t)
        T(mmst)
        def cdr(v):
            src = expac_f[:, C - 1:C].broadcast_to([NH, 128])
            s_v.inc(v.tensor_copy(cdrep[:, :], src))
        V(cdr)
        vn = s_v.n
        def mmcd(t, vn=vn):
            t.wait_ge(s_v.h, vn)
            for q in range(2):
                t.matmul(P2[:, q * 512:(q + 1) * 512], cdrep[:, :],
                         eind2_s[:, q * 512:(q + 1) * 512], start=True, stop=True)
                t.matmul(P3[:, q * 512:(q + 1) * 512], cdrep[:, :],
                         eind2_s[:, 1024 + q * 512: 1024 + (q + 1) * 512], start=True, stop=True)
            pe_done(t)
        T(mmcd)
        pn = s_pe.n
        def upd(v, pn=pn, first_main=first_main):
            v.wait_ge(s_pe.h, pn)
            if first_main:
                pass
            v.tensor_tensor(state[:, 0:1024], state[:, 0:1024], P2[:], ALU.mult)
            v.tensor_tensor(state[:, 1024:2048], state[:, 1024:2048], P3[:], ALU.mult)
            v.tensor_tensor(state[:, 0:1024], state[:, 0:1024], P0[:], ALU.add)
            s_v.inc(v.tensor_tensor(state[:, 1024:2048], state[:, 1024:2048], P1[:], ALU.add))
        V(upd)
        if d_state is not None and not is_pre:
            vn2 = s_v.n
            def dbg_s(sync, b=b, c=c, vn2=vn2):
                sync.wait_ge(s_v.h, vn2)
                s_st.inc(sync.dma_start(out=d_state[b * NCB + c], in_=state[:]), 16)
            SY(dbg_s)

    def emit_flag():
        def fl(v):
            v.tensor_scalar_mul(state[:], state[:], smalls[:, SM_FLAG:SM_FLAG + 1])
            s_v.inc(v.tensor_scalar_mul(tails[:], tails[:], smalls[:, SM_FLAG:SM_FLAG + 1]))
        V(fl)
        if d_pstate is not None:
            vn = s_v.n
            def dbg_p(sync, vn=vn):
                sync.wait_ge(s_v.h, vn)
                s_st.inc(sync.dma_start(out=d_pstate, in_=state[:]), 16)
            SY(dbg_p)

    def emit_outproj_h(b):
        vv = s_v.n
        for ftile in range(NFIN):
            par = ftile % 2
            pv = s_pe.n
            def ldw(sync, ftile=ftile, par=par, pv=pv):
                if pv:
                    sync.wait_ge(s_pe.h, pv)
                sync.wait_ge(s_ws.h, preload.wready["wout"])
                s_ld.inc(sync.dma_start(out=wo_rot[:, par * D:(par + 1) * D],
                                        in_=dr_woutbf[128 * ftile:128 * (ftile + 1), :]), 16)
            SY(ldw)
            ln = s_ld.n
            def fw(v, ftile=ftile, par=par, ln=ln):
                v.wait_ge(s_ld.h, ln)
                s_v.inc(v.tensor_scalar_mul(wo_rot[:, par * D:(par + 1) * D],
                                            wo_rot[:, par * D:(par + 1) * D],
                                            smalls[:, 8 + ftile:9 + ftile]))
            V(fw)
            vvf = s_v.n
            def mmo(t, ftile=ftile, par=par, ln=ln, vv=vv, vvf=vvf):
                t.wait_ge(s_ld.h, ln)
                t.wait_ge(s_v.h, max(vv, vvf))
                for tt in range(NTT):
                    PZ = PS[tt]
                    for half in range(2):
                        t.matmul(PZ[:, half * 512:(half + 1) * 512],
                                 g_blk[:, ftile * BLK + tt * 128:ftile * BLK + (tt + 1) * 128],
                                 wo_rot[:, par * D + half * 512: par * D + (half + 1) * 512],
                                 start=(ftile == 0), stop=(ftile == NFIN - 1))
                if ftile == NFIN - 1:
                    pe_done(t)
                else:
                    s_pe.inc(t.nop())
            T(mmo)
        if d_g is not None:
            def dbg_g(sync, b=b, vv=vv):
                sync.wait_ge(s_v.h, vv)
                s_st.inc(sync.dma_start(out=d_g[b], in_=g_blk[:]), 16)
            SY(dbg_g)
        pn = s_pe.n
        for tt in range(NTT):
            s0 = 6 * tt
            par = tt % 2
            stn = s_st.n
            def he(v, tt=tt, par=par, pn=pn, s0=s0, stn=stn):
                v.wait_ge(s_pe.h, pn)
                if stn:
                    v.wait_ge(s_st.h, stn)
                PZ = PS[tt]
                hbuf = h_rot[:, par * D:(par + 1) * D]
                v.tensor_scalar_mul(sc128[:, 1024:1024 + D], u_tm[:, tt * D:(tt + 1) * D], stats[:, s0 + 4:s0 + 5])
                s_v.inc(v.tensor_tensor(hbuf, sc128[:, 1024:1024 + D], PZ[:], ALU.add))
            V(he)
            vn = s_v.n
            def hs(sync, b=b, tt=tt, par=par, vn=vn):
                sync.wait_ge(s_v.h, vn)
                s_st.inc(sync.dma_start(out=dr_h[b * BLK + tt * 128: b * BLK + (tt + 1) * 128, :],
                                        in_=h_rot[:, par * D:(par + 1) * D]), 16)
                if d_h is not None:
                    s_st.inc(sync.dma_start(out=d_h[b * BLK + tt * 128: b * BLK + (tt + 1) * 128, :],
                                            in_=h_rot[:, par * D:(par + 1) * D]), 16)
            SY(hs)
            def hns(s, tt=tt, par=par, s0=s0, vn=vn):
                s.wait_ge(s_v.h, vn)
                s.activation(sc128[:, 0:D], h_rot[:, par * D:(par + 1) * D], AF.Square,
                             accum_out=stats[:, s0 + 5:s0 + 6])
                s_s.inc(s.activation(stats[:, s0 + 5:s0 + 6], stats[:, s0 + 5:s0 + 6], AF.Sqrt,
                                     bias=epsc[:, 1:2], scale=1.0 / D))
            SC(hns)
            sn = s_s.n
            def hnv(v, tt=tt, par=par, s0=s0, sn=sn):
                v.wait_ge(s_s.h, sn)
                v.reciprocal(stats[:, s0 + 5:s0 + 6], stats[:, s0 + 5:s0 + 6])
                s_v.inc(v.tensor_scalar_mul(hn_rot[:, par * D:(par + 1) * D],
                                            h_rot[:, par * D:(par + 1) * D], stats[:, s0 + 5:s0 + 6]))
            V(hnv)
            vn2 = s_v.n
            def trh(t, tt=tt, par=par, vn2=vn2):
                t.wait_ge(s_v.h, vn2)
                PZb = PS[tt][:].bitcast(BF16)
                for k in range(NKD):
                    t.matmul(PZb[:, k * 128:(k + 1) * 128],
                             hn_rot[:, par * D + k * 128: par * D + (k + 1) * 128],
                             ident_s[:], is_transpose=True, start=True, stop=True)
                pe_done(t)
            T(trh)
            pn2 = s_pe.n
            def evh(v, tt=tt, pn2=pn2):
                v.wait_ge(s_pe.h, pn2)
                PZb = PS[tt][:].bitcast(BF16)
                par2 = tt % 2
                s_v.inc(v.tensor_copy(hnf_rot[:, par2 * D:(par2 + 1) * D], PZb[:, 0:D]))
            V(evh)
            vn3 = s_v.n
            def hnst(sync, b=b, tt=tt, vn3=vn3):
                sync.wait_ge(s_v.h, vn3)
                par2 = tt % 2
                for k in range(NKD):
                    s_st.inc(sync.dma_start(
                        out=dr_hn[k * 128:(k + 1) * 128, b * BLK + tt * 128: b * BLK + (tt + 1) * 128],
                        in_=hnf_rot[:, par2 * D + k * 128: par2 * D + (k + 1) * 128]), 16)
                    if d_hn is not None:
                        s_st.inc(sync.dma_start(
                            out=d_hn[k * 128:(k + 1) * 128, b * BLK + tt * 128: b * BLK + (tt + 1) * 128],
                            in_=hnf_rot[:, par2 * D + k * 128: par2 * D + (k + 1) * 128]), 16)
            SY(hnst)

    def emit_m2():
        barrier()
        es1.close()
        es2 = contextlib.ExitStack()
        sb2 = lambda n, sh, dt_: es2.enter_context(nc.sbuf_tensor(n, sh, dt_))
        g1_bf = sb2("g1_bf", [128, NFF * BLK], BF16)
        relu_s = sb2("relu_s", [128, 4 * BLK], BF16)
        hn_s = sb2("hn_s", [128, NKD * BLK], BF16)
        x1_s = sb2("x1_s", [128, 2 * D], F32)
        wproj_rot = sb2("wproj_rot", [128, 11 * D], BF16)
        pm2_s = sb2("pm2_s", [128, 2 * D], F32) if d_pm2 is not None else None

        esw = contextlib.ExitStack()
        wbig = esw.enter_context(nc.sbuf_tensor("wbig", [128, NKD * FF], BF16))

        vnw = s_v.n
        def ldwfc(sync, vnw=vnw):
            sync.wait_ge(s_v.h, vnw)
            sync.wait_ge(s_ws.h, preload.wready["wfc"])
            for k in range(NKD):
                s_ld.inc(sync.dma_start(out=wbig[:, k * FF:(k + 1) * FF],
                                        in_=dr_wfcbf[128 * k:128 * (k + 1), :]), 16)
        SY(ldwfc)
        wfc_ld = s_ld.n


        mlp_state = {"last_v": 0, "last_s": 0, "m1_pe_end": 0, "tail_free_v": 0,
                     "tail_free_s": 0, "pending": [], "ys4_free": {}}

        def mlp_tb(tb, proj_src, is_ffn, rhs_ld, pair_tail, rhs_v=0):
            sq2_at = {}
            rl_at = {}
            work = mlp_state["pending"]
            mlp_state["pending"] = []
            p01v, p01s = mlp_state["last_v"], mlp_state["last_s"]
            for f0 in range(0, NFF, 4):
                pns = {}
                for f in range(f0, f0 + 4):
                    PZ = PS[(f % 4) // 2]
                    off = ((f % 4) % 2) * 512
                    def mmf(t, f=f, PZ=PZ, off=off, rhs_ld=rhs_ld, p01v=p01v, p01s=p01s, rhs_v=rhs_v):
                        if f == 0:
                            t.wait_ge(s_ld.h, wfc_ld)
                            if rhs_ld:
                                t.wait_ge(s_ld.h, rhs_ld)
                            if rhs_v:
                                t.wait_ge(s_v.h, rhs_v)
                        if f < 4:
                            if p01v:
                                t.wait_ge(s_v.h, p01v)
                            if p01s:
                                t.wait_ge(s_s.h, p01s)
                            if EXPT_SERIAL and mlp_state["tail_free_v"]:
                                t.wait_ge(s_v.h, mlp_state["tail_free_v"])
                        else:
                            t.wait_ge(s_v.h, sq2_at[f - 4])
                            t.wait_ge(s_s.h, rl_at[f - 4])
                        for k in range(NKD):
                            t.matmul(PZ[:, off:off + 512],
                                     wbig[:, k * FF + f * 128:k * FF + (f + 1) * 128],
                                     hn_s[:, k * BLK:(k + 1) * BLK],
                                     start=(k == 0), stop=(k == NKD - 1))
                        pe_done(t)
                    T(mmf)
                    pns[f] = s_pe.n
                def rl(sc, f0=f0, pns=dict(pns)):
                    for f in range(f0, f0 + 4):
                        PZ = PS[(f % 4) // 2]
                        off = ((f % 4) % 2) * 512
                        sc.wait_ge(s_pe.h, pns[f])
                        s_s.inc(sc.activation(relu_s[:, (f % 4) * BLK:(f % 4 + 1) * BLK],
                                              PZ[:, off:off + 512], AF.Relu))
                SC(rl)
                for f in range(f0, f0 + 4):
                    rl_at[f] = s_s.n - (f0 + 3 - f)
                for f in range(f0, f0 + 4):
                    PZ = PS[(f % 4) // 2]
                    off = ((f % 4) % 2) * 512
                    sn = rl_at[f]
                    def sq2(v, f=f, PZ=PZ, off=off, sn=sn, dst=mlp_state.get("last_st", 0)):
                        if f == 0 and dst:
                            v.wait_ge(s_st.h, dst)
                        v.wait_ge(s_s.h, sn)
                        s_v.inc(v.tensor_tensor(g1_bf[:, f * BLK:(f + 1) * BLK],
                                                relu_s[:, (f % 4) * BLK:(f % 4 + 1) * BLK], PZ[:, off:off + 512], ALU.mult))
                    V(sq2)
                    sq2_at[f] = s_v.n
                if work and f0 in (12, 28):
                    work.pop(0)()
            mlp_state["last_v"] = s_v.n
            mlp_state["last_s"] = s_s.n
            mlp_state["m1_pe_end"] = s_pe.n
            g1_done = s_v.n
            if d_g1 is not None and not is_ffn:
                def dbg_g1(sync, tb=tb, g1_done=g1_done):
                    sync.wait_ge(s_v.h, g1_done)
                    s_st.inc(sync.dma_start(out=d_g1[tb], in_=g1_bf[:]), 16)
                SY(dbg_g1)
                mlp_state["last_st"] = s_st.n
            tfv, tfs = mlp_state["tail_free_v"], mlp_state["tail_free_s"]
            mmp_at = {}
            lns = {}

            def emit_ldwp(pair, f, pv):
                slot = f % 11
                def ldwp(q, f=f, slot=slot, pv=pv, is_ffn=is_ffn, proj_src=proj_src):
                    if pv:
                        q.wait_ge(s_pe.h, pv)
                    q.wait_ge(s_ws.h, preload.wready["Wproj" if is_ffn else "wproj"])
                    if is_ffn:
                        s_wp.inc(q.dma_start(out=wproj_rot[:, slot * D:(slot + 1) * D],
                                             in_=proj_src[:, f * 128:(f + 1) * 128], transpose=True), 16)
                    else:
                        s_wp.inc(q.dma_start(out=wproj_rot[:, slot * D:(slot + 1) * D],
                                             in_=proj_src[f * 128:(f + 1) * 128, :]), 16)
                if f % 2 == 1 and not is_ffn:
                    G(ldwp)
                else:
                    SY(ldwp)
                lns[(pair, f)] = s_wp.n

            for pair in range(2):
                for j in range(7):
                    if pair == 1:
                        pv = mmp_at[(0, min(j + 25, NFF - 1))]
                    else:
                        pv = mlp_state.get("prev_tail7", [0] * 7)[j]
                    emit_ldwp(pair, j, pv)
                for f in range(NFF):
                    slot = f % 11
                    ln = lns[(pair, min(f + 6, NFF - 1))]
                    def mmp(t, f=f, pair=pair, slot=slot, ln=ln, g1_done=g1_done, tfv=tfv, tfs=tfs):
                        t.wait_ge(s_wp.h, ln)
                        if f == 0:
                            t.wait_ge(s_v.h, g1_done if pair == 0 else mlp_state["pt_v"])
                            if pair == 0:
                                if tfv:
                                    t.wait_ge(s_v.h, tfv)
                                if tfs:
                                    t.wait_ge(s_s.h, tfs)
                            else:
                                if mlp_state["pt_s"]:
                                    t.wait_ge(s_s.h, mlp_state["pt_s"])
                        for ti in range(2):
                            tt = 2 * pair + ti
                            for hf in range(2):
                                t.matmul(PS[2 + ti][:, hf * 512:(hf + 1) * 512],
                                         g1_bf[:, f * BLK + tt * 128:f * BLK + (tt + 1) * 128],
                                         wproj_rot[:, slot * D + hf * 512:slot * D + (hf + 1) * 512],
                                         start=(f == 0), stop=(f == NFF - 1))
                        if f == NFF - 1:
                            pe_done(t)
                        else:
                            s_pe.inc(t.nop())
                    T(mmp)
                    mmp_at[(pair, f)] = s_pe.n
                    if f + 7 < NFF:
                        emit_ldwp(pair, f + 7, mmp_at[(pair, f)])
                if pair == 1:
                    mlp_state["prev_tail7"] = [mmp_at[(1, min(j + 25, NFF - 1))] for j in range(7)]
                if pair == 0:
                    if work:
                        work.pop(0)()
                else:
                    while work:
                        work.pop(0)()
                tv, ts = pair_tail(tb, pair, s_pe.n)
                if pair == 0:
                    mlp_state["pt_v"], mlp_state["pt_s"] = tv, ts
                else:
                    mlp_state["tail_free_v"], mlp_state["tail_free_s"] = tv, ts

        def m2a_tail(tb, pair, pe_cnt):
            lastv = 0
            dumped = 0
            if d_pm2 is not None:
                stp = s_st.n
                def dmpv(v, pe_cnt=pe_cnt, stp=stp):
                    v.wait_ge(s_pe.h, pe_cnt)
                    if stp:
                        v.wait_ge(s_st.h, stp)
                    v.tensor_copy(pm2_s[:, 0:D], PS[2][:])
                    s_v.inc(v.tensor_copy(pm2_s[:, D:2 * D], PS[3][:]))
                V(dmpv)
                vq = s_v.n
                def dmp(sync, tb=tb, pair=pair, vq=vq):
                    sync.wait_ge(s_v.h, vq)
                    s_st.inc(sync.dma_start(out=d_pm2[tb, pair, 0], in_=pm2_s[:, 0:D]), 16)
                    s_st.inc(sync.dma_start(out=d_pm2[tb, pair, 1], in_=pm2_s[:, D:2 * D]), 16)
                SY(dmp)
                dumped = s_st.n
            for ti in range(2):
                tt = 2 * pair + ti
                par = tt % 2
                vv = s_v.n
                sv = s_s.n
                stn = s_st.n
                def ldxh(sync, tb=tb, tt=tt, par=par, vv=vv, sv=sv):
                    if vv:
                        sync.wait_ge(s_v.h, vv)
                    if sv:
                        sync.wait_ge(s_s.h, sv)
                    s_ld.inc(sync.dma_start(out=x0_rot[:, par * D:(par + 1) * D],
                                            in_=xm[tb * BLK + tt * 128:tb * BLK + (tt + 1) * 128, :]), 16)
                    s_ld.inc(sync.dma_start(out=h_rot[:, par * D:(par + 1) * D],
                                            in_=dr_h[tb * BLK + tt * 128:tb * BLK + (tt + 1) * 128, :]), 16)
                SY(ldxh)
                ln = s_ld.n
                def x1a(v, ti=ti, tt=tt, par=par, ln=ln, pe_cnt=pe_cnt, stn=stn, dumped=dumped):
                    v.wait_ge(s_ld.h, ln)
                    v.wait_ge(s_pe.h, pe_cnt)
                    if dumped:
                        v.wait_ge(s_st.h, dumped)
                    if stn:
                        v.wait_ge(s_st.h, stn)
                    v.tensor_tensor(sc128[:, 0:D], x0_rot[:, par * D:(par + 1) * D],
                                    h_rot[:, par * D:(par + 1) * D], ALU.add)
                    s_v.inc(v.tensor_tensor(x1_s[:, par * D:(par + 1) * D], sc128[:, 0:D],
                                            PS[2 + ti][:], ALU.add))
                V(x1a)
                lastv = s_v.n
                vn = s_v.n
                def stx1(sync, tb=tb, tt=tt, par=par, vn=vn):
                    sync.wait_ge(s_v.h, vn)
                    s_st.inc(sync.dma_start(out=dr_x1[tb * BLK + tt * 128:tb * BLK + (tt + 1) * 128, :],
                                            in_=x1_s[:, par * D:(par + 1) * D]), 16)
                    if d_x1 is not None:
                        s_st.inc(sync.dma_start(out=d_x1[tb * BLK + tt * 128:tb * BLK + (tt + 1) * 128, :],
                                                in_=x1_s[:, par * D:(par + 1) * D]), 16)
                SY(stx1)
            return lastv, 0

        for tb in range(NBLK):
            pv0 = mlp_state["m1_pe_end"] if tb > 0 else 0
            def ldhn(sync, tb=tb, pv0=pv0):
                if pv0:
                    sync.wait_ge(s_pe.h, pv0)
                for k in range(NKD):
                    s_ld.inc(sync.dma_start(out=hn_s[:, k * BLK:(k + 1) * BLK],
                                            in_=dr_hn[k * 128:(k + 1) * 128, tb * BLK:(tb + 1) * BLK]), 16)
            SY(ldhn)
            hnld = s_ld.n
            mlp_tb(tb, dr_wprojbf, False, hnld, m2a_tail)

        barrier()
        esw.close()
        esw2 = contextlib.ExitStack()
        wbig2 = esw2.enter_context(nc.sbuf_tensor("wbig2", [128, NKD * FF], BF16))
        ys4 = esw2.enter_context(nc.sbuf_tensor("ys4", [128, 4 * D], F32))
        mem_g = esw2.enter_context(nc.sbuf_tensor("mem_g", [128, D], F32))
        spk_g = esw2.enter_context(nc.sbuf_tensor("spk_g", [128, D], F32))
        spk2_g = esw2.enter_context(nc.sbuf_tensor("spk2_g", [128, D], F32))
        spks_g = esw2.enter_context(nc.sbuf_tensor("spks_g", [128, D], F32))
        def ldWfcT(sync):
            sync.wait_ge(s_ws.h, preload.wready["Wfc"])
            for k in range(NKD):
                for f0 in range(0, FF, 512):
                    s_ld.inc(sync.dma_start(out=wbig2[:, k * FF + f0:k * FF + f0 + 512],
                                            in_=dr_Wfcbf[f0:f0 + 512, k * 128:(k + 1) * 128],
                                            transpose=True), 16)
        SY(ldWfcT)
        WfcT_ld = s_ld.n

        def m2b_tail(tb, pair, pe_cnt):
            lasts = 0
            for ti in range(2):
                tt = 2 * pair + ti
                vv0, gv0 = mlp_state["ys4_free"].get(tt, (0, 0))
                def evy(sc, ti=ti, tt=tt, pe_cnt=pe_cnt, gv0=gv0, vv0=vv0):
                    sc.wait_ge(s_pe.h, pe_cnt)
                    if gv0:
                        sc.wait_ge(s_g.h, gv0)
                    if vv0:
                        sc.wait_ge(s_v.h, vv0)
                    s_s.inc(sc.activation(ys4[:, tt * D:(tt + 1) * D], PS[2 + ti][:], AF.Copy))
                SC(evy)
                lasts = s_s.n
                mlp_state["pending"].append(
                    lambda tb=tb, tt=tt, sn=s_s.n: emit_piece(tb, tt, sn))
            return 0, lasts

        def emit_piece(tb, tt, sn):
            par = tt % 2
            vvx = s_v.n
            def ldx1b(g, tb=tb, tt=tt, par=par, vvx=vvx):
                if vvx:
                    g.wait_ge(s_v.h, vvx)
                s_px.inc(g.dma_start(out=h_rot[:, par * D:(par + 1) * D],
                                     in_=dr_x1[tb * BLK + tt * 128:tb * BLK + (tt + 1) * 128, :]), 16)
            G(ldx1b)
            lnb = s_px.n
            ys = ys4[:, tt * D:(tt + 1) * D]
            spks = [spk_g, spk2_g]
            gw0 = s_g.n
            def snn_v1(v, sn=sn, ys=ys, gw0=gw0):
                v.wait_ge(s_s.h, sn)
                if gw0:
                    v.wait_ge(s_g.h, gw0)
                v.tensor_single_scalar(spk_g[:], ys, 1.0, ALU.is_gt)
                s_v.inc(v.scalar_tensor_tensor(mem_g[:], ys, 1.9, spk_g[:], ALU.mult, ALU.subtract))
            V(snn_v1)
            vq = s_v.n
            def snn_g1(g, vq=vq):
                g.wait_ge(s_v.h, vq)
                s_g.inc(g.tensor_copy(spks_g[:], spk_g[:]))
            G(snn_g1)
            for step in (3, 4, 5, 6):
                buf = spks[(step - 3 + 1) % 2]
                gq = s_g.n
                def snn_vs(v, step=step, buf=buf, ys=ys, gq=gq):
                    v.wait_ge(s_g.h, gq)
                    v.tensor_single_scalar(buf[:], mem_g[:], 1.0, ALU.is_gt)
                    if step < 6:
                        v.scalar_tensor_tensor(mem_g[:], mem_g[:], 0.9, buf[:], ALU.mult, ALU.subtract)
                        i = v.tensor_tensor(mem_g[:], mem_g[:], ys, ALU.add)
                    else:
                        i = v.nop()
                    s_v.inc(i)
                V(snn_vs)
                vq2 = s_v.n
                def snn_gs(g, buf=buf, vq2=vq2):
                    g.wait_ge(s_v.h, vq2)
                    s_g.inc(g.tensor_add(spks_g[:], spks_g[:], buf[:]))
                G(snn_gs)
            gn2 = s_g.n
            stn = s_st.n
            def outa(v, tt=tt, par=par, gn2=gn2, lnb=lnb, stn=stn):
                v.wait_ge(s_g.h, gn2)
                v.wait_ge(s_px.h, lnb)
                if stn:
                    v.wait_ge(s_st.h, stn)
                s_v.inc(v.scalar_tensor_tensor(x1_s[:, par * D:(par + 1) * D], spks_g[:], 0.2,
                                               h_rot[:, par * D:(par + 1) * D], ALU.mult, ALU.add))
            V(outa)
            vn = s_v.n
            def sto(g, tb=tb, tt=tt, par=par, vn=vn):
                g.wait_ge(s_v.h, vn)
                s_st.inc(g.dma_start(out=out[tb * BLK + tt * 128:tb * BLK + (tt + 1) * 128, :],
                                     in_=x1_s[:, par * D:(par + 1) * D]), 16)
            G(sto)
            mlp_state["ys4_free"][tt] = (s_v.n, s_g.n)

        for tb in range(NBLK):
            for tt in range(NTT):
                par = tt % 2
                vv = s_v.n
                sv = s_s.n
                def ldx1(sync, tb=tb, tt=tt, par=par, vv=vv, sv=sv):
                    if vv:
                        sync.wait_ge(s_v.h, vv)
                    if sv:
                        sync.wait_ge(s_s.h, sv)
                    s_ld.inc(sync.dma_start(out=x0_rot[:, par * D:(par + 1) * D],
                                            in_=dr_x1[tb * BLK + tt * 128:tb * BLK + (tt + 1) * 128, :]), 16)
                SY(ldx1)
                ln = s_ld.n
                s0 = 6 * tt
                xa = x0_rot[:, par * D:(par + 1) * D]
                def sq(sc, ln=ln, s0=s0, xa=xa):
                    sc.wait_ge(s_ld.h, ln)
                    s_s.inc(sc.activation(sc128[:, 0:D], xa, AF.Square, accum_out=stats[:, s0 + 1:s0 + 2]))
                SC(sq)
                sn = s_s.n
                def v1(v, ln=ln, sn=sn, s0=s0, xa=xa):
                    v.wait_ge(s_ld.h, ln)
                    v.wait_ge(s_s.h, sn)
                    v.reduce_sum(stats[:, s0:s0 + 1], xa, axis=AX)
                    v.tensor_scalar_mul(stats[:, s0:s0 + 1], stats[:, s0:s0 + 1], 1.0 / D)
                    v.tensor_scalar_mul(stats[:, s0 + 1:s0 + 2], stats[:, s0 + 1:s0 + 2], 1.0 / D)
                    v.scalar_tensor_tensor(stats[:, s0 + 1:s0 + 2], stats[:, s0:s0 + 1], stats[:, s0:s0 + 1],
                                           stats[:, s0 + 1:s0 + 2], ALU.mult, ALU.subtract)
                    s_v.inc(v.tensor_scalar_mul(stats[:, s0 + 1:s0 + 2], stats[:, s0 + 1:s0 + 2], -1.0))
                V(v1)
                vn1 = s_v.n
                def s1(sc, vn1=vn1, s0=s0):
                    sc.wait_ge(s_v.h, vn1)
                    s_s.inc(sc.activation(stats[:, s0 + 2:s0 + 3], stats[:, s0 + 1:s0 + 2], AF.Sqrt, bias=epsc[:, 0:1]))
                SC(s1)
                sn1 = s_s.n
                pvh = s_pe.n
                def v2(v, sn1=sn1, s0=s0, xa=xa, par=par, pvh=pvh):
                    v.wait_ge(s_s.h, sn1)
                    if pvh:
                        v.wait_ge(s_pe.h, pvh)
                    v.reciprocal(stats[:, s0 + 2:s0 + 3], stats[:, s0 + 2:s0 + 3])
                    s_v.inc(v.tensor_scalar(hn_rot[:, par * D:(par + 1) * D], xa,
                                            stats[:, s0:s0 + 1], stats[:, s0 + 2:s0 + 3],
                                            ALU.subtract, ALU.mult))
                V(v2)
                vn2 = s_v.n
                def trt(t, tt=tt, par=par, vn2=vn2):
                    t.wait_ge(s_v.h, vn2)
                    P3b = P3[:].bitcast(BF16)
                    for k in range(NKD):
                        t.matmul(P3b[:, k * 128:(k + 1) * 128],
                                 hn_rot[:, par * D + k * 128:par * D + (k + 1) * 128],
                                 ident_s[:], is_transpose=True, start=True, stop=True)
                    pe_done(t)
                T(trt)
                pn = s_pe.n
                def evt(v, tt=tt, pn=pn):
                    v.wait_ge(s_pe.h, pn)
                    P3b = P3[:].bitcast(BF16)
                    dst = hn_s[:].rearrange("p (k b) -> p k b", k=NKD)[:, :, tt * 128:(tt + 1) * 128]
                    s_v.inc(v.tensor_copy(dst, P3b[:, 0:1024].rearrange("p (k l) -> p k l", k=NKD)))
                V(evt)
            if tb == 0:
                def wWT(t):
                    t.wait_ge(s_ld.h, WfcT_ld)
                T(wWT)
            mlp_tb(tb, dr_Wprojbf, True, 0, m2b_tail, rhs_v=s_v.n)
        for th in mlp_state["pending"]:
            th()
        mlp_state["pending"] = []
        esw2.close()
        es2.close()

    preload()
    SA = stop_after or "all"
    def go(tag):
        order = ["preload", "prefront0", "pre0a", "pre0b", "pre0c", "pre0d", "pre0", "pre", "m1front0", "m1acum", "m1xh", "m1scores", "m1uh", "m1ea", "m1quad", "m1yoff", "m1c0", "m1b0", "m1", "all"]
        return order.index(SA) >= order.index(tag)
    if go("prefront0"):
        for b in range(NBLK):
            emit_block_front(xp, b, is_pre=True)
            if SA == "prefront0":
                break
            if go("pre0a"):
                for c in range(NCB):
                    emit_acum(b, c, is_pre=True)
                    if go("pre0b"):
                        emit_xh_tm(b, c)
                    if go("pre0c"):
                        emit_uh(b, c, is_pre=True)
                    if go("pre0d"):
                        emit_scan2(b, c, is_pre=True)
                    if not go("pre0"):
                        break
            if not go("pre"):
                break
        if go("pre"):
            emit_flag()
    if go("m1front0"):
        for b in range(NBLK):
            emit_block_front(xm, b, is_pre=False)
            if SA == "m1front0":
                break
            for c in range(NCB):
                emit_acum(b, c, is_pre=False)
                if go("m1xh"):
                    emit_xh_tm(b, c)
                if go("m1scores"):
                    emit_scores(b, c)
                if go("m1uh"):
                    emit_uh(b, c, is_pre=False)
                if go("m1ea"):
                    emit_ea(b, c)
                if go("m1quad"):
                    emit_ssd_quadratic(b, c)
                if go("m1yoff"):
                    emit_yoff_and_y(b, c)
                if go("m1c0"):
                    emit_scan2(b, c, is_pre=False)
                if not go("m1b0"):
                    break
            if not go("m1b0"):
                break
            emit_outproj_h(b)
            if SA == "m1b0":
                break
        if go("all"):
            emit_m2()

    stn = s_st.n
    axn = s_ax.n
    def fin(sync, stn=stn, axn=axn):
        if stn:
            sync.wait_ge(s_st.h, stn)
        if axn:
            sync.wait_ge(s_ax.h, axn)
    SY(fin)
    barrier()

    es.close()
    return nc, dbg_outs


def _consts():
    sl = np.arange(128)
    triu = (sl[:, None] <= sl[None, :])
    c = {}
    c["triu_f"] = triu.astype(np.float32)
    c["triu_bf"] = triu.astype(ml_dtypes.bfloat16)
    c["maskneg"] = np.where(~triu, np.float32(MASKVAL), np.float32(0)).astype(ml_dtypes.bfloat16)
    c["ident_in"] = np.eye(128, dtype=ml_dtypes.bfloat16)
    c["onesrow_in"] = np.ones((1, 128), dtype=ml_dtypes.bfloat16)
    e = np.zeros((32, 16, 128), dtype=ml_dtypes.bfloat16)
    for hp in range(16):
        e[hp, hp, :] = 1
    c["eind_in"] = e.reshape(32, 2048)
    e2 = np.zeros((NH, NH, PH), dtype=ml_dtypes.bfloat16)
    for h in range(NH):
        e2[h, h, :] = 1
    c["eind2_in"] = e2.reshape(NH, DIN)
    return c


def kernel(L0_attn_x, L0_attn_w_in, L0_attn_conv_w, L0_attn_conv_b, L0_attn_dt_bias,
           L0_attn_a_log, L0_attn_d_skip, L0_attn_norm_w, L0_attn_w_out, L0_attn_w_fc,
           L0_attn_w_proj, L0_ffn_W_fc, L0_ffn_W_proj, _dbg=(), _stop=None, _nc_cache={}):
    x = np.ascontiguousarray(L0_attn_x)
    B, L, _ = x.shape
    key = (tuple(sorted(_dbg)), _stop)
    if key not in _nc_cache:
        _nc_cache[key] = build_program(dbg=_dbg, stop_after=_stop)
    nc, dbg_outs = _nc_cache[key]

    consts = _consts()
    dsk = np.asarray(L0_attn_d_skip, np.float32)
    dskd = np.zeros((128, NFIN * 128), np.float32)
    for t in range(NFIN):
        vals = np.concatenate([np.repeat(dsk[2 * t], 64), np.repeat(dsk[2 * t + 1], 64)])
        blockd = np.zeros((128, 128), np.float32)
        np.fill_diagonal(blockd, vals)
        dskd[:, t * 128:(t + 1) * 128] = blockd

    base = dict(
        w_in=np.ascontiguousarray(L0_attn_w_in, np.float32),
        conv_w=np.ascontiguousarray(L0_attn_conv_w, np.float32),
        conv_b=np.ascontiguousarray(L0_attn_conv_b, np.float32).reshape(CONVD, 1),
        dt_bias=np.ascontiguousarray(L0_attn_dt_bias, np.float32).reshape(NH, 1),
        a_log=np.ascontiguousarray(L0_attn_a_log, np.float32).reshape(NH, 1),
        d_skip_diag=dskd,
        norm_w=np.ascontiguousarray(L0_attn_norm_w, np.float32).reshape(DIN, 1),
        w_out=np.ascontiguousarray(L0_attn_w_out, np.float32),
        w_fc=np.ascontiguousarray(L0_attn_w_fc, np.float32),
        w_proj=np.ascontiguousarray(L0_attn_w_proj, np.float32),
        Wfc=np.ascontiguousarray(L0_ffn_W_fc, np.float32),
        Wproj=np.ascontiguousarray(L0_ffn_W_proj, np.float32),
        **consts,
    )
    in_maps = []
    for core in range(8):
        b, half = core // 2, core % 2
        m = dict(base)
        m["xm"] = np.ascontiguousarray(x[b, half * TOK:(half + 1) * TOK, :], np.float32)
        m["xp"] = np.ascontiguousarray(x[b, 0:TOK, :], np.float32)
        m["flag_in"] = np.full((128, 1), float(half), np.float32)
        in_maps.append(m)

    res = run_bass_kernel_spmd(nc, in_maps, core_ids=list(range(8)))
    outs = np.zeros((B, L, D), np.float32)
    for core in range(8):
        b, half = core // 2, core % 2
        outs[b, half * TOK:(half + 1) * TOK, :] = res.results[core]["out"]
    if _dbg:
        return outs, [res.results[c] for c in range(8)]
    return outs


if __name__ == "__main__":
    nc, _ = build_program()
    print("built")


# version 14
# speedup vs baseline: 1.0176x; 1.0176x over previous
import contextlib
import numpy as np
import ml_dtypes
import concourse.bass as bass
import concourse.mybir as mybir
from concourse.bass_utils import run_bass_kernel_spmd

F32 = mybir.dt.float32
BF16 = mybir.dt.bfloat16
AF = mybir.ActivationFunctionType
ALU = mybir.AluOpType
AX = mybir.AxisListType.X

D = 1024
DIN = 2048
DST = 128
NH = 32
PH = 64
CONVD = DIN + 2 * DST
DINPROJ = 2 * DIN + 2 * DST + NH
EXPT_SERIAL = False
FF = 4096
NKD = 8
NFIN = 16
NFCV = 18
NFF = 32

TOK = 2048
BLK = 512
NBLK = 4
C = 128
NCB = 4
NTT = 4

EPS_LN = 1e-5
EPS_OUT = 1.1920929e-07
EPS_GATE = 1e-05
BETA = 0.9
MASKVAL = -1e30


def build_program(dbg=(), stop_after=None):
    dbg = set(dbg)
    nc = bass.Bass("TRN2", target_bir_lowering=False, debug=False)

    xm = nc.dram_tensor("xm", [TOK, D], F32, kind="ExternalInput").ap()
    xp = nc.dram_tensor("xp", [TOK, D], F32, kind="ExternalInput").ap()
    w_in = nc.dram_tensor("w_in", [D, DINPROJ], F32, kind="ExternalInput").ap()
    conv_w = nc.dram_tensor("conv_w", [CONVD, 4], F32, kind="ExternalInput").ap()
    conv_b = nc.dram_tensor("conv_b", [CONVD, 1], F32, kind="ExternalInput").ap()
    dt_bias = nc.dram_tensor("dt_bias", [NH, 1], F32, kind="ExternalInput").ap()
    a_log = nc.dram_tensor("a_log", [NH, 1], F32, kind="ExternalInput").ap()
    d_skip_diag = nc.dram_tensor("d_skip_diag", [128, NFIN * 128], F32, kind="ExternalInput").ap()
    norm_w = nc.dram_tensor("norm_w", [DIN, 1], F32, kind="ExternalInput").ap()
    w_out = nc.dram_tensor("w_out", [DIN, D], F32, kind="ExternalInput").ap()
    w_fc = nc.dram_tensor("w_fc", [D, FF], F32, kind="ExternalInput").ap()
    w_proj = nc.dram_tensor("w_proj", [FF, D], F32, kind="ExternalInput").ap()
    Wfc = nc.dram_tensor("Wfc", [D, FF], F32, kind="ExternalInput").ap()
    Wproj = nc.dram_tensor("Wproj", [FF, D], F32, kind="ExternalInput").ap()
    triu_f = nc.dram_tensor("triu_f", [128, 128], F32, kind="ExternalInput").ap()
    triu_bf = nc.dram_tensor("triu_bf", [128, 128], BF16, kind="ExternalInput").ap()
    maskneg = nc.dram_tensor("maskneg", [128, 128], BF16, kind="ExternalInput").ap()
    ident_in = nc.dram_tensor("ident_in", [128, 128], BF16, kind="ExternalInput").ap()
    onesrow_in = nc.dram_tensor("onesrow_in", [1, 128], BF16, kind="ExternalInput").ap()
    eind_in = nc.dram_tensor("eind_in", [32, 2048], BF16, kind="ExternalInput").ap()
    eind2_in = nc.dram_tensor("eind2_in", [NH, DIN], BF16, kind="ExternalInput").ap()
    flag_in = nc.dram_tensor("flag_in", [128, 1], F32, kind="ExternalInput").ap()

    out = nc.dram_tensor("out", [TOK, D], F32, kind="ExternalOutput").ap()
    dr_h = nc.dram_tensor("dr_h", [TOK, D], F32, kind="Internal").ap()
    dr_x1 = nc.dram_tensor("dr_x1", [TOK, D], F32, kind="Internal").ap()
    dr_hn = nc.dram_tensor("dr_hn", [D, TOK], BF16, kind="Internal").ap()
    dr_woutbf = nc.dram_tensor("dr_woutbf", [DIN, D], BF16, kind="Internal").ap()
    dr_winzbf = nc.dram_tensor("dr_winzbf", [D, 2048], BF16, kind="Internal").ap()
    dr_wfcbf = nc.dram_tensor("dr_wfcbf", [D, FF], BF16, kind="Internal").ap()
    dr_wprojbf = nc.dram_tensor("dr_wprojbf", [FF, D], BF16, kind="Internal").ap()
    dr_Wfcbf = nc.dram_tensor("dr_Wfcbf", [D, FF], BF16, kind="Internal").ap()
    dr_Wprojbf = nc.dram_tensor("dr_Wprojbf", [FF, D], BF16, kind="Internal").ap()
    dr_flat = nc.dram_tensor("dr_flat", [2, NH * C], BF16, kind="Internal").ap()

    dbg_outs = {}
    def dbgt(name, shape, dtype=F32):
        if name in dbg:
            dbg_outs[name] = nc.dram_tensor("dbg_" + name, shape, dtype, kind="ExternalOutput").ap()
        return dbg_outs.get(name)

    d_u = dbgt("u", [NBLK, 128, NKD * BLK], BF16)
    d_xcv = dbgt("xcv", [NBLK, 128, NFCV * BLK], BF16)
    d_z = dbgt("z", [NBLK, 128, NTT * DIN], BF16)
    d_dt = dbgt("dt", [NBLK, NH, BLK])
    d_acum = dbgt("acum", [NBLK * NCB, NH, C])
    d_y = dbgt("y", [NBLK * NCB, 128, DIN])
    d_g = dbgt("g", [NBLK, 128, NFIN * BLK], BF16)
    d_h = dbgt("h", [TOK, D])
    d_st0 = dbgt("st0", [128, 6 * NTT])
    d_x00 = dbgt("x00", [128, 2 * D])
    d_eind = dbgt("eind", [33, 2048], BF16)
    d_state = dbgt("state", [NBLK * NCB, 128, DIN])
    d_pstate = dbgt("pstate", [128, DIN])
    d_x1 = dbgt("x1", [TOK, D])
    d_hn = dbgt("hn", [NKD * 128, TOK], BF16)
    d_g1 = dbgt("g1", [NBLK, 128, NFF * BLK], BF16)
    d_pm2 = dbgt("pm2", [NBLK, 2, 2, 128, D])

    es = contextlib.ExitStack()
    es1 = contextlib.ExitStack()
    sbt = lambda name, shape, dtype: es.enter_context(nc.sbuf_tensor(name, shape, dtype))
    sb1 = lambda name, shape, dtype: es1.enter_context(nc.sbuf_tensor(name, shape, dtype))
    pst = lambda name, shape, dtype: es.enter_context(nc.psum_tensor(name, shape, dtype))

    x0_rot = sbt("x0_rot", [128, 2 * D], F32)
    h_rot = sbt("h_rot", [128, 2 * D], F32)
    hn_rot = sbt("hn_rot", [128, 2 * D], BF16)
    hnf_rot = sbt("hnf_rot", [128, 2 * D], BF16)
    wo_rot = sbt("wo_rot", [128, 2 * D], BF16)
    stats = sbt("stats", [128, 6 * NTT], F32)
    smalls = sbt("smalls", [128, 28], F32)
    consts = sbt("consts", [128, 4], BF16)
    onesrow = sbt("onesrow", [1, 128], BF16)
    ident_s = sbt("ident_s", [128, 128], BF16)
    sc128 = sbt("sc128", [128, DIN], F32)
    epsc = sbt("epsc", [128, 4], F32)

    WXB = DINPROJ - 2048
    w_in_bf = sb1("w_in_bf", [128, NKD * WXB], BF16)
    xcv_bf = sb1("xcv_bf", [128, NFCV * BLK], BF16)
    z_bf = sb1("z_bf", [128, NTT * DIN], BF16)
    state = sb1("state", [128, DIN], F32)
    state_bf = sb1("state_bf", [128, DIN], BF16)
    egt = sb1("egt", [128, NH * C // 2], BF16)
    ea_bf = sb1("ea_bf", [128, DIN], BF16)
    uh_bf = sb1("uh_bf", [128, DIN], BF16)
    uh2_bf = sb1("uh2_bf", [128, DIN], BF16)
    xh_tm = sb1("xh_tm", [128, DIN], BF16)
    b_tm = sb1("b_tm", [128, DST], BF16)
    mskst = sb1("mskst", [128, C], BF16)
    y_f32 = sb1("y_f32", [128, DIN], F32)
    g_blk = sb1("g_blk", [128, NFIN * BLK], BF16)
    u_tm = sb1("u_tm", [128, NTT * D], BF16)
    u_fm = sb1("u_fm", [128, NKD * BLK], BF16)
    tails = sb1("tails", [128, NFCV * 3], F32)
    dt_blk = sb1("dt_blk", [NH, BLK], F32)
    dtb_bf = sb1("dtb_bf", [NH, BLK], BF16)
    a_bf = sb1("a_bf", [NH, BLK], BF16)
    a_tmb = sb1("a_tmb", [128, NH], BF16)
    acum_f = sb1("acum_f", [NH, C], F32)
    acf_h = sb1("acf_h", [16, C], F32)
    ab_hi = sb1("ab_hi", [64, C], BF16)
    ab_lo = sb1("ab_lo", [64, C], BF16)
    hi_t = sb1("hi_t", [16, 2 * C], BF16)
    lo_t = sb1("lo_t", [16, 2 * C], BF16)
    eind_hi = sb1("eind_hi", [33, NH * C // 2], BF16)
    eind_lo = sb1("eind_lo", [33, NH * C // 2], BF16)
    eind2_s = sb1("eind2_s", [NH, DIN], BF16)
    dtdec_bf = sb1("dtdec_bf", [NH, C], BF16)
    expac_f = sb1("expac_f", [NH, C], F32)
    expac_bf = sb1("expac_bf", [NH, C], BF16)
    dec_bf = sb1("dec_bf", [NH, C], BF16)
    triu_fs = sb1("triu_fs", [128, 128], F32)
    triu_bfs = sb1("triu_bfs", [128, 128], BF16)
    maskneg_s = sb1("maskneg_s", [128, 128], BF16)
    dskd_s = sb1("dskd_s", [128, NFIN * 128], BF16)
    convw_s = sb1("convw_s", [128, NFCV * 4], F32)
    convb_s = sb1("convb_s", [128, NFCV], F32)
    cdrep = sb1("cdrep", [NH, 128], BF16)

    SM_DTB, SM_NA, SM_TMP, SM_FLAG, SM_TMP2 = 0, 1, 2, 3, 4

    P0 = pst("P0", [128, 1024], F32)
    P1 = pst("P1", [128, 1024], F32)
    P2 = pst("P2", [128, 1024], F32)
    P3 = pst("P3", [128, 1024], F32)
    PS = [P0, P1, P2, P3]

    class Sem:
        def __init__(self, name):
            self.h = es.enter_context(nc.semaphore(name))
            self.n = 0
        def inc(self, instr, k=1):
            instr.then_inc(self.h, k)
            self.n += k
    s_ld = Sem("s_ld")
    s_ax = Sem("s_ax")
    s_st = Sem("s_st")
    s_pe = Sem("s_pe")
    s_v = Sem("s_v")
    s_s = Sem("s_s")
    s_g = Sem("s_g")
    s_wl = Sem("s_wl")
    s_ws = Sem("s_ws")
    s_wp = Sem("s_wp")
    s_px = Sem("s_px")
    ALLS = None

    block = es.enter_context(nc.Block())

    _OPNAMES = {"tensor_copy", "tensor_tensor", "tensor_add", "tensor_sub", "tensor_mul",
                "tensor_max", "tensor_scalar", "tensor_single_scalar", "tensor_scalar_add",
                "tensor_scalar_sub", "tensor_scalar_mul", "tensor_scalar_max", "tensor_scalar_min",
                "tensor_relu", "tensor_reduce", "reduce_max", "reduce_sum", "scalar_tensor_tensor",
                "tensor_tensor_scan", "memset", "memzero", "reciprocal", "select", "activation",
                "copy", "add", "mul", "square", "sqrt", "sign", "transpose"}

    class _DW:
        def __init__(self, eng):
            self._e = eng
        def __getattr__(self, name):
            attr = getattr(self._e, name)
            if name not in _OPNAMES:
                return attr
            def wrapped(*args, **kw):
                r = attr(*args, **kw)
                need = kw.get("accum_out") is not None
                if not need and args:
                    try:
                        need = args[0].free_size() < 512
                    except Exception:
                        need = True
                if need:
                    self._e.drain()
                    return self._e.nop()
                return r
            return wrapped

    def T(f): block.tensor(f)
    def V(f): block.vector(lambda e: f(_DW(e)))
    def SC(f): block.scalar(lambda e: f(_DW(e)))
    def G(f): block.gpsimd(lambda e: f(_DW(e)))
    def SY(f): block.sync(f)

    def wait_upto(eng, *sems):
        for s, v in [(s, s.n) for s in sems]:
            if v:
                eng.wait_ge(s.h, v)

    def barrier():
        sems = [s_pe, s_v, s_s, s_g, s_ld, s_ax, s_st]
        vals = [(s.h, s.n) for s in sems if s.n]
        def mk(eng, vals=vals):
            for h, v in vals:
                eng.wait_ge(h, v)
        T(mk); V(mk); SC(mk); G(mk); SY(mk)

    from concourse.hw_specs import get_activation_tables
    _TABSETS = list(get_activation_tables(nc.m.arch))
    _NLE = _TABSETS.index("natural_log_exp_and_others")
    _SILU = _TABSETS.index("silu_and_others")

    def load_tab(sc, idx):
        sc.add_instruction(mybir.InstLoadActFuncSet(
            name=nc.get_next_instruction_name(), act_func_set_id=idx, ins=[], outs=[]))

    def pe_done(t):
        t.drain()
        s_pe.inc(t.nop())

    def preload():
        def ld(sync):
            s_ld.inc(sync.dma_start(out=triu_fs[:], in_=triu_f), 16)
            s_ld.inc(sync.dma_start(out=triu_bfs[:], in_=triu_bf), 16)
            s_ld.inc(sync.dma_start(out=maskneg_s[:], in_=maskneg), 16)
            s_ld.inc(sync.dma_start(out=ident_s[:], in_=ident_in), 16)
            s_ld.inc(sync.dma_start(out=onesrow[:], in_=onesrow_in), 16)
            s_ld.inc(sync.dma_start(out=eind_hi[0:32, :], in_=eind_in), 16)
            s_ld.inc(sync.dma_start(out=eind_lo[0:32, :], in_=eind_in), 16)
            s_ld.inc(sync.dma_start(out=eind2_s[:], in_=eind2_in), 16)
            s_ld.inc(sync.dma_start(out=sc128[:, 0:2048], in_=d_skip_diag), 16)
            s_ld.inc(sync.dma_start(out=smalls[0:NH, SM_DTB:SM_DTB + 1], in_=dt_bias), 16)
            s_ld.inc(sync.dma_start(out=smalls[0:NH, SM_NA:SM_NA + 1], in_=a_log), 16)
            s_ld.inc(sync.dma_start(out=smalls[:, SM_FLAG:SM_FLAG + 1], in_=flag_in), 16)
            for t in range(NFCV):
                s_ld.inc(sync.dma_start(out=convw_s[:, 4 * t:4 * t + 4], in_=conv_w[128 * t:128 * (t + 1), :]), 16)
                s_ld.inc(sync.dma_start(out=convb_s[:, t:t + 1], in_=conv_b[128 * t:128 * (t + 1), :]), 16)
            for t in range(NFIN):
                s_ld.inc(sync.dma_start(out=smalls[:, 8 + t:9 + t], in_=norm_w[128 * t:128 * (t + 1), :]), 16)
        SY(ld)
        ldn = s_ld.n
        def pre_s(s, ldn=ldn):
            s.wait_ge(s_ld.h, ldn)
            s_s.inc(s.activation(smalls[0:NH, SM_NA:SM_NA + 1], smalls[0:NH, SM_NA:SM_NA + 1], AF.Exp))
        SC(pre_s)
        sn = s_s.n
        def pre_v(v, sn=sn, ldn=ldn):
            v.wait_ge(s_s.h, sn)
            v.tensor_scalar_mul(smalls[0:NH, SM_NA:SM_NA + 1], smalls[0:NH, SM_NA:SM_NA + 1], -1.0)
            v.memset(consts[:, 0:1], 1.0)
            v.memset(epsc[:, 0:1], EPS_LN)
            v.memset(epsc[:, 1:2], EPS_OUT)
            v.memset(epsc[:, 2:3], EPS_GATE)
            v.memset(epsc[:, 3:4], EPS_GATE)
            v.memset(ab_hi[0:32, :], 0.0)
            v.memset(ab_lo[0:32, :], 0.0)
            v.memset(ab_hi[32:64, :], 1.0)
            v.memset(ab_lo[32:64, :], 1.0)
            v.memset(state[:], 0.0)
            v.memset(tails[:], 0.0)
            s_v.inc(v.tensor_copy(dskd_s[:], sc128[:, 0:2048]))
        V(pre_v)
        for k in range(NKD):
            for c0 in range(0, WXB, 2048):
                cw = min(2048, WXB - c0)
                vn = s_v.n
                def ldk(sync, k=k, c0=c0, cw=cw, vn=vn):
                    sync.wait_ge(s_v.h, vn)
                    s_ld.inc(sync.dma_start(out=sc128[:, 0:cw], in_=w_in[128 * k:128 * (k + 1), 2048 + c0:2048 + c0 + cw]), 16)
                ln = s_ld.n + 16
                def cstk(v, k=k, c0=c0, cw=cw, ln=ln):
                    v.wait_ge(s_ld.h, ln)
                    s_v.inc(v.tensor_copy(w_in_bf[:, WXB * k + c0:WXB * k + c0 + cw], sc128[:, 0:cw]))
                SY(ldk); V(cstk)
        barrier()
        zf32 = z_bf[:].bitcast(F32)
        wready = {}
        pieces = []
        def add_stream(name, src, dst, rows, cols):
            for r0 in range(0, rows, 128):
                for c0 in range(0, cols, 2048):
                    pieces.append((name, src, dst, r0, c0, min(2048, cols - c0)))
        add_stream("winz", w_in[:, 0:2048], dr_winzbf, D, 2048)
        add_stream("wout", w_out, dr_woutbf, DIN, D)
        add_stream("wfc", w_fc, dr_wfcbf, D, FF)
        add_stream("wproj", w_proj, dr_wprojbf, FF, D)
        add_stream("Wfc", Wfc, dr_Wfcbf, D, FF)
        add_stream("Wproj", Wproj, dr_Wprojbf, FF, D)
        NPC = len(pieces)
        ld_at = [0] * NPC
        st_at = [0] * NPC
        def emit_gstream(g):
            for i in range(NPC + 1):
                if i < NPC:
                    name, src, dst, r0, c0, cw = pieces[i]
                    par = i % 2
                    g.dma_start(out=zf32[:, par * 2048:par * 2048 + cw],
                                in_=src[r0:r0 + 128, c0:c0 + cw]).then_inc(s_wl.h, 16)
                    s_wl.n += 16
                    ld_at[i] = s_wl.n
                if i >= 1:
                    j = i - 1
                    name, src, dst, r0, c0, cw = pieces[j]
                    par = j % 2
                    g.wait_ge(s_wl.h, ld_at[j])
                    if j >= 2:
                        g.wait_ge(s_ws.h, 16 * (j - 1))
                    g.tensor_copy(g_blk[:, par * 2048:par * 2048 + cw],
                                  zf32[:, par * 2048:par * 2048 + cw])
                    g.dma_start(out=dst[r0:r0 + 128, c0:c0 + cw],
                                in_=g_blk[:, par * 2048:par * 2048 + cw]).then_inc(s_ws.h, 16)
                    s_ws.n += 16
        G(emit_gstream)
        for i, p in enumerate(pieces):
            wready[p[0]] = 16 * (i + 1)
        preload.wready = wready
        preload.stream_done = 16 * NPC

    def emit_stats_u(tt, par, rms=True, ldwait=0):
        s0 = 6 * tt
        xa = x0_rot[:, par * D:(par + 1) * D]
        c_mean = stats[:, s0:s0 + 1]
        c_var = stats[:, s0 + 1:s0 + 2]
        c_rln = stats[:, s0 + 2:s0 + 3]
        c_R = stats[:, s0 + 3:s0 + 4]
        c_ir2 = stats[:, s0 + 4:s0 + 5]
        def sq(s, ldwait=ldwait):
            if ldwait:
                s.wait_ge(s_ld.h, ldwait)
            s_s.inc(s.activation(sc128[:, 0:D], xa, AF.Square, accum_out=c_var))
        SC(sq)
        sn = s_s.n
        def v1(v, sn=sn):
            v.wait_ge(s_s.h, sn)
            v.reduce_sum(c_mean, xa, axis=AX)
            v.tensor_scalar_mul(c_mean, c_mean, 1.0 / D)
            v.tensor_scalar_mul(c_var, c_var, 1.0 / D)
            v.scalar_tensor_tensor(c_var, c_mean, c_mean, c_var, ALU.mult, ALU.subtract)
            s_v.inc(v.tensor_scalar_mul(c_var, c_var, -1.0))
        V(v1)
        vn = s_v.n
        def s1(s, vn=vn):
            s.wait_ge(s_v.h, vn)
            s_s.inc(s.activation(c_rln, c_var, AF.Sqrt, bias=epsc[:, 0:1]))
        SC(s1)
        if not rms:
            sn2 = s_s.n
            def v3(v, sn2=sn2, tt=tt):
                v.wait_ge(s_s.h, sn2)
                v.reciprocal(c_R, c_rln)
                v.tensor_copy(c_rln, c_R)
                s_v.inc(v.tensor_scalar(u_tm[:, tt * D:(tt + 1) * D], xa, c_mean, c_rln,
                                        ALU.subtract, ALU.mult))
            V(v3)
            return
        sn2 = s_s.n
        def v2(v, sn2=sn2):
            v.wait_ge(s_s.h, sn2)
            v.reciprocal(c_R, c_rln)
            v.tensor_copy(c_rln, c_R)
            v.tensor_mul(c_R, c_var, c_rln)
            s_v.inc(v.tensor_mul(c_R, c_R, c_rln))
        V(v2)
        vn2 = s_v.n
        def s2(s, vn2=vn2):
            s.wait_ge(s_v.h, vn2)
            s_s.inc(s.activation(c_ir2, c_R, AF.Sqrt, bias=epsc[:, 1:2]))
        SC(s2)
        sn3 = s_s.n
        def v3(v, sn3=sn3, tt=tt):
            v.wait_ge(s_s.h, sn3)
            v.reciprocal(c_R, c_ir2)
            v.tensor_mul(c_R, c_R, c_rln)
            s_v.inc(v.tensor_scalar(u_tm[:, tt * D:(tt + 1) * D], xa, c_mean, c_R,
                                    ALU.subtract, ALU.mult))
        V(v3)

    def emit_u_transpose(tt):
        vn = s_v.n
        def tr(t, tt=tt, vn=vn):
            t.wait_ge(s_v.h, vn)
            P3b = P3[:].bitcast(BF16)
            for k in range(NKD):
                t.matmul(P3b[:, k * 128:(k + 1) * 128],
                         u_tm[:, tt * D + k * 128: tt * D + (k + 1) * 128],
                         ident_s[:], is_transpose=True, start=True, stop=True)
            pe_done(t)
        T(tr)
        pn = s_pe.n
        def ev(v, tt=tt, pn=pn):
            v.wait_ge(s_pe.h, pn)
            P3b = P3[:].bitcast(BF16)
            dst = u_fm[:].rearrange("p (k b) -> p k b", k=NKD)[:, :, tt * 128:(tt + 1) * 128]
            src = P3b[:, 0:1024].rearrange("p (k l) -> p k l", k=NKD)
            s_v.inc(v.tensor_copy(dst, src))
        V(ev)

    def emit_block_front(xsrc, b, is_pre):
        for tt in range(NTT):
            par = tt % 2
            vv = s_v.n
            sv = s_s.n
            def ldx(sync, tt=tt, par=par, vv=vv, sv=sv, xsrc=xsrc, b=b):
                if vv:
                    sync.wait_ge(s_v.h, vv)
                if sv:
                    sync.wait_ge(s_s.h, sv)
                s_ld.inc(sync.dma_start(out=x0_rot[:, par * D:(par + 1) * D],
                                        in_=xsrc[b * BLK + tt * 128: b * BLK + (tt + 1) * 128, :]), 16)
            ln = s_ld.n + 16
            SY(ldx)
            def wx(v, ln=ln):
                v.wait_ge(s_ld.h, ln)
            V(wx)
            emit_stats_u(tt, par, rms=True, ldwait=ln)
            emit_u_transpose(tt)
        if d_u is not None and not is_pre:
            vn = s_v.n
            def dbg_st(sync, b=b, vn=vn):
                sync.wait_ge(s_v.h, vn)
                s_st.inc(sync.dma_start(out=d_u[b], in_=u_fm[:]), 16)
            SY(dbg_st)
        if d_st0 is not None and not is_pre and b == 0:
            vn = s_v.n
            def dbg_s0(sync, vn=vn):
                sync.wait_ge(s_v.h, vn)
                s_st.inc(sync.dma_start(out=d_st0, in_=stats[:]), 16)
                if d_x00 is not None:
                    s_st.inc(sync.dma_start(out=d_x00, in_=x0_rot[:]), 16)
            SY(dbg_s0)

        uvn = s_v.n
        if not is_pre:
            for zs in range(4):
                half = zs % 2
                sv = s_s.n
                for k in range(NKD):
                    par = k % 4
                    pvw = s_pe.n
                    def ldwz(sync, zs=zs, k=k, par=par, pvw=pvw):
                        if pvw:
                            sync.wait_ge(s_pe.h, pvw)
                        sync.wait_ge(s_ws.h, preload.wready["winz"])
                        s_ld.inc(sync.dma_start(out=wo_rot[:, par * 512:(par + 1) * 512],
                                                in_=dr_winzbf[128 * k:128 * (k + 1), zs * 512:(zs + 1) * 512]), 16)
                    SY(ldwz)
                    ln = s_ld.n
                    def mmz(t, zs=zs, k=k, par=par, ln=ln, sv=sv, uvn=uvn, half=half):
                        t.wait_ge(s_ld.h, ln)
                        if k == 0:
                            t.wait_ge(s_v.h, uvn)
                            if sv:
                                t.wait_ge(s_s.h, sv)
                        for tt in range(NTT):
                            PZ = PS[tt]
                            t.matmul(PZ[:, half * 512:(half + 1) * 512],
                                     u_fm[:, k * BLK + tt * 128:k * BLK + (tt + 1) * 128],
                                     wo_rot[:, par * 512:(par + 1) * 512],
                                     start=(k == 0), stop=(k == NKD - 1))
                        if k == NKD - 1:
                            pe_done(t)
                        else:
                            s_pe.inc(t.nop())
                    T(mmz)
                pn = s_pe.n
                def evz(s, zs=zs, pn=pn, half=half, b=b):
                    s.wait_ge(s_pe.h, pn)
                    if b == 0:
                        s.wait_ge(s_ws.h, preload.stream_done)
                    for tt in range(NTT):
                        i = s.activation(z_bf[:, tt * DIN + zs * 512: tt * DIN + (zs + 1) * 512],
                                         PS[tt][:, half * 512:(half + 1) * 512], AF.Silu)
                    s_s.inc(i)
                SC(evz)
            if d_z is not None:
                sn = s_s.n
                def dbg_z(sync, b=b, sn=sn):
                    sync.wait_ge(s_s.h, sn)
                    s_st.inc(sync.dma_start(out=d_z[b], in_=z_bf[:]), 16)
                SY(dbg_z)

        mstart = 16
        for m in range(mstart, 35):
            ft = m - 16
            pslot = 2 + (ft % 2)
            PZ = PS[pslot]
            rows = 128 if m < 34 else 32
            vv = s_v.n
            sv = s_s.n
            def mmx(t, m=m, PZ=PZ, vv=vv, sv=sv, uvn=uvn, rows=rows):
                t.wait_ge(s_v.h, vv)
                for k in range(NKD):
                    t.matmul(PZ[0:rows, 0:512],
                             w_in_bf[:, WXB * k + (m - 16) * 128: WXB * k + (m - 16) * 128 + rows],
                             u_fm[:, k * BLK:(k + 1) * BLK],
                             start=(k == 0), stop=(k == NKD - 1))
                pe_done(t)
            T(mmx)
            pn = s_pe.n
            if m < 34:
                def cv(v, ft=ft, PZ=PZ, pn=pn):
                    v.wait_ge(s_pe.h, pn)
                    wc = lambda k: convw_s[:, 4 * ft + k:4 * ft + k + 1]
                    acc = sc128[:, 0:512]
                    v.tensor_scalar(acc, PZ[:, 0:512], wc(3), convb_s[:, ft:ft + 1], ALU.mult, ALU.add)
                    for k in (2, 1, 0):
                        sh = 3 - k
                        v.scalar_tensor_tensor(acc[:, sh:512], PZ[:, 0:512 - sh], wc(k),
                                               acc[:, sh:512], ALU.mult, ALU.add)
                    tl = tails[:, 3 * ft:3 * ft + 3]
                    v.scalar_tensor_tensor(acc[:, 0:3], tl[:, 0:3], wc(0), acc[:, 0:3], ALU.mult, ALU.add)
                    v.scalar_tensor_tensor(acc[:, 0:2], tl[:, 1:3], wc(1), acc[:, 0:2], ALU.mult, ALU.add)
                    v.scalar_tensor_tensor(acc[:, 0:1], tl[:, 2:3], wc(2), acc[:, 0:1], ALU.mult, ALU.add)
                    s_v.inc(v.tensor_copy(tl, PZ[:, 509:512]))
                V(cv)
                vn = s_v.n
                def cvs(s, ft=ft, vn=vn):
                    s.wait_ge(s_v.h, vn)
                    s_s.inc(s.activation(xcv_bf[:, ft * BLK:(ft + 1) * BLK], sc128[:, 0:512], AF.Silu))
                SC(cvs)
            else:
                def dtev(s, PZ=PZ, pn=pn):
                    s.wait_ge(s_pe.h, pn)
                    s.activation(dt_blk[:, :], PZ[0:NH, 0:512], AF.Relu, bias=smalls[0:NH, SM_DTB:SM_DTB + 1])
                    s.activation(sc128[0:NH, 512:1024], PZ[0:NH, 0:512], AF.Abs, bias=smalls[0:NH, SM_DTB:SM_DTB + 1])
                    s_s.inc(s.activation(sc128[0:NH, 512:1024], sc128[0:NH, 512:1024], AF.Exp, scale=-1.0))
                SC(dtev)
                sn = s_s.n
                def dtv(v, sn=sn):
                    v.wait_ge(s_s.h, sn)
                    s_v.inc(v.tensor_scalar_add(sc128[0:NH, 512:1024], sc128[0:NH, 512:1024], 1.0))
                V(dtv)
                vn = s_v.n
                def dtl(s, vn=vn):
                    s.wait_ge(s_v.h, vn)
                    s_s.inc(s.activation(sc128[0:NH, 512:1024], sc128[0:NH, 512:1024], AF.Ln))
                SC(dtl)
                sn2 = s_s.n
                def dtf(v, sn2=sn2, b=b):
                    v.wait_ge(s_s.h, sn2)
                    v.tensor_add(dt_blk[:, :], dt_blk[:, :], sc128[0:NH, 512:1024])
                    v.tensor_copy(dtb_bf[:, :], dt_blk[:, :])
                    s_v.inc(v.tensor_scalar_mul(a_bf[:, :], dt_blk[:, :], smalls[0:NH, SM_NA:SM_NA + 1]))
                V(dtf)
        if d_xcv is not None and not is_pre:
            sn = s_s.n
            def dbg_xc(sync, b=b, sn=sn):
                sync.wait_ge(s_s.h, sn)
                s_st.inc(sync.dma_start(out=d_xcv[b], in_=xcv_bf[:]), 16)
            SY(dbg_xc)
        if d_dt is not None and not is_pre:
            vn = s_v.n
            def dbg_dt(sync, b=b, vn=vn):
                sync.wait_ge(s_v.h, vn)
                s_st.inc(sync.dma_start(out=d_dt[b], in_=dt_blk[:, :]), 16)
            SY(dbg_dt)

    def emit_acum(b, c, is_pre):
        vv0 = s_v.n
        def tra(t, c=c, vv0=vv0):
            if vv0:
                t.wait_ge(s_v.h, vv0)
            P2b = P2[:].bitcast(BF16)
            t.matmul(P2b[:, 0:NH], a_bf[:, c * C:(c + 1) * C], ident_s[0:NH, 0:NH],
                     is_transpose=True, start=True, stop=True)
            pe_done(t)
        T(tra)
        pn = s_pe.n
        def eva(v, pn=pn):
            v.wait_ge(s_pe.h, pn)
            P2b = P2[:].bitcast(BF16)
            s_v.inc(v.tensor_copy(a_tmb[:], P2b[:, 0:NH]))
        V(eva)
        vn = s_v.n
        def mmac(t, vn=vn, is_pre=is_pre):
            t.wait_ge(s_v.h, vn)
            t.matmul(P2[0:NH, 0:C], a_tmb[:], triu_bfs[:], start=True, stop=True)
            if not is_pre:
                t.matmul(P2[0:16, 256:256 + C], a_tmb[:, 0:16], triu_bfs[:], start=True, stop=True)
                t.matmul(P2[0:16, 512:512 + C], a_tmb[:, 16:32], triu_bfs[:], start=True, stop=True)
            pe_done(t)
        T(mmac)
        pn2 = s_pe.n
        def evac(v, pn2=pn2):
            v.wait_ge(s_pe.h, pn2)
            s_v.inc(v.tensor_copy(acum_f[:, :], P2[0:NH, 0:C]))
        V(evac)
        if not is_pre:
            pvh = s_pe.n
            axh = s_ax.n
            def evhh(v, pvh=pvh, axh=axh):
                v.wait_ge(s_pe.h, pvh)
                if axh:
                    v.wait_ge(s_ax.h, axh)
                for half in range(2):
                    src = P2[0:16, 256 + 256 * half:256 + 256 * half + C]
                    hs = hi_t[:, half * C:(half + 1) * C]
                    ls = lo_t[:, half * C:(half + 1) * C]
                    v.tensor_copy(acf_h[:, :], src)
                    v.tensor_copy(hs, src)
                    v.tensor_tensor(acf_h[:, :], acf_h[:, :], hs, ALU.subtract)
                    v.tensor_copy(ls, acf_h[:, :])
                v.tensor_scalar_mul(ab_hi[0:16, :], hi_t[:, 0:C], -1.0)
                s_v.inc(v.tensor_scalar_mul(ab_lo[0:16, :], lo_t[:, 0:C], -1.0))
            V(evhh)
            vnh = s_v.n
            def flh(sync, vnh=vnh):
                sync.wait_ge(s_v.h, vnh)
                s_ax.inc(sync.dma_start(out=dr_flat[0:1, 0:2048], in_=hi_t[:, 0:C]), 16)
                s_ax.inc(sync.dma_start(out=dr_flat[0:1, 2048:4096], in_=hi_t[:, C:2 * C]), 16)
                s_ax.inc(sync.dma_start(out=dr_flat[1:2, 0:2048], in_=lo_t[:, 0:C]), 16)
                s_ax.inc(sync.dma_start(out=dr_flat[1:2, 2048:4096], in_=lo_t[:, C:2 * C]), 16)
            SC(flh)
            axn2 = s_ax.n
            def flr0(sync, axn2=axn2):
                sync.wait_ge(s_ax.h, axn2)
                s_ax.inc(sync.dma_start(out=eind_hi[32:33, :], in_=dr_flat[0:1, 0:2048]), 16)
                s_ax.inc(sync.dma_start(out=eind_lo[32:33, :], in_=dr_flat[1:2, 0:2048]), 16)
            SC(flr0)
            def prep_half1():
                pv1 = s_pe.n
                def ab1(v, pv1=pv1):
                    v.wait_ge(s_pe.h, pv1)
                    v.tensor_scalar_mul(ab_hi[0:16, :], hi_t[:, C:2 * C], -1.0)
                    s_v.inc(v.tensor_scalar_mul(ab_lo[0:16, :], lo_t[:, C:2 * C], -1.0))
                V(ab1)
                def flr1(sync, pv1=pv1):
                    sync.wait_ge(s_pe.h, pv1)
                    s_ax.inc(sync.dma_start(out=eind_hi[32:33, :], in_=dr_flat[0:1, 2048:4096]), 16)
                    s_ax.inc(sync.dma_start(out=eind_lo[32:33, :], in_=dr_flat[1:2, 2048:4096]), 16)
                SC(flr1)
            emit_acum.prep_half1 = prep_half1
        vn2 = s_v.n
        def sdec(s, vn2=vn2, is_pre=is_pre):
            s.wait_ge(s_v.h, vn2)
            s.activation(dec_bf[:, :], acum_f[:, :], AF.Exp, bias=acum_f[:, C - 1:C], scale=-1.0)
            i = s.activation(expac_f[:, :], acum_f[:, :], AF.Exp)
            s_s.inc(i)
        SC(sdec)
        sn = s_s.n
        def vdec(v, sn=sn, is_pre=is_pre, c=c):
            v.wait_ge(s_s.h, sn)
            if is_pre:
                i = v.tensor_tensor(dtdec_bf[:, :], dtb_bf[:, c * C:(c + 1) * C], dec_bf[:, :], ALU.mult)
            else:
                i = v.tensor_copy(expac_bf[:, :], expac_f[:, :])
            s_v.inc(i)
        V(vdec)
        if d_acum is not None and not is_pre:
            vn3 = s_v.n
            def dbg_ac(sync, b=b, c=c, vn3=vn3):
                sync.wait_ge(s_v.h, vn3)
                s_st.inc(sync.dma_start(out=d_acum[b * NCB + c], in_=acum_f[:, :]), 16)
            SY(dbg_ac)

    def emit_xh_tm(b, c):
        sv = s_s.n
        vv = s_v.n
        def trx(t, c=c, sv=sv, vv=vv):
            t.wait_ge(s_s.h, sv)
            t.wait_ge(s_v.h, vv)
            P3b = P3[:].bitcast(BF16)
            for ftile in range(NFIN):
                t.matmul(P3b[:, ftile * 128:(ftile + 1) * 128],
                         xcv_bf[:, ftile * BLK + c * C: ftile * BLK + (c + 1) * C],
                         ident_s[:], is_transpose=True, start=True, stop=True)
            P2b = P2[:].bitcast(BF16)
            t.matmul(P2b[:, 256:256 + 128],
                     xcv_bf[:, 16 * BLK + c * C: 16 * BLK + (c + 1) * C],
                     ident_s[:], is_transpose=True, start=True, stop=True)
            pe_done(t)
        T(trx)
        pn = s_pe.n
        def evx(v, pn=pn):
            v.wait_ge(s_pe.h, pn)
            P3b = P3[:].bitcast(BF16)
            P2b = P2[:].bitcast(BF16)
            v.tensor_copy(xh_tm[:], P3b[:, 0:DIN])
            s_v.inc(v.tensor_copy(b_tm[:], P2b[:, 256:256 + 128]))
        V(evx)

    def emit_scores(b, c):
        sv = s_s.n
        vv0 = s_v.n
        def mms(t, c=c, sv=sv, vv0=vv0):
            t.wait_ge(s_s.h, sv)
            t.wait_ge(s_v.h, vv0)
            t.matmul(P2[:, 512:512 + 128],
                     xcv_bf[:, 16 * BLK + c * C:16 * BLK + (c + 1) * C],
                     xcv_bf[:, 17 * BLK + c * C:17 * BLK + (c + 1) * C],
                     start=True, stop=True)
            pe_done(t)
        T(mms)
        pn = s_pe.n
        def evs(v, pn=pn):
            v.wait_ge(s_pe.h, pn)
            s_v.inc(v.tensor_tensor(mskst[:], P2[:, 512:512 + 128], triu_fs[:], ALU.mult))
        V(evs)

    def emit_uh(b, c, is_pre):
        vv = s_v.n
        def mmdt(t, c=c, vv=vv, is_pre=is_pre):
            t.wait_ge(s_v.h, vv)
            lhs = dtdec_bf[:, :] if is_pre else dtb_bf[:, c * C:(c + 1) * C]
            for half in range(2):
                PZ = PS[half]
                for q in range(2):
                    t.matmul(PZ[:, q * 512:(q + 1) * 512], lhs,
                             eind2_s[:, half * 1024 + q * 512: half * 1024 + (q + 1) * 512],
                             start=True, stop=True)
            pe_done(t)
        T(mmdt)
        pn = s_pe.n
        def evuh(v, pn=pn, is_pre=is_pre):
            v.wait_ge(s_pe.h, pn)
            dst = uh2_bf if is_pre else uh_bf
            v.tensor_tensor(dst[:, 0:1024], xh_tm[:, 0:1024], P0[:], ALU.mult)
            s_v.inc(v.tensor_tensor(dst[:, 1024:2048], xh_tm[:, 1024:2048], P1[:], ALU.mult))
        V(evuh)
        if not is_pre:
            vv2 = s_v.n
            def mmdec(t, c=c, vv2=vv2):
                t.wait_ge(s_v.h, vv2)
                for half in range(2):
                    PZ = PS[half]
                    for q in range(2):
                        t.matmul(PZ[:, q * 512:(q + 1) * 512], dec_bf[:, :],
                                 eind2_s[:, half * 1024 + q * 512: half * 1024 + (q + 1) * 512],
                                 start=True, stop=True)
                pe_done(t)
            T(mmdec)
            pn2 = s_pe.n
            def evuh2(v, pn2=pn2):
                v.wait_ge(s_pe.h, pn2)
                v.tensor_tensor(uh2_bf[:, 0:1024], uh_bf[:, 0:1024], P0[:], ALU.mult)
                s_v.inc(v.tensor_tensor(uh2_bf[:, 1024:2048], uh_bf[:, 1024:2048], P1[:], ALU.mult))
            V(evuh2)

    def emit_ea(b, c):
        vv = s_v.n
        def mmea(t, vv=vv):
            t.wait_ge(s_v.h, vv)
            for half in range(2):
                PZ = PS[half]
                for q in range(2):
                    t.matmul(PZ[:, q * 512:(q + 1) * 512], expac_bf[:, :],
                             eind2_s[:, half * 1024 + q * 512: half * 1024 + (q + 1) * 512],
                             start=True, stop=True)
            pe_done(t)
        T(mmea)
        pn = s_pe.n
        def evea(s, pn=pn):
            s.wait_ge(s_pe.h, pn)
            s.activation(ea_bf[:, 0:1024], P0[:], AF.Copy)
            s_s.inc(s.activation(ea_bf[:, 1024:2048], P1[:], AF.Copy))
        SC(evea)

    def emit_ssd_quadratic(b, c):
        vv = s_v.n
        sv = s_s.n
        def mmdsk(t, c=c, sv=sv, vv=vv):
            t.wait_ge(s_s.h, sv)
            t.wait_ge(s_v.h, vv)
            for ftile in range(NFIN):
                PZ = P2 if ftile < 8 else P3
                off = (ftile % 8) * 128
                t.matmul(PZ[:, off:off + 128],
                         xcv_bf[:, ftile * BLK + c * C: ftile * BLK + (c + 1) * C],
                         dskd_s[:, ftile * 128:(ftile + 1) * 128],
                         start=((ftile % 8) % 4 == 0), stop=False)
            pe_done(t)
        T(mmdsk)
        if d_eind is not None and b == 0 and c == 0:
            axq = s_ax.n
            def dbg_ei(sync, axq=axq):
                sync.wait_ge(s_ax.h, axq)
                s_st.inc(sync.dma_start(out=d_eind, in_=eind_hi[:]), 16)
            SY(dbg_ei)
        for half in range(2):
            axn = s_ax.n
            vv2 = s_v.n
            sv2 = s_s.n
            def mmr(t, half=half, axn=axn, vv2=vv2, sv2=sv2):
                t.wait_ge(s_ax.h, axn)
                t.wait_ge(s_v.h, vv2)
                t.wait_ge(s_s.h, sv2)
                for q in range(4):
                    PZ = PS[q // 2]
                    off = (q % 2) * 512
                    t.matmul(PZ[:, off:off + 512], ab_hi[0:33, :],
                             eind_hi[:, q * 512:(q + 1) * 512], start=True, stop=False)
                    t.matmul(PZ[:, off:off + 512], ab_lo[0:33, :],
                             eind_lo[:, q * 512:(q + 1) * 512], start=False, stop=False)
                for hl in range(16):
                    PZ = PS[hl // 8]
                    off = (hl % 8) * 128
                    t.matmul(PZ[:, off:off + 128], ident_s[:], maskneg_s[:],
                             start=False, stop=True)
                pe_done(t)
            T(mmr)
            if half == 0:
                emit_acum.prep_half1()
            pn = s_pe.n
            def eve(s, pn=pn):
                s.wait_ge(s_pe.h, pn)
                s.activation(egt[:, 0:1024], P0[:], AF.Exp)
                s_s.inc(s.activation(egt[:, 1024:2048], P1[:], AF.Exp))
            SC(eve)
            sn = s_s.n
            def gtm(v, sn=sn):
                v.wait_ge(s_s.h, sn)
                src = mskst[:].unsqueeze(1).broadcast_to([128, 16, C])
                s_v.inc(v.tensor_tensor(egt[:].rearrange("p (h l) -> p h l", h=16),
                                        egt[:].rearrange("p (h l) -> p h l", h=16), src, ALU.mult))
            V(gtm)
            vn = s_v.n
            def mmyd(t, half=half, vn=vn):
                t.wait_ge(s_v.h, vn)
                PZ = P2 if half == 0 else P3
                for hl in range(16):
                    h = half * 16 + hl
                    t.matmul(PZ[:, hl * 64:hl * 64 + 64],
                             egt[:, hl * C:(hl + 1) * C],
                             uh_bf[:, h * PH:(h + 1) * PH],
                             start=False, stop=(hl == 15))
                pe_done(t)
            T(mmyd)

    def emit_yoff_and_y(b, c):
        vv = s_v.n
        sn0 = s_s.n
        def cpsb(v, vv=vv):
            s_v.inc(v.tensor_copy(state_bf[:], state[:]))
        V(cpsb)
        vn0 = s_v.n
        def mmyo(t, c=c, vn0=vn0, sn0=sn0):
            t.wait_ge(s_v.h, vn0)
            t.wait_ge(s_s.h, sn0)
            for h in range(NH):
                PZ = P0 if h < 16 else P1
                t.matmul(PZ[:, (h % 16) * 64:(h % 16) * 64 + 64],
                         xcv_bf[:, 17 * BLK + c * C:17 * BLK + (c + 1) * C],
                         state_bf[:, h * PH:(h + 1) * PH],
                         start=(h % 8 == 0), stop=(h % 16 == 15))
            pe_done(t)
        T(mmyo)
        pn = s_pe.n
        def ya(v, pn=pn, c=c):
            v.wait_ge(s_pe.h, pn)
            v.tensor_tensor(y_f32[:, 0:1024], P0[:], ea_bf[:, 0:1024], ALU.mult)
            v.tensor_tensor(y_f32[:, 1024:2048], P1[:], ea_bf[:, 1024:2048], ALU.mult)
            v.tensor_tensor(y_f32[:, 0:1024], y_f32[:, 0:1024], P2[:], ALU.add)
            s_v.inc(v.tensor_tensor(y_f32[:, 1024:2048], y_f32[:, 1024:2048], P3[:], ALU.add))
        V(ya)
        if d_y is not None:
            vn = s_v.n
            def dbg_y(sync, b=b, c=c, vn=vn):
                sync.wait_ge(s_v.h, vn)
                s_st.inc(sync.dma_start(out=d_y[b * NCB + c], in_=y_f32[:]), 16)
            SY(dbg_y)
        stn_y = s_st.n
        def yz(v, c=c, stn_y=stn_y):
            if stn_y:
                v.wait_ge(s_st.h, stn_y)
            s_v.inc(v.tensor_tensor(y_f32[:], y_f32[:], z_bf[:, c * DIN:(c + 1) * DIN], ALU.mult))
        V(yz)
        vn2 = s_v.n
        def gn(s, vn2=vn2):
            s.wait_ge(s_v.h, vn2)
            s.activation(sc128[:, 0:DIN], y_f32[:], AF.Square, accum_out=smalls[:, SM_TMP:SM_TMP + 1])
            s_s.inc(s.activation(smalls[:, SM_TMP:SM_TMP + 1], smalls[:, SM_TMP:SM_TMP + 1], AF.Sqrt,
                                 bias=epsc[:, 3:4], scale=1.0 / DIN))
        SC(gn)
        sn = s_s.n
        def gv(v, sn=sn):
            v.wait_ge(s_s.h, sn)
            v.reciprocal(smalls[:, SM_TMP2:SM_TMP2 + 1], smalls[:, SM_TMP:SM_TMP + 1])
            s_v.inc(v.tensor_scalar_mul(uh_bf[:], y_f32[:], smalls[:, SM_TMP2:SM_TMP2 + 1]))
        V(gv)
        vn3 = s_v.n
        def trg(t, c=c, vn3=vn3):
            t.wait_ge(s_v.h, vn3)
            P3b = P3[:].bitcast(BF16)
            for ftile in range(NFIN):
                t.matmul(P3b[:, ftile * 128:(ftile + 1) * 128],
                         uh_bf[:, ftile * 128:(ftile + 1) * 128],
                         ident_s[:], is_transpose=True, start=True, stop=True)
            pe_done(t)
        T(trg)
        pn2 = s_pe.n
        def evg(v, c=c, pn2=pn2, b=b):
            v.wait_ge(s_pe.h, pn2)
            if b == 0 and c == 0:
                v.wait_ge(s_ws.h, preload.stream_done)
            P3b = P3[:].bitcast(BF16)
            dst = g_blk[:].rearrange("p (f b) -> p f b", f=NFIN)[:, :, c * C:(c + 1) * C]
            s_v.inc(v.tensor_copy(dst, P3b.rearrange("p (f l) -> p f l", f=NFIN)))
        V(evg)

    def emit_scan2(b, c, is_pre, last_pre=False, first_main=False):
        vv = s_v.n
        def mmst(t, vv=vv):
            t.wait_ge(s_v.h, vv)
            for h in range(NH):
                PZ = P0 if h < 16 else P1
                t.matmul(PZ[:, (h % 16) * 64:(h % 16) * 64 + 64],
                         b_tm[:],
                         uh2_bf[:, h * PH:(h + 1) * PH],
                         start=(h % 8 == 0), stop=(h % 16 == 15))
            pe_done(t)
        T(mmst)
        def cdr(v):
            src = expac_f[:, C - 1:C].broadcast_to([NH, 128])
            s_v.inc(v.tensor_copy(cdrep[:, :], src))
        V(cdr)
        vn = s_v.n
        def mmcd(t, vn=vn):
            t.wait_ge(s_v.h, vn)
            for q in range(2):
                t.matmul(P2[:, q * 512:(q + 1) * 512], cdrep[:, :],
                         eind2_s[:, q * 512:(q + 1) * 512], start=True, stop=True)
                t.matmul(P3[:, q * 512:(q + 1) * 512], cdrep[:, :],
                         eind2_s[:, 1024 + q * 512: 1024 + (q + 1) * 512], start=True, stop=True)
            pe_done(t)
        T(mmcd)
        pn = s_pe.n
        def upd(v, pn=pn, first_main=first_main):
            v.wait_ge(s_pe.h, pn)
            if first_main:
                pass
            v.tensor_tensor(state[:, 0:1024], state[:, 0:1024], P2[:], ALU.mult)
            v.tensor_tensor(state[:, 1024:2048], state[:, 1024:2048], P3[:], ALU.mult)
            v.tensor_tensor(state[:, 0:1024], state[:, 0:1024], P0[:], ALU.add)
            s_v.inc(v.tensor_tensor(state[:, 1024:2048], state[:, 1024:2048], P1[:], ALU.add))
        V(upd)
        if d_state is not None and not is_pre:
            vn2 = s_v.n
            def dbg_s(sync, b=b, c=c, vn2=vn2):
                sync.wait_ge(s_v.h, vn2)
                s_st.inc(sync.dma_start(out=d_state[b * NCB + c], in_=state[:]), 16)
            SY(dbg_s)

    def emit_flag():
        def fl(v):
            v.tensor_scalar_mul(state[:], state[:], smalls[:, SM_FLAG:SM_FLAG + 1])
            s_v.inc(v.tensor_scalar_mul(tails[:], tails[:], smalls[:, SM_FLAG:SM_FLAG + 1]))
        V(fl)
        if d_pstate is not None:
            vn = s_v.n
            def dbg_p(sync, vn=vn):
                sync.wait_ge(s_v.h, vn)
                s_st.inc(sync.dma_start(out=d_pstate, in_=state[:]), 16)
            SY(dbg_p)

    def emit_outproj_h(b):
        vv = s_v.n
        for ftile in range(NFIN):
            par = ftile % 2
            pv = s_pe.n
            def ldw(sync, ftile=ftile, par=par, pv=pv):
                if pv:
                    sync.wait_ge(s_pe.h, pv)
                sync.wait_ge(s_ws.h, preload.wready["wout"])
                s_ld.inc(sync.dma_start(out=wo_rot[:, par * D:(par + 1) * D],
                                        in_=dr_woutbf[128 * ftile:128 * (ftile + 1), :]), 16)
            SY(ldw)
            ln = s_ld.n
            def fw(v, ftile=ftile, par=par, ln=ln):
                v.wait_ge(s_ld.h, ln)
                s_v.inc(v.tensor_scalar_mul(wo_rot[:, par * D:(par + 1) * D],
                                            wo_rot[:, par * D:(par + 1) * D],
                                            smalls[:, 8 + ftile:9 + ftile]))
            V(fw)
            vvf = s_v.n
            def mmo(t, ftile=ftile, par=par, ln=ln, vv=vv, vvf=vvf):
                t.wait_ge(s_ld.h, ln)
                t.wait_ge(s_v.h, max(vv, vvf))
                for tt in range(NTT):
                    PZ = PS[tt]
                    for half in range(2):
                        t.matmul(PZ[:, half * 512:(half + 1) * 512],
                                 g_blk[:, ftile * BLK + tt * 128:ftile * BLK + (tt + 1) * 128],
                                 wo_rot[:, par * D + half * 512: par * D + (half + 1) * 512],
                                 start=(ftile == 0), stop=(ftile == NFIN - 1))
                if ftile == NFIN - 1:
                    pe_done(t)
                else:
                    s_pe.inc(t.nop())
            T(mmo)
        if d_g is not None:
            def dbg_g(sync, b=b, vv=vv):
                sync.wait_ge(s_v.h, vv)
                s_st.inc(sync.dma_start(out=d_g[b], in_=g_blk[:]), 16)
            SY(dbg_g)
        pn = s_pe.n
        for tt in range(NTT):
            s0 = 6 * tt
            par = tt % 2
            stn = s_st.n
            def he(v, tt=tt, par=par, pn=pn, s0=s0, stn=stn):
                v.wait_ge(s_pe.h, pn)
                if stn:
                    v.wait_ge(s_st.h, stn)
                PZ = PS[tt]
                hbuf = h_rot[:, par * D:(par + 1) * D]
                v.tensor_scalar_mul(sc128[:, 1024:1024 + D], u_tm[:, tt * D:(tt + 1) * D], stats[:, s0 + 4:s0 + 5])
                s_v.inc(v.tensor_tensor(hbuf, sc128[:, 1024:1024 + D], PZ[:], ALU.add))
            V(he)
            vn = s_v.n
            def hs(sync, b=b, tt=tt, par=par, vn=vn):
                sync.wait_ge(s_v.h, vn)
                s_st.inc(sync.dma_start(out=dr_h[b * BLK + tt * 128: b * BLK + (tt + 1) * 128, :],
                                        in_=h_rot[:, par * D:(par + 1) * D]), 16)
                if d_h is not None:
                    s_st.inc(sync.dma_start(out=d_h[b * BLK + tt * 128: b * BLK + (tt + 1) * 128, :],
                                            in_=h_rot[:, par * D:(par + 1) * D]), 16)
            SY(hs)
            def hns(s, tt=tt, par=par, s0=s0, vn=vn):
                s.wait_ge(s_v.h, vn)
                s.activation(sc128[:, 0:D], h_rot[:, par * D:(par + 1) * D], AF.Square,
                             accum_out=stats[:, s0 + 5:s0 + 6])
                s_s.inc(s.activation(stats[:, s0 + 5:s0 + 6], stats[:, s0 + 5:s0 + 6], AF.Sqrt,
                                     bias=epsc[:, 1:2], scale=1.0 / D))
            SC(hns)
            sn = s_s.n
            def hnv(v, tt=tt, par=par, s0=s0, sn=sn):
                v.wait_ge(s_s.h, sn)
                v.reciprocal(stats[:, s0 + 5:s0 + 6], stats[:, s0 + 5:s0 + 6])
                s_v.inc(v.tensor_scalar_mul(hn_rot[:, par * D:(par + 1) * D],
                                            h_rot[:, par * D:(par + 1) * D], stats[:, s0 + 5:s0 + 6]))
            V(hnv)
            vn2 = s_v.n
            def trh(t, tt=tt, par=par, vn2=vn2):
                t.wait_ge(s_v.h, vn2)
                PZb = PS[tt][:].bitcast(BF16)
                for k in range(NKD):
                    t.matmul(PZb[:, k * 128:(k + 1) * 128],
                             hn_rot[:, par * D + k * 128: par * D + (k + 1) * 128],
                             ident_s[:], is_transpose=True, start=True, stop=True)
                pe_done(t)
            T(trh)
            pn2 = s_pe.n
            def evh(v, tt=tt, pn2=pn2):
                v.wait_ge(s_pe.h, pn2)
                PZb = PS[tt][:].bitcast(BF16)
                par2 = tt % 2
                s_v.inc(v.tensor_copy(hnf_rot[:, par2 * D:(par2 + 1) * D], PZb[:, 0:D]))
            V(evh)
            vn3 = s_v.n
            def hnst(sync, b=b, tt=tt, vn3=vn3):
                sync.wait_ge(s_v.h, vn3)
                par2 = tt % 2
                for k in range(NKD):
                    s_st.inc(sync.dma_start(
                        out=dr_hn[k * 128:(k + 1) * 128, b * BLK + tt * 128: b * BLK + (tt + 1) * 128],
                        in_=hnf_rot[:, par2 * D + k * 128: par2 * D + (k + 1) * 128]), 16)
                    if d_hn is not None:
                        s_st.inc(sync.dma_start(
                            out=d_hn[k * 128:(k + 1) * 128, b * BLK + tt * 128: b * BLK + (tt + 1) * 128],
                            in_=hnf_rot[:, par2 * D + k * 128: par2 * D + (k + 1) * 128]), 16)
            SY(hnst)

    def emit_m2():
        barrier()
        es1.close()
        es2 = contextlib.ExitStack()
        sb2 = lambda n, sh, dt_: es2.enter_context(nc.sbuf_tensor(n, sh, dt_))
        g1_bf = sb2("g1_bf", [128, NFF * BLK], BF16)
        relu_s = sb2("relu_s", [128, 4 * BLK], BF16)
        hn_s = sb2("hn_s", [128, NKD * BLK], BF16)
        x1_s = sb2("x1_s", [128, 2 * D], F32)
        wproj_rot = sb2("wproj_rot", [128, 11 * D], BF16)
        pm2_s = sb2("pm2_s", [128, 2 * D], F32) if d_pm2 is not None else None

        esw = contextlib.ExitStack()
        wbig = esw.enter_context(nc.sbuf_tensor("wbig", [128, NKD * FF], BF16))

        vnw = s_v.n
        def ldwfc(sync, vnw=vnw):
            sync.wait_ge(s_v.h, vnw)
            sync.wait_ge(s_ws.h, preload.wready["wfc"])
            for k in range(NKD):
                s_ld.inc(sync.dma_start(out=wbig[:, k * FF:(k + 1) * FF],
                                        in_=dr_wfcbf[128 * k:128 * (k + 1), :]), 16)
        SY(ldwfc)
        wfc_ld = s_ld.n


        mlp_state = {"last_v": 0, "last_s": 0, "m1_pe_end": 0, "tail_free_v": 0,
                     "tail_free_s": 0, "pending": [], "ys4_free": {}}

        def mlp_tb(tb, proj_src, is_ffn, rhs_ld, pair_tail, rhs_v=0):
            sq2_at = {}
            rl_at = {}
            work = mlp_state["pending"]
            mlp_state["pending"] = []
            p01v, p01s = mlp_state["last_v"], mlp_state["last_s"]
            for f0 in range(0, NFF, 4):
                pns = {}
                for f in range(f0, f0 + 4):
                    PZ = PS[(f % 4) // 2]
                    off = ((f % 4) % 2) * 512
                    def mmf(t, f=f, PZ=PZ, off=off, rhs_ld=rhs_ld, p01v=p01v, p01s=p01s, rhs_v=rhs_v):
                        if f == 0:
                            t.wait_ge(s_ld.h, wfc_ld)
                            if rhs_ld:
                                t.wait_ge(s_ld.h, rhs_ld)
                            if rhs_v:
                                t.wait_ge(s_v.h, rhs_v)
                        if f < 4:
                            if p01v:
                                t.wait_ge(s_v.h, p01v)
                            if p01s:
                                t.wait_ge(s_s.h, p01s)
                            if EXPT_SERIAL and mlp_state["tail_free_v"]:
                                t.wait_ge(s_v.h, mlp_state["tail_free_v"])
                        else:
                            t.wait_ge(s_v.h, sq2_at[f - 4])
                            t.wait_ge(s_s.h, rl_at[f - 4])
                        for k in range(NKD):
                            t.matmul(PZ[:, off:off + 512],
                                     wbig[:, k * FF + f * 128:k * FF + (f + 1) * 128],
                                     hn_s[:, k * BLK:(k + 1) * BLK],
                                     start=(k == 0), stop=(k == NKD - 1))
                        pe_done(t)
                    T(mmf)
                    pns[f] = s_pe.n
                def rl(sc, f0=f0, pns=dict(pns)):
                    for f in range(f0, f0 + 4):
                        PZ = PS[(f % 4) // 2]
                        off = ((f % 4) % 2) * 512
                        sc.wait_ge(s_pe.h, pns[f])
                        s_s.inc(sc.activation(relu_s[:, (f % 4) * BLK:(f % 4 + 1) * BLK],
                                              PZ[:, off:off + 512], AF.Relu))
                SC(rl)
                for f in range(f0, f0 + 4):
                    rl_at[f] = s_s.n - (f0 + 3 - f)
                for f in range(f0, f0 + 4):
                    PZ = PS[(f % 4) // 2]
                    off = ((f % 4) % 2) * 512
                    sn = rl_at[f]
                    def sq2(v, f=f, PZ=PZ, off=off, sn=sn, dst=mlp_state.get("last_st", 0)):
                        if f == 0 and dst:
                            v.wait_ge(s_st.h, dst)
                        v.wait_ge(s_s.h, sn)
                        s_v.inc(v.tensor_tensor(g1_bf[:, f * BLK:(f + 1) * BLK],
                                                relu_s[:, (f % 4) * BLK:(f % 4 + 1) * BLK], PZ[:, off:off + 512], ALU.mult))
                    V(sq2)
                    sq2_at[f] = s_v.n
                if work and f0 in (12, 28):
                    work.pop(0)()
            mlp_state["last_v"] = s_v.n
            mlp_state["last_s"] = s_s.n
            mlp_state["m1_pe_end"] = s_pe.n
            g1_done = s_v.n
            if d_g1 is not None and not is_ffn:
                def dbg_g1(sync, tb=tb, g1_done=g1_done):
                    sync.wait_ge(s_v.h, g1_done)
                    s_st.inc(sync.dma_start(out=d_g1[tb], in_=g1_bf[:]), 16)
                SY(dbg_g1)
                mlp_state["last_st"] = s_st.n
            tfv, tfs = mlp_state["tail_free_v"], mlp_state["tail_free_s"]
            mmp_at = {}
            lns = {}

            def emit_ldwp(pair, f, pv):
                slot = f % 11
                def ldwp(sync, f=f, slot=slot, pv=pv, is_ffn=is_ffn, proj_src=proj_src):
                    if pv:
                        sync.wait_ge(s_pe.h, pv)
                    sync.wait_ge(s_ws.h, preload.wready["Wproj" if is_ffn else "wproj"])
                    s_wp.inc(sync.dma_start(out=wproj_rot[:, slot * D:(slot + 1) * D],
                                            in_=proj_src[f * 128:(f + 1) * 128, :]), 16)
                SY(ldwp)
                lns[(pair, f)] = s_wp.n

            for pair in range(2):
                for j in range(7):
                    if pair == 1:
                        pv = mmp_at[(0, min(j + 25, NFF - 1))]
                    else:
                        pv = mlp_state.get("prev_tail7", [0] * 7)[j]
                    emit_ldwp(pair, j, pv)
                for f in range(NFF):
                    slot = f % 11
                    ln = lns[(pair, min(f + 6, NFF - 1))]
                    def mmp(t, f=f, pair=pair, slot=slot, ln=ln, g1_done=g1_done, tfv=tfv, tfs=tfs):
                        t.wait_ge(s_wp.h, ln)
                        if f == 0:
                            t.wait_ge(s_v.h, g1_done if pair == 0 else mlp_state["pt_v"])
                            if pair == 0:
                                if tfv:
                                    t.wait_ge(s_v.h, tfv)
                                if tfs:
                                    t.wait_ge(s_s.h, tfs)
                            else:
                                if mlp_state["pt_s"]:
                                    t.wait_ge(s_s.h, mlp_state["pt_s"])
                        for ti in range(2):
                            tt = 2 * pair + ti
                            for hf in range(2):
                                t.matmul(PS[2 + ti][:, hf * 512:(hf + 1) * 512],
                                         g1_bf[:, f * BLK + tt * 128:f * BLK + (tt + 1) * 128],
                                         wproj_rot[:, slot * D + hf * 512:slot * D + (hf + 1) * 512],
                                         start=(f == 0), stop=(f == NFF - 1))
                        if f == NFF - 1:
                            pe_done(t)
                        else:
                            s_pe.inc(t.nop())
                    T(mmp)
                    mmp_at[(pair, f)] = s_pe.n
                    if f + 7 < NFF:
                        emit_ldwp(pair, f + 7, mmp_at[(pair, f)])
                if pair == 1:
                    mlp_state["prev_tail7"] = [mmp_at[(1, min(j + 25, NFF - 1))] for j in range(7)]
                if pair == 0:
                    if work:
                        work.pop(0)()
                else:
                    while work:
                        work.pop(0)()
                tv, ts = pair_tail(tb, pair, s_pe.n)
                if pair == 0:
                    mlp_state["pt_v"], mlp_state["pt_s"] = tv, ts
                else:
                    mlp_state["tail_free_v"], mlp_state["tail_free_s"] = tv, ts

        def m2a_tail(tb, pair, pe_cnt):
            lastv = 0
            dumped = 0
            if d_pm2 is not None:
                stp = s_st.n
                def dmpv(v, pe_cnt=pe_cnt, stp=stp):
                    v.wait_ge(s_pe.h, pe_cnt)
                    if stp:
                        v.wait_ge(s_st.h, stp)
                    v.tensor_copy(pm2_s[:, 0:D], PS[2][:])
                    s_v.inc(v.tensor_copy(pm2_s[:, D:2 * D], PS[3][:]))
                V(dmpv)
                vq = s_v.n
                def dmp(sync, tb=tb, pair=pair, vq=vq):
                    sync.wait_ge(s_v.h, vq)
                    s_st.inc(sync.dma_start(out=d_pm2[tb, pair, 0], in_=pm2_s[:, 0:D]), 16)
                    s_st.inc(sync.dma_start(out=d_pm2[tb, pair, 1], in_=pm2_s[:, D:2 * D]), 16)
                SY(dmp)
                dumped = s_st.n
            for ti in range(2):
                tt = 2 * pair + ti
                par = tt % 2
                vv = s_v.n
                sv = s_s.n
                stn = s_st.n
                def ldxh(sync, tb=tb, tt=tt, par=par, vv=vv, sv=sv):
                    if vv:
                        sync.wait_ge(s_v.h, vv)
                    if sv:
                        sync.wait_ge(s_s.h, sv)
                    s_ld.inc(sync.dma_start(out=x0_rot[:, par * D:(par + 1) * D],
                                            in_=xm[tb * BLK + tt * 128:tb * BLK + (tt + 1) * 128, :]), 16)
                    s_ld.inc(sync.dma_start(out=h_rot[:, par * D:(par + 1) * D],
                                            in_=dr_h[tb * BLK + tt * 128:tb * BLK + (tt + 1) * 128, :]), 16)
                SY(ldxh)
                ln = s_ld.n
                def x1a(v, ti=ti, tt=tt, par=par, ln=ln, pe_cnt=pe_cnt, stn=stn, dumped=dumped):
                    v.wait_ge(s_ld.h, ln)
                    v.wait_ge(s_pe.h, pe_cnt)
                    if dumped:
                        v.wait_ge(s_st.h, dumped)
                    if stn:
                        v.wait_ge(s_st.h, stn)
                    v.tensor_tensor(sc128[:, 0:D], x0_rot[:, par * D:(par + 1) * D],
                                    h_rot[:, par * D:(par + 1) * D], ALU.add)
                    s_v.inc(v.tensor_tensor(x1_s[:, par * D:(par + 1) * D], sc128[:, 0:D],
                                            PS[2 + ti][:], ALU.add))
                V(x1a)
                lastv = s_v.n
                vn = s_v.n
                def stx1(sync, tb=tb, tt=tt, par=par, vn=vn):
                    sync.wait_ge(s_v.h, vn)
                    s_st.inc(sync.dma_start(out=dr_x1[tb * BLK + tt * 128:tb * BLK + (tt + 1) * 128, :],
                                            in_=x1_s[:, par * D:(par + 1) * D]), 16)
                    if d_x1 is not None:
                        s_st.inc(sync.dma_start(out=d_x1[tb * BLK + tt * 128:tb * BLK + (tt + 1) * 128, :],
                                                in_=x1_s[:, par * D:(par + 1) * D]), 16)
                SY(stx1)
            return lastv, 0

        for tb in range(NBLK):
            pv0 = mlp_state["m1_pe_end"] if tb > 0 else 0
            def ldhn(sync, tb=tb, pv0=pv0):
                if pv0:
                    sync.wait_ge(s_pe.h, pv0)
                for k in range(NKD):
                    s_ld.inc(sync.dma_start(out=hn_s[:, k * BLK:(k + 1) * BLK],
                                            in_=dr_hn[k * 128:(k + 1) * 128, tb * BLK:(tb + 1) * BLK]), 16)
            SY(ldhn)
            hnld = s_ld.n
            mlp_tb(tb, dr_wprojbf, False, hnld, m2a_tail)

        barrier()
        esw.close()
        esw2 = contextlib.ExitStack()
        wbig2 = esw2.enter_context(nc.sbuf_tensor("wbig2", [128, NKD * FF], BF16))
        ys4 = esw2.enter_context(nc.sbuf_tensor("ys4", [128, 4 * D], F32))
        mem_g = esw2.enter_context(nc.sbuf_tensor("mem_g", [128, D], F32))
        spk_g = esw2.enter_context(nc.sbuf_tensor("spk_g", [128, D], F32))
        spk2_g = esw2.enter_context(nc.sbuf_tensor("spk2_g", [128, D], F32))
        spks_g = esw2.enter_context(nc.sbuf_tensor("spks_g", [128, D], F32))
        def ldWfcT(sync):
            sync.wait_ge(s_ws.h, preload.wready["Wfc"])
            for k in range(NKD):
                s_ld.inc(sync.dma_start(out=wbig2[:, k * FF:(k + 1) * FF],
                                        in_=dr_Wfcbf[128 * k:128 * (k + 1), :]), 16)
        SY(ldWfcT)
        WfcT_ld = s_ld.n

        def m2b_tail(tb, pair, pe_cnt):
            lasts = 0
            for ti in range(2):
                tt = 2 * pair + ti
                vv0, gv0 = mlp_state["ys4_free"].get(tt, (0, 0))
                def evy(sc, ti=ti, tt=tt, pe_cnt=pe_cnt, gv0=gv0, vv0=vv0):
                    sc.wait_ge(s_pe.h, pe_cnt)
                    if gv0:
                        sc.wait_ge(s_g.h, gv0)
                    if vv0:
                        sc.wait_ge(s_v.h, vv0)
                    s_s.inc(sc.activation(ys4[:, tt * D:(tt + 1) * D], PS[2 + ti][:], AF.Copy))
                SC(evy)
                lasts = s_s.n
                mlp_state["pending"].append(
                    lambda tb=tb, tt=tt, sn=s_s.n: emit_piece(tb, tt, sn))
            return 0, lasts

        def emit_piece(tb, tt, sn):
            par = tt % 2
            vvx = s_v.n
            def ldx1b(g, tb=tb, tt=tt, par=par, vvx=vvx):
                if vvx:
                    g.wait_ge(s_v.h, vvx)
                s_px.inc(g.dma_start(out=h_rot[:, par * D:(par + 1) * D],
                                     in_=dr_x1[tb * BLK + tt * 128:tb * BLK + (tt + 1) * 128, :]), 16)
            G(ldx1b)
            lnb = s_px.n
            ys = ys4[:, tt * D:(tt + 1) * D]
            spks = [spk_g, spk2_g]
            gw0 = s_g.n
            def snn_v1(v, sn=sn, ys=ys, gw0=gw0):
                v.wait_ge(s_s.h, sn)
                if gw0:
                    v.wait_ge(s_g.h, gw0)
                v.tensor_single_scalar(spk_g[:], ys, 1.0, ALU.is_gt)
                s_v.inc(v.scalar_tensor_tensor(mem_g[:], ys, 1.9, spk_g[:], ALU.mult, ALU.subtract))
            V(snn_v1)
            vq = s_v.n
            def snn_g1(g, vq=vq):
                g.wait_ge(s_v.h, vq)
                s_g.inc(g.tensor_copy(spks_g[:], spk_g[:]))
            G(snn_g1)
            for step in (3, 4, 5, 6):
                buf = spks[(step - 3 + 1) % 2]
                gq = s_g.n
                def snn_vs(v, step=step, buf=buf, ys=ys, gq=gq):
                    v.wait_ge(s_g.h, gq)
                    v.tensor_single_scalar(buf[:], mem_g[:], 1.0, ALU.is_gt)
                    if step < 6:
                        v.scalar_tensor_tensor(mem_g[:], mem_g[:], 0.9, buf[:], ALU.mult, ALU.subtract)
                        i = v.tensor_tensor(mem_g[:], mem_g[:], ys, ALU.add)
                    else:
                        i = v.nop()
                    s_v.inc(i)
                V(snn_vs)
                vq2 = s_v.n
                def snn_gs(g, buf=buf, vq2=vq2):
                    g.wait_ge(s_v.h, vq2)
                    s_g.inc(g.tensor_add(spks_g[:], spks_g[:], buf[:]))
                G(snn_gs)
            gn2 = s_g.n
            stn = s_st.n
            def outa(v, tt=tt, par=par, gn2=gn2, lnb=lnb, stn=stn):
                v.wait_ge(s_g.h, gn2)
                v.wait_ge(s_px.h, lnb)
                if stn:
                    v.wait_ge(s_st.h, stn)
                s_v.inc(v.scalar_tensor_tensor(x1_s[:, par * D:(par + 1) * D], spks_g[:], 0.2,
                                               h_rot[:, par * D:(par + 1) * D], ALU.mult, ALU.add))
            V(outa)
            vn = s_v.n
            def sto(g, tb=tb, tt=tt, par=par, vn=vn):
                g.wait_ge(s_v.h, vn)
                s_st.inc(g.dma_start(out=out[tb * BLK + tt * 128:tb * BLK + (tt + 1) * 128, :],
                                     in_=x1_s[:, par * D:(par + 1) * D]), 16)
            G(sto)
            mlp_state["ys4_free"][tt] = (s_v.n, s_g.n)

        for tb in range(NBLK):
            for tt in range(NTT):
                par = tt % 2
                vv = s_v.n
                sv = s_s.n
                def ldx1(sync, tb=tb, tt=tt, par=par, vv=vv, sv=sv):
                    if vv:
                        sync.wait_ge(s_v.h, vv)
                    if sv:
                        sync.wait_ge(s_s.h, sv)
                    s_ld.inc(sync.dma_start(out=x0_rot[:, par * D:(par + 1) * D],
                                            in_=dr_x1[tb * BLK + tt * 128:tb * BLK + (tt + 1) * 128, :]), 16)
                SY(ldx1)
                ln = s_ld.n
                s0 = 6 * tt
                xa = x0_rot[:, par * D:(par + 1) * D]
                def sq(sc, ln=ln, s0=s0, xa=xa):
                    sc.wait_ge(s_ld.h, ln)
                    s_s.inc(sc.activation(sc128[:, 0:D], xa, AF.Square, accum_out=stats[:, s0 + 1:s0 + 2]))
                SC(sq)
                sn = s_s.n
                def v1(v, ln=ln, sn=sn, s0=s0, xa=xa):
                    v.wait_ge(s_ld.h, ln)
                    v.wait_ge(s_s.h, sn)
                    v.reduce_sum(stats[:, s0:s0 + 1], xa, axis=AX)
                    v.tensor_scalar_mul(stats[:, s0:s0 + 1], stats[:, s0:s0 + 1], 1.0 / D)
                    v.tensor_scalar_mul(stats[:, s0 + 1:s0 + 2], stats[:, s0 + 1:s0 + 2], 1.0 / D)
                    v.scalar_tensor_tensor(stats[:, s0 + 1:s0 + 2], stats[:, s0:s0 + 1], stats[:, s0:s0 + 1],
                                           stats[:, s0 + 1:s0 + 2], ALU.mult, ALU.subtract)
                    s_v.inc(v.tensor_scalar_mul(stats[:, s0 + 1:s0 + 2], stats[:, s0 + 1:s0 + 2], -1.0))
                V(v1)
                vn1 = s_v.n
                def s1(sc, vn1=vn1, s0=s0):
                    sc.wait_ge(s_v.h, vn1)
                    s_s.inc(sc.activation(stats[:, s0 + 2:s0 + 3], stats[:, s0 + 1:s0 + 2], AF.Sqrt, bias=epsc[:, 0:1]))
                SC(s1)
                sn1 = s_s.n
                pvh = s_pe.n
                def v2(v, sn1=sn1, s0=s0, xa=xa, par=par, pvh=pvh):
                    v.wait_ge(s_s.h, sn1)
                    if pvh:
                        v.wait_ge(s_pe.h, pvh)
                    v.reciprocal(stats[:, s0 + 2:s0 + 3], stats[:, s0 + 2:s0 + 3])
                    s_v.inc(v.tensor_scalar(hn_rot[:, par * D:(par + 1) * D], xa,
                                            stats[:, s0:s0 + 1], stats[:, s0 + 2:s0 + 3],
                                            ALU.subtract, ALU.mult))
                V(v2)
                vn2 = s_v.n
                def trt(t, tt=tt, par=par, vn2=vn2):
                    t.wait_ge(s_v.h, vn2)
                    P3b = P3[:].bitcast(BF16)
                    for k in range(NKD):
                        t.matmul(P3b[:, k * 128:(k + 1) * 128],
                                 hn_rot[:, par * D + k * 128:par * D + (k + 1) * 128],
                                 ident_s[:], is_transpose=True, start=True, stop=True)
                    pe_done(t)
                T(trt)
                pn = s_pe.n
                def evt(v, tt=tt, pn=pn):
                    v.wait_ge(s_pe.h, pn)
                    P3b = P3[:].bitcast(BF16)
                    dst = hn_s[:].rearrange("p (k b) -> p k b", k=NKD)[:, :, tt * 128:(tt + 1) * 128]
                    s_v.inc(v.tensor_copy(dst, P3b[:, 0:1024].rearrange("p (k l) -> p k l", k=NKD)))
                V(evt)
            if tb == 0:
                def wWT(t):
                    t.wait_ge(s_ld.h, WfcT_ld)
                T(wWT)
            mlp_tb(tb, dr_Wprojbf, True, 0, m2b_tail, rhs_v=s_v.n)
        for th in mlp_state["pending"]:
            th()
        mlp_state["pending"] = []
        esw2.close()
        es2.close()

    preload()
    SA = stop_after or "all"
    def go(tag):
        order = ["preload", "prefront0", "pre0a", "pre0b", "pre0c", "pre0d", "pre0", "pre", "m1front0", "m1acum", "m1xh", "m1scores", "m1uh", "m1ea", "m1quad", "m1yoff", "m1c0", "m1b0", "m1", "all"]
        return order.index(SA) >= order.index(tag)
    if go("prefront0"):
        for b in range(NBLK):
            emit_block_front(xp, b, is_pre=True)
            if SA == "prefront0":
                break
            if go("pre0a"):
                for c in range(NCB):
                    emit_acum(b, c, is_pre=True)
                    if go("pre0b"):
                        emit_xh_tm(b, c)
                    if go("pre0c"):
                        emit_uh(b, c, is_pre=True)
                    if go("pre0d"):
                        emit_scan2(b, c, is_pre=True)
                    if not go("pre0"):
                        break
            if not go("pre"):
                break
        if go("pre"):
            emit_flag()
    if go("m1front0"):
        for b in range(NBLK):
            emit_block_front(xm, b, is_pre=False)
            if SA == "m1front0":
                break
            for c in range(NCB):
                emit_acum(b, c, is_pre=False)
                if go("m1xh"):
                    emit_xh_tm(b, c)
                if go("m1scores"):
                    emit_scores(b, c)
                if go("m1uh"):
                    emit_uh(b, c, is_pre=False)
                if go("m1ea"):
                    emit_ea(b, c)
                if go("m1quad"):
                    emit_ssd_quadratic(b, c)
                if go("m1yoff"):
                    emit_yoff_and_y(b, c)
                if go("m1c0"):
                    emit_scan2(b, c, is_pre=False)
                if not go("m1b0"):
                    break
            if not go("m1b0"):
                break
            emit_outproj_h(b)
            if SA == "m1b0":
                break
        if go("all"):
            emit_m2()

    stn = s_st.n
    axn = s_ax.n
    def fin(sync, stn=stn, axn=axn):
        if stn:
            sync.wait_ge(s_st.h, stn)
        if axn:
            sync.wait_ge(s_ax.h, axn)
    SY(fin)
    barrier()

    es.close()
    return nc, dbg_outs


def _consts():
    sl = np.arange(128)
    triu = (sl[:, None] <= sl[None, :])
    c = {}
    c["triu_f"] = triu.astype(np.float32)
    c["triu_bf"] = triu.astype(ml_dtypes.bfloat16)
    c["maskneg"] = np.where(~triu, np.float32(MASKVAL), np.float32(0)).astype(ml_dtypes.bfloat16)
    c["ident_in"] = np.eye(128, dtype=ml_dtypes.bfloat16)
    c["onesrow_in"] = np.ones((1, 128), dtype=ml_dtypes.bfloat16)
    e = np.zeros((32, 16, 128), dtype=ml_dtypes.bfloat16)
    for hp in range(16):
        e[hp, hp, :] = 1
    c["eind_in"] = e.reshape(32, 2048)
    e2 = np.zeros((NH, NH, PH), dtype=ml_dtypes.bfloat16)
    for h in range(NH):
        e2[h, h, :] = 1
    c["eind2_in"] = e2.reshape(NH, DIN)
    return c


def kernel(L0_attn_x, L0_attn_w_in, L0_attn_conv_w, L0_attn_conv_b, L0_attn_dt_bias,
           L0_attn_a_log, L0_attn_d_skip, L0_attn_norm_w, L0_attn_w_out, L0_attn_w_fc,
           L0_attn_w_proj, L0_ffn_W_fc, L0_ffn_W_proj, _dbg=(), _stop=None, _nc_cache={}):
    x = np.ascontiguousarray(L0_attn_x)
    B, L, _ = x.shape
    key = (tuple(sorted(_dbg)), _stop)
    if key not in _nc_cache:
        _nc_cache[key] = build_program(dbg=_dbg, stop_after=_stop)
    nc, dbg_outs = _nc_cache[key]

    consts = _consts()
    dsk = np.asarray(L0_attn_d_skip, np.float32)
    dskd = np.zeros((128, NFIN * 128), np.float32)
    for t in range(NFIN):
        vals = np.concatenate([np.repeat(dsk[2 * t], 64), np.repeat(dsk[2 * t + 1], 64)])
        blockd = np.zeros((128, 128), np.float32)
        np.fill_diagonal(blockd, vals)
        dskd[:, t * 128:(t + 1) * 128] = blockd

    base = dict(
        w_in=np.ascontiguousarray(L0_attn_w_in, np.float32),
        conv_w=np.ascontiguousarray(L0_attn_conv_w, np.float32),
        conv_b=np.ascontiguousarray(L0_attn_conv_b, np.float32).reshape(CONVD, 1),
        dt_bias=np.ascontiguousarray(L0_attn_dt_bias, np.float32).reshape(NH, 1),
        a_log=np.ascontiguousarray(L0_attn_a_log, np.float32).reshape(NH, 1),
        d_skip_diag=dskd,
        norm_w=np.ascontiguousarray(L0_attn_norm_w, np.float32).reshape(DIN, 1),
        w_out=np.ascontiguousarray(L0_attn_w_out, np.float32),
        w_fc=np.ascontiguousarray(L0_attn_w_fc, np.float32),
        w_proj=np.ascontiguousarray(L0_attn_w_proj, np.float32),
        Wfc=np.ascontiguousarray(np.asarray(L0_ffn_W_fc, np.float32).T),
        Wproj=np.ascontiguousarray(np.asarray(L0_ffn_W_proj, np.float32).T),
        **consts,
    )
    in_maps = []
    for core in range(8):
        b, half = core // 2, core % 2
        m = dict(base)
        m["xm"] = np.ascontiguousarray(x[b, half * TOK:(half + 1) * TOK, :], np.float32)
        m["xp"] = np.ascontiguousarray(x[b, 0:TOK, :], np.float32)
        m["flag_in"] = np.full((128, 1), float(half), np.float32)
        in_maps.append(m)

    res = run_bass_kernel_spmd(nc, in_maps, core_ids=list(range(8)))
    outs = np.zeros((B, L, D), np.float32)
    for core in range(8):
        b, half = core // 2, core % 2
        outs[b, half * TOK:(half + 1) * TOK, :] = res.results[core]["out"]
    if _dbg:
        return outs, [res.results[c] for c in range(8)]
    return outs


if __name__ == "__main__":
    nc, _ = build_program()
    print("built")


# version 16
# speedup vs baseline: 1.0362x; 1.0183x over previous
import contextlib
import numpy as np
import ml_dtypes
import concourse.bass as bass
import concourse.mybir as mybir
from concourse.bass_utils import run_bass_kernel_spmd

F32 = mybir.dt.float32
BF16 = mybir.dt.bfloat16
AF = mybir.ActivationFunctionType
ALU = mybir.AluOpType
AX = mybir.AxisListType.X

D = 1024
DIN = 2048
DST = 128
NH = 32
PH = 64
CONVD = DIN + 2 * DST
DINPROJ = 2 * DIN + 2 * DST + NH
EXPT_SERIAL = False
FF = 4096
NKD = 8
NFIN = 16
NFCV = 18
NFF = 32

TOK = 2048
BLK = 512
NBLK = 4
C = 128
NCB = 4
NTT = 4

EPS_LN = 1e-5
EPS_OUT = 1.1920929e-07
EPS_GATE = 1e-05
BETA = 0.9
MASKVAL = -1e30


def build_program(dbg=(), stop_after=None):
    dbg = set(dbg)
    nc = bass.Bass("TRN2", target_bir_lowering=False, debug=False)

    xm = nc.dram_tensor("xm", [TOK, D], F32, kind="ExternalInput").ap()
    xp = nc.dram_tensor("xp", [TOK, D], F32, kind="ExternalInput").ap()
    w_in = nc.dram_tensor("w_in", [D, DINPROJ], F32, kind="ExternalInput").ap()
    conv_w = nc.dram_tensor("conv_w", [CONVD, 4], F32, kind="ExternalInput").ap()
    conv_b = nc.dram_tensor("conv_b", [CONVD, 1], F32, kind="ExternalInput").ap()
    dt_bias = nc.dram_tensor("dt_bias", [NH, 1], F32, kind="ExternalInput").ap()
    a_log = nc.dram_tensor("a_log", [NH, 1], F32, kind="ExternalInput").ap()
    d_skip_diag = nc.dram_tensor("d_skip_diag", [128, NFIN * 128], F32, kind="ExternalInput").ap()
    norm_w = nc.dram_tensor("norm_w", [DIN, 1], F32, kind="ExternalInput").ap()
    w_out = nc.dram_tensor("w_out", [DIN, D], F32, kind="ExternalInput").ap()
    w_fc = nc.dram_tensor("w_fc", [D, FF], F32, kind="ExternalInput").ap()
    w_proj = nc.dram_tensor("w_proj", [FF, D], F32, kind="ExternalInput").ap()
    Wfc = nc.dram_tensor("Wfc", [D, FF], F32, kind="ExternalInput").ap()
    Wproj = nc.dram_tensor("Wproj", [FF, D], F32, kind="ExternalInput").ap()
    triu_f = nc.dram_tensor("triu_f", [128, 128], F32, kind="ExternalInput").ap()
    triu_bf = nc.dram_tensor("triu_bf", [128, 128], BF16, kind="ExternalInput").ap()
    maskneg = nc.dram_tensor("maskneg", [128, 128], BF16, kind="ExternalInput").ap()
    ident_in = nc.dram_tensor("ident_in", [128, 128], BF16, kind="ExternalInput").ap()
    onesrow_in = nc.dram_tensor("onesrow_in", [1, 128], BF16, kind="ExternalInput").ap()
    eind_in = nc.dram_tensor("eind_in", [32, 2048], BF16, kind="ExternalInput").ap()
    eind2_in = nc.dram_tensor("eind2_in", [NH, DIN], BF16, kind="ExternalInput").ap()
    flag_in = nc.dram_tensor("flag_in", [128, 1], F32, kind="ExternalInput").ap()

    out = nc.dram_tensor("out", [TOK, D], F32, kind="ExternalOutput").ap()
    dr_h = nc.dram_tensor("dr_h", [TOK, D], F32, kind="Internal").ap()
    dr_x1 = nc.dram_tensor("dr_x1", [TOK, D], F32, kind="Internal").ap()
    dr_hn = nc.dram_tensor("dr_hn", [D, TOK], BF16, kind="Internal").ap()
    dr_woutbf = nc.dram_tensor("dr_woutbf", [DIN, D], BF16, kind="Internal").ap()
    dr_winzbf = nc.dram_tensor("dr_winzbf", [D, 2048], BF16, kind="Internal").ap()
    dr_wfcbf = nc.dram_tensor("dr_wfcbf", [D, FF], BF16, kind="Internal").ap()
    dr_wprojbf = nc.dram_tensor("dr_wprojbf", [FF, D], BF16, kind="Internal").ap()
    dr_Wfcbf = nc.dram_tensor("dr_Wfcbf", [D, FF], BF16, kind="Internal").ap()
    dr_Wprojbf = nc.dram_tensor("dr_Wprojbf", [FF, D], BF16, kind="Internal").ap()
    dr_flat = nc.dram_tensor("dr_flat", [2, NH * C], BF16, kind="Internal").ap()

    dbg_outs = {}
    def dbgt(name, shape, dtype=F32):
        if name in dbg:
            dbg_outs[name] = nc.dram_tensor("dbg_" + name, shape, dtype, kind="ExternalOutput").ap()
        return dbg_outs.get(name)

    d_u = dbgt("u", [NBLK, 128, NKD * BLK], BF16)
    d_xcv = dbgt("xcv", [NBLK, 128, NFCV * BLK], BF16)
    d_z = dbgt("z", [NBLK, 128, NTT * DIN], BF16)
    d_dt = dbgt("dt", [NBLK, NH, BLK])
    d_acum = dbgt("acum", [NBLK * NCB, NH, C])
    d_y = dbgt("y", [NBLK * NCB, 128, DIN])
    d_g = dbgt("g", [NBLK, 128, NFIN * BLK], BF16)
    d_h = dbgt("h", [TOK, D])
    d_st0 = dbgt("st0", [128, 6 * NTT])
    d_x00 = dbgt("x00", [128, 2 * D])
    d_eind = dbgt("eind", [33, 2048], BF16)
    d_state = dbgt("state", [NBLK * NCB, 128, DIN])
    d_pstate = dbgt("pstate", [128, DIN])
    d_x1 = dbgt("x1", [TOK, D])
    d_hn = dbgt("hn", [NKD * 128, TOK], BF16)
    d_g1 = dbgt("g1", [NBLK, 128, NFF * BLK], BF16)
    d_pm2 = dbgt("pm2", [NBLK, 2, 2, 128, D])

    es = contextlib.ExitStack()
    es1 = contextlib.ExitStack()
    sbt = lambda name, shape, dtype: es.enter_context(nc.sbuf_tensor(name, shape, dtype))
    sb1 = lambda name, shape, dtype: es1.enter_context(nc.sbuf_tensor(name, shape, dtype))
    pst = lambda name, shape, dtype: es.enter_context(nc.psum_tensor(name, shape, dtype))

    x0_rot = sbt("x0_rot", [128, 2 * D], F32)
    h_rot = sbt("h_rot", [128, 2 * D], F32)
    hn_rot = sbt("hn_rot", [128, 2 * D], BF16)
    hnf_rot = sbt("hnf_rot", [128, 2 * D], BF16)
    wo_rot = sbt("wo_rot", [128, 2 * D], BF16)
    stats = sbt("stats", [128, 6 * NTT], F32)
    smalls = sbt("smalls", [128, 28], F32)
    consts = sbt("consts", [128, 4], BF16)
    onesrow = sbt("onesrow", [1, 128], BF16)
    ident_s = sbt("ident_s", [128, 128], BF16)
    sc128 = sbt("sc128", [128, DIN], F32)
    epsc = sbt("epsc", [128, 4], F32)

    WXB = DINPROJ - 2048
    w_in_bf = sb1("w_in_bf", [128, NKD * WXB], BF16)
    xcv_bf = sb1("xcv_bf", [128, NFCV * BLK], BF16)
    z_bf = sb1("z_bf", [128, NTT * DIN], BF16)
    state = sb1("state", [128, DIN], F32)
    state_bf = sb1("state_bf", [128, DIN], BF16)
    egt = sb1("egt", [128, NH * C // 2], BF16)
    ea_bf = sb1("ea_bf", [128, DIN], BF16)
    uh_bf = sb1("uh_bf", [128, DIN], BF16)
    uh2_bf = sb1("uh2_bf", [128, DIN], BF16)
    xh_tm = sb1("xh_tm", [128, DIN], BF16)
    b_tm = sb1("b_tm", [128, DST], BF16)
    mskst = sb1("mskst", [128, C], BF16)
    y_f32 = sb1("y_f32", [128, DIN], F32)
    g_blk = sb1("g_blk", [128, NFIN * BLK], BF16)
    u_tm = sb1("u_tm", [128, NTT * D], BF16)
    u_fm = sb1("u_fm", [128, NKD * BLK], BF16)
    tails = sb1("tails", [128, NFCV * 3], F32)
    dt_blk = sb1("dt_blk", [NH, BLK], F32)
    dtb_bf = sb1("dtb_bf", [NH, BLK], BF16)
    a_bf = sb1("a_bf", [NH, BLK], BF16)
    a_tmb = sb1("a_tmb", [128, NH], BF16)
    acum_f = sb1("acum_f", [NH, C], F32)
    acf_h = sb1("acf_h", [16, C], F32)
    ab_hi = sb1("ab_hi", [64, C], BF16)
    ab_lo = sb1("ab_lo", [64, C], BF16)
    hi_t = sb1("hi_t", [16, 2 * C], BF16)
    lo_t = sb1("lo_t", [16, 2 * C], BF16)
    eind_hi = sb1("eind_hi", [33, NH * C // 2], BF16)
    eind_lo = sb1("eind_lo", [33, NH * C // 2], BF16)
    eind2_s = sb1("eind2_s", [NH, DIN], BF16)
    dtdec_bf = sb1("dtdec_bf", [NH, C], BF16)
    expac_f = sb1("expac_f", [NH, C], F32)
    expac_bf = sb1("expac_bf", [NH, C], BF16)
    dec_bf = sb1("dec_bf", [NH, C], BF16)
    triu_fs = sb1("triu_fs", [128, 128], F32)
    triu_bfs = sb1("triu_bfs", [128, 128], BF16)
    maskneg_s = sb1("maskneg_s", [128, 128], BF16)
    dskd_s = sb1("dskd_s", [128, NFIN * 128], BF16)
    convw_s = sb1("convw_s", [128, NFCV * 4], F32)
    convb_s = sb1("convb_s", [128, NFCV], F32)
    cdrep = sb1("cdrep", [NH, 128], BF16)

    SM_DTB, SM_NA, SM_TMP, SM_FLAG, SM_TMP2 = 0, 1, 2, 3, 4

    P0 = pst("P0", [128, 1024], F32)
    P1 = pst("P1", [128, 1024], F32)
    P2 = pst("P2", [128, 1024], F32)
    P3 = pst("P3", [128, 1024], F32)
    PS = [P0, P1, P2, P3]

    class Sem:
        def __init__(self, name):
            self.h = es.enter_context(nc.semaphore(name))
            self.n = 0
        def inc(self, instr, k=1):
            instr.then_inc(self.h, k)
            self.n += k
    s_ld = Sem("s_ld")
    s_ax = Sem("s_ax")
    s_st = Sem("s_st")
    s_pe = Sem("s_pe")
    s_v = Sem("s_v")
    s_s = Sem("s_s")
    s_g = Sem("s_g")
    s_wl = Sem("s_wl")
    s_ws = Sem("s_ws")
    s_wp = Sem("s_wp")
    s_px = Sem("s_px")
    ALLS = None

    block = es.enter_context(nc.Block())

    _OPNAMES = {"tensor_copy", "tensor_tensor", "tensor_add", "tensor_sub", "tensor_mul",
                "tensor_max", "tensor_scalar", "tensor_single_scalar", "tensor_scalar_add",
                "tensor_scalar_sub", "tensor_scalar_mul", "tensor_scalar_max", "tensor_scalar_min",
                "tensor_relu", "tensor_reduce", "reduce_max", "reduce_sum", "scalar_tensor_tensor",
                "tensor_tensor_scan", "memset", "memzero", "reciprocal", "select", "activation",
                "copy", "add", "mul", "square", "sqrt", "sign", "transpose"}

    class _DW:
        def __init__(self, eng):
            self._e = eng
        def __getattr__(self, name):
            attr = getattr(self._e, name)
            if name not in _OPNAMES:
                return attr
            def wrapped(*args, **kw):
                r = attr(*args, **kw)
                need = kw.get("accum_out") is not None
                if not need and args:
                    try:
                        need = args[0].free_size() < 512
                    except Exception:
                        need = True
                if need:
                    self._e.drain()
                    return self._e.nop()
                return r
            return wrapped

    def T(f): block.tensor(f)
    def V(f): block.vector(lambda e: f(_DW(e)))
    def SC(f): block.scalar(lambda e: f(_DW(e)))
    def G(f): block.gpsimd(lambda e: f(_DW(e)))
    def SY(f): block.sync(f)

    def wait_upto(eng, *sems):
        for s, v in [(s, s.n) for s in sems]:
            if v:
                eng.wait_ge(s.h, v)

    def barrier():
        sems = [s_pe, s_v, s_s, s_g, s_ld, s_ax, s_st]
        vals = [(s.h, s.n) for s in sems if s.n]
        def mk(eng, vals=vals):
            for h, v in vals:
                eng.wait_ge(h, v)
        T(mk); V(mk); SC(mk); G(mk); SY(mk)

    from concourse.hw_specs import get_activation_tables
    _TABSETS = list(get_activation_tables(nc.m.arch))
    _NLE = _TABSETS.index("natural_log_exp_and_others")
    _SILU = _TABSETS.index("silu_and_others")

    def load_tab(sc, idx):
        sc.add_instruction(mybir.InstLoadActFuncSet(
            name=nc.get_next_instruction_name(), act_func_set_id=idx, ins=[], outs=[]))

    def pe_done(t):
        t.drain()
        s_pe.inc(t.nop())

    def preload():
        def ld(sync):
            s_ld.inc(sync.dma_start(out=triu_fs[:], in_=triu_f), 16)
            s_ld.inc(sync.dma_start(out=triu_bfs[:], in_=triu_bf), 16)
            s_ld.inc(sync.dma_start(out=maskneg_s[:], in_=maskneg), 16)
            s_ld.inc(sync.dma_start(out=ident_s[:], in_=ident_in), 16)
            s_ld.inc(sync.dma_start(out=onesrow[:], in_=onesrow_in), 16)
            s_ld.inc(sync.dma_start(out=eind_hi[0:32, :], in_=eind_in), 16)
            s_ld.inc(sync.dma_start(out=eind_lo[0:32, :], in_=eind_in), 16)
            s_ld.inc(sync.dma_start(out=eind2_s[:], in_=eind2_in), 16)
            s_ld.inc(sync.dma_start(out=sc128[:, 0:2048], in_=d_skip_diag), 16)
            s_ld.inc(sync.dma_start(out=smalls[0:NH, SM_DTB:SM_DTB + 1], in_=dt_bias), 16)
            s_ld.inc(sync.dma_start(out=smalls[0:NH, SM_NA:SM_NA + 1], in_=a_log), 16)
            s_ld.inc(sync.dma_start(out=smalls[:, SM_FLAG:SM_FLAG + 1], in_=flag_in), 16)
            for t in range(NFCV):
                s_ld.inc(sync.dma_start(out=convw_s[:, 4 * t:4 * t + 4], in_=conv_w[128 * t:128 * (t + 1), :]), 16)
                s_ld.inc(sync.dma_start(out=convb_s[:, t:t + 1], in_=conv_b[128 * t:128 * (t + 1), :]), 16)
            for t in range(NFIN):
                s_ld.inc(sync.dma_start(out=smalls[:, 8 + t:9 + t], in_=norm_w[128 * t:128 * (t + 1), :]), 16)
        SY(ld)
        ldn = s_ld.n
        def pre_s(s, ldn=ldn):
            s.wait_ge(s_ld.h, ldn)
            s_s.inc(s.activation(smalls[0:NH, SM_NA:SM_NA + 1], smalls[0:NH, SM_NA:SM_NA + 1], AF.Exp))
        SC(pre_s)
        sn = s_s.n
        def pre_v(v, sn=sn, ldn=ldn):
            v.wait_ge(s_s.h, sn)
            v.tensor_scalar_mul(smalls[0:NH, SM_NA:SM_NA + 1], smalls[0:NH, SM_NA:SM_NA + 1], -1.0)
            v.memset(consts[:, 0:1], 1.0)
            v.memset(epsc[:, 0:1], EPS_LN)
            v.memset(epsc[:, 1:2], EPS_OUT)
            v.memset(epsc[:, 2:3], EPS_GATE)
            v.memset(epsc[:, 3:4], EPS_GATE)
            v.memset(ab_hi[0:32, :], 0.0)
            v.memset(ab_lo[0:32, :], 0.0)
            v.memset(ab_hi[32:64, :], 1.0)
            v.memset(ab_lo[32:64, :], 1.0)
            v.memset(state[:], 0.0)
            v.memset(tails[:], 0.0)
            s_v.inc(v.tensor_copy(dskd_s[:], sc128[:, 0:2048]))
        V(pre_v)
        for k in range(NKD):
            for c0 in range(0, WXB, 2048):
                cw = min(2048, WXB - c0)
                vn = s_v.n
                def ldk(sync, k=k, c0=c0, cw=cw, vn=vn):
                    sync.wait_ge(s_v.h, vn)
                    s_ld.inc(sync.dma_start(out=sc128[:, 0:cw], in_=w_in[128 * k:128 * (k + 1), 2048 + c0:2048 + c0 + cw]), 16)
                ln = s_ld.n + 16
                def cstk(v, k=k, c0=c0, cw=cw, ln=ln):
                    v.wait_ge(s_ld.h, ln)
                    s_v.inc(v.tensor_copy(w_in_bf[:, WXB * k + c0:WXB * k + c0 + cw], sc128[:, 0:cw]))
                SY(ldk); V(cstk)
        barrier()
        zf32 = z_bf[:].bitcast(F32)
        wready = {}
        pieces = []
        def add_stream(name, src, dst, rows, cols):
            for r0 in range(0, rows, 128):
                for c0 in range(0, cols, 2048):
                    pieces.append((name, src, dst, r0, c0, min(2048, cols - c0)))
        add_stream("winz", w_in[:, 0:2048], dr_winzbf, D, 2048)
        add_stream("wout", w_out, dr_woutbf, DIN, D)
        add_stream("wfc", w_fc, dr_wfcbf, D, FF)
        add_stream("wproj", w_proj, dr_wprojbf, FF, D)
        add_stream("Wfc", Wfc, dr_Wfcbf, D, FF)
        add_stream("Wproj", Wproj, dr_Wprojbf, FF, D)
        NPC = len(pieces)
        ld_at = [0] * NPC
        st_at = [0] * NPC
        def emit_gstream(g):
            for i in range(NPC + 1):
                if i < NPC:
                    name, src, dst, r0, c0, cw = pieces[i]
                    par = i % 2
                    g.dma_start(out=zf32[:, par * 2048:par * 2048 + cw],
                                in_=src[r0:r0 + 128, c0:c0 + cw]).then_inc(s_wl.h, 16)
                    s_wl.n += 16
                    ld_at[i] = s_wl.n
                if i >= 1:
                    j = i - 1
                    name, src, dst, r0, c0, cw = pieces[j]
                    par = j % 2
                    g.wait_ge(s_wl.h, ld_at[j])
                    if j >= 2:
                        g.wait_ge(s_ws.h, 16 * (j - 1))
                    g.tensor_copy(g_blk[:, par * 2048:par * 2048 + cw],
                                  zf32[:, par * 2048:par * 2048 + cw])
                    g.dma_start(out=dst[r0:r0 + 128, c0:c0 + cw],
                                in_=g_blk[:, par * 2048:par * 2048 + cw]).then_inc(s_ws.h, 16)
                    s_ws.n += 16
        G(emit_gstream)
        for i, p in enumerate(pieces):
            wready[p[0]] = 16 * (i + 1)
        preload.wready = wready
        preload.stream_done = 16 * NPC

    def emit_stats_u(tt, par, rms=True, ldwait=0):
        s0 = 6 * tt
        xa = x0_rot[:, par * D:(par + 1) * D]
        c_mean = stats[:, s0:s0 + 1]
        c_var = stats[:, s0 + 1:s0 + 2]
        c_rln = stats[:, s0 + 2:s0 + 3]
        c_R = stats[:, s0 + 3:s0 + 4]
        c_ir2 = stats[:, s0 + 4:s0 + 5]
        def sq(s, ldwait=ldwait):
            if ldwait:
                s.wait_ge(s_ld.h, ldwait)
            s_s.inc(s.activation(sc128[:, 0:D], xa, AF.Square, accum_out=c_var))
        SC(sq)
        sn = s_s.n
        def v1(v, sn=sn):
            v.wait_ge(s_s.h, sn)
            v.reduce_sum(c_mean, xa, axis=AX)
            v.tensor_scalar_mul(c_mean, c_mean, 1.0 / D)
            v.tensor_scalar_mul(c_var, c_var, 1.0 / D)
            v.scalar_tensor_tensor(c_var, c_mean, c_mean, c_var, ALU.mult, ALU.subtract)
            s_v.inc(v.tensor_scalar_mul(c_var, c_var, -1.0))
        V(v1)
        vn = s_v.n
        def s1(s, vn=vn):
            s.wait_ge(s_v.h, vn)
            s_s.inc(s.activation(c_rln, c_var, AF.Sqrt, bias=epsc[:, 0:1]))
        SC(s1)
        if not rms:
            sn2 = s_s.n
            def v3(v, sn2=sn2, tt=tt):
                v.wait_ge(s_s.h, sn2)
                v.reciprocal(c_R, c_rln)
                v.tensor_copy(c_rln, c_R)
                s_v.inc(v.tensor_scalar(u_tm[:, tt * D:(tt + 1) * D], xa, c_mean, c_rln,
                                        ALU.subtract, ALU.mult))
            V(v3)
            return
        sn2 = s_s.n
        def v2(v, sn2=sn2):
            v.wait_ge(s_s.h, sn2)
            v.reciprocal(c_R, c_rln)
            v.tensor_copy(c_rln, c_R)
            v.tensor_mul(c_R, c_var, c_rln)
            s_v.inc(v.tensor_mul(c_R, c_R, c_rln))
        V(v2)
        vn2 = s_v.n
        def s2(s, vn2=vn2):
            s.wait_ge(s_v.h, vn2)
            s_s.inc(s.activation(c_ir2, c_R, AF.Sqrt, bias=epsc[:, 1:2]))
        SC(s2)
        sn3 = s_s.n
        def v3(v, sn3=sn3, tt=tt):
            v.wait_ge(s_s.h, sn3)
            v.reciprocal(c_R, c_ir2)
            v.tensor_mul(c_R, c_R, c_rln)
            s_v.inc(v.tensor_scalar(u_tm[:, tt * D:(tt + 1) * D], xa, c_mean, c_R,
                                    ALU.subtract, ALU.mult))
        V(v3)

    def emit_u_transpose(tt):
        vn = s_v.n
        def tr(t, tt=tt, vn=vn):
            t.wait_ge(s_v.h, vn)
            P3b = P3[:].bitcast(BF16)
            for k in range(NKD):
                t.matmul(P3b[:, k * 128:(k + 1) * 128],
                         u_tm[:, tt * D + k * 128: tt * D + (k + 1) * 128],
                         ident_s[:], is_transpose=True, start=True, stop=True)
            pe_done(t)
        T(tr)
        pn = s_pe.n
        def ev(v, tt=tt, pn=pn):
            v.wait_ge(s_pe.h, pn)
            P3b = P3[:].bitcast(BF16)
            dst = u_fm[:].rearrange("p (k b) -> p k b", k=NKD)[:, :, tt * 128:(tt + 1) * 128]
            src = P3b[:, 0:1024].rearrange("p (k l) -> p k l", k=NKD)
            s_v.inc(v.tensor_copy(dst, src))
        V(ev)

    def emit_block_front(xsrc, b, is_pre):
        for tt in range(NTT):
            par = tt % 2
            vv = s_v.n
            sv = s_s.n
            def ldx(sync, tt=tt, par=par, vv=vv, sv=sv, xsrc=xsrc, b=b):
                if vv:
                    sync.wait_ge(s_v.h, vv)
                if sv:
                    sync.wait_ge(s_s.h, sv)
                s_ld.inc(sync.dma_start(out=x0_rot[:, par * D:(par + 1) * D],
                                        in_=xsrc[b * BLK + tt * 128: b * BLK + (tt + 1) * 128, :]), 16)
            ln = s_ld.n + 16
            SY(ldx)
            def wx(v, ln=ln):
                v.wait_ge(s_ld.h, ln)
            V(wx)
            emit_stats_u(tt, par, rms=True, ldwait=ln)
            emit_u_transpose(tt)
        if d_u is not None and not is_pre:
            vn = s_v.n
            def dbg_st(sync, b=b, vn=vn):
                sync.wait_ge(s_v.h, vn)
                s_st.inc(sync.dma_start(out=d_u[b], in_=u_fm[:]), 16)
            SY(dbg_st)
        if d_st0 is not None and not is_pre and b == 0:
            vn = s_v.n
            def dbg_s0(sync, vn=vn):
                sync.wait_ge(s_v.h, vn)
                s_st.inc(sync.dma_start(out=d_st0, in_=stats[:]), 16)
                if d_x00 is not None:
                    s_st.inc(sync.dma_start(out=d_x00, in_=x0_rot[:]), 16)
            SY(dbg_s0)

        uvn = s_v.n
        if not is_pre:
            for zs in range(4):
                half = zs % 2
                sv = s_s.n
                for k in range(NKD):
                    par = k % 4
                    pvw = s_pe.n
                    def ldwz(sync, zs=zs, k=k, par=par, pvw=pvw):
                        if pvw:
                            sync.wait_ge(s_pe.h, pvw)
                        sync.wait_ge(s_ws.h, preload.wready["winz"])
                        s_ld.inc(sync.dma_start(out=wo_rot[:, par * 512:(par + 1) * 512],
                                                in_=dr_winzbf[128 * k:128 * (k + 1), zs * 512:(zs + 1) * 512]), 16)
                    SY(ldwz)
                    ln = s_ld.n
                    def mmz(t, zs=zs, k=k, par=par, ln=ln, sv=sv, uvn=uvn, half=half):
                        t.wait_ge(s_ld.h, ln)
                        if k == 0:
                            t.wait_ge(s_v.h, uvn)
                            if sv:
                                t.wait_ge(s_s.h, sv)
                        for tt in range(NTT):
                            PZ = PS[tt]
                            t.matmul(PZ[:, half * 512:(half + 1) * 512],
                                     u_fm[:, k * BLK + tt * 128:k * BLK + (tt + 1) * 128],
                                     wo_rot[:, par * 512:(par + 1) * 512],
                                     start=(k == 0), stop=(k == NKD - 1))
                        if k == NKD - 1:
                            pe_done(t)
                        else:
                            s_pe.inc(t.nop())
                    T(mmz)
                pn = s_pe.n
                def evz(s, zs=zs, pn=pn, half=half, b=b):
                    s.wait_ge(s_pe.h, pn)
                    if b == 0:
                        s.wait_ge(s_ws.h, preload.stream_done)
                    for tt in range(NTT):
                        i = s.activation(z_bf[:, tt * DIN + zs * 512: tt * DIN + (zs + 1) * 512],
                                         PS[tt][:, half * 512:(half + 1) * 512], AF.Silu)
                    s_s.inc(i)
                SC(evz)
            if d_z is not None:
                sn = s_s.n
                def dbg_z(sync, b=b, sn=sn):
                    sync.wait_ge(s_s.h, sn)
                    s_st.inc(sync.dma_start(out=d_z[b], in_=z_bf[:]), 16)
                SY(dbg_z)

        mstart = 16
        for m in range(mstart, 35):
            ft = m - 16
            pslot = 2 + (ft % 2)
            PZ = PS[pslot]
            rows = 128 if m < 34 else 32
            vv = s_v.n
            sv = s_s.n
            def mmx(t, m=m, PZ=PZ, vv=vv, sv=sv, uvn=uvn, rows=rows):
                t.wait_ge(s_v.h, vv)
                for k in range(NKD):
                    t.matmul(PZ[0:rows, 0:512],
                             w_in_bf[:, WXB * k + (m - 16) * 128: WXB * k + (m - 16) * 128 + rows],
                             u_fm[:, k * BLK:(k + 1) * BLK],
                             start=(k == 0), stop=(k == NKD - 1))
                pe_done(t)
            T(mmx)
            pn = s_pe.n
            if m < 34:
                def cv(v, ft=ft, PZ=PZ, pn=pn):
                    v.wait_ge(s_pe.h, pn)
                    wc = lambda k: convw_s[:, 4 * ft + k:4 * ft + k + 1]
                    acc = sc128[:, 0:512]
                    v.tensor_scalar(acc, PZ[:, 0:512], wc(3), convb_s[:, ft:ft + 1], ALU.mult, ALU.add)
                    for k in (2, 1, 0):
                        sh = 3 - k
                        v.scalar_tensor_tensor(acc[:, sh:512], PZ[:, 0:512 - sh], wc(k),
                                               acc[:, sh:512], ALU.mult, ALU.add)
                    tl = tails[:, 3 * ft:3 * ft + 3]
                    v.scalar_tensor_tensor(acc[:, 0:3], tl[:, 0:3], wc(0), acc[:, 0:3], ALU.mult, ALU.add)
                    v.scalar_tensor_tensor(acc[:, 0:2], tl[:, 1:3], wc(1), acc[:, 0:2], ALU.mult, ALU.add)
                    v.scalar_tensor_tensor(acc[:, 0:1], tl[:, 2:3], wc(2), acc[:, 0:1], ALU.mult, ALU.add)
                    s_v.inc(v.tensor_copy(tl, PZ[:, 509:512]))
                V(cv)
                vn = s_v.n
                def cvs(s, ft=ft, vn=vn):
                    s.wait_ge(s_v.h, vn)
                    s_s.inc(s.activation(xcv_bf[:, ft * BLK:(ft + 1) * BLK], sc128[:, 0:512], AF.Silu))
                SC(cvs)
            else:
                def dtev(s, PZ=PZ, pn=pn):
                    s.wait_ge(s_pe.h, pn)
                    s.activation(dt_blk[:, :], PZ[0:NH, 0:512], AF.Relu, bias=smalls[0:NH, SM_DTB:SM_DTB + 1])
                    s.activation(sc128[0:NH, 512:1024], PZ[0:NH, 0:512], AF.Abs, bias=smalls[0:NH, SM_DTB:SM_DTB + 1])
                    s_s.inc(s.activation(sc128[0:NH, 512:1024], sc128[0:NH, 512:1024], AF.Exp, scale=-1.0))
                SC(dtev)
                sn = s_s.n
                def dtv(v, sn=sn):
                    v.wait_ge(s_s.h, sn)
                    s_v.inc(v.tensor_scalar_add(sc128[0:NH, 512:1024], sc128[0:NH, 512:1024], 1.0))
                V(dtv)
                vn = s_v.n
                def dtl(s, vn=vn):
                    s.wait_ge(s_v.h, vn)
                    s_s.inc(s.activation(sc128[0:NH, 512:1024], sc128[0:NH, 512:1024], AF.Ln))
                SC(dtl)
                sn2 = s_s.n
                def dtf(v, sn2=sn2, b=b):
                    v.wait_ge(s_s.h, sn2)
                    v.tensor_add(dt_blk[:, :], dt_blk[:, :], sc128[0:NH, 512:1024])
                    v.tensor_copy(dtb_bf[:, :], dt_blk[:, :])
                    s_v.inc(v.tensor_scalar_mul(a_bf[:, :], dt_blk[:, :], smalls[0:NH, SM_NA:SM_NA + 1]))
                V(dtf)
        if d_xcv is not None and not is_pre:
            sn = s_s.n
            def dbg_xc(sync, b=b, sn=sn):
                sync.wait_ge(s_s.h, sn)
                s_st.inc(sync.dma_start(out=d_xcv[b], in_=xcv_bf[:]), 16)
            SY(dbg_xc)
        if d_dt is not None and not is_pre:
            vn = s_v.n
            def dbg_dt(sync, b=b, vn=vn):
                sync.wait_ge(s_v.h, vn)
                s_st.inc(sync.dma_start(out=d_dt[b], in_=dt_blk[:, :]), 16)
            SY(dbg_dt)

    def emit_acum(b, c, is_pre):
        vv0 = s_v.n
        def tra(t, c=c, vv0=vv0):
            if vv0:
                t.wait_ge(s_v.h, vv0)
            P2b = P2[:].bitcast(BF16)
            t.matmul(P2b[:, 0:NH], a_bf[:, c * C:(c + 1) * C], ident_s[0:NH, 0:NH],
                     is_transpose=True, start=True, stop=True)
            pe_done(t)
        T(tra)
        pn = s_pe.n
        def eva(v, pn=pn):
            v.wait_ge(s_pe.h, pn)
            P2b = P2[:].bitcast(BF16)
            s_v.inc(v.tensor_copy(a_tmb[:], P2b[:, 0:NH]))
        V(eva)
        vn = s_v.n
        def mmac(t, vn=vn, is_pre=is_pre):
            t.wait_ge(s_v.h, vn)
            t.matmul(P2[0:NH, 0:C], a_tmb[:], triu_bfs[:], start=True, stop=True)
            if not is_pre:
                t.matmul(P2[0:16, 256:256 + C], a_tmb[:, 0:16], triu_bfs[:], start=True, stop=True)
                t.matmul(P2[0:16, 512:512 + C], a_tmb[:, 16:32], triu_bfs[:], start=True, stop=True)
            pe_done(t)
        T(mmac)
        pn2 = s_pe.n
        def evac(v, pn2=pn2):
            v.wait_ge(s_pe.h, pn2)
            s_v.inc(v.tensor_copy(acum_f[:, :], P2[0:NH, 0:C]))
        V(evac)
        if not is_pre:
            pvh = s_pe.n
            axh = s_ax.n
            def evhh(v, pvh=pvh, axh=axh):
                v.wait_ge(s_pe.h, pvh)
                if axh:
                    v.wait_ge(s_ax.h, axh)
                for half in range(2):
                    src = P2[0:16, 256 + 256 * half:256 + 256 * half + C]
                    hs = hi_t[:, half * C:(half + 1) * C]
                    ls = lo_t[:, half * C:(half + 1) * C]
                    v.tensor_copy(acf_h[:, :], src)
                    v.tensor_copy(hs, src)
                    v.tensor_tensor(acf_h[:, :], acf_h[:, :], hs, ALU.subtract)
                    v.tensor_copy(ls, acf_h[:, :])
                v.tensor_scalar_mul(ab_hi[0:16, :], hi_t[:, 0:C], -1.0)
                s_v.inc(v.tensor_scalar_mul(ab_lo[0:16, :], lo_t[:, 0:C], -1.0))
            V(evhh)
            vnh = s_v.n
            def flh(sync, vnh=vnh):
                sync.wait_ge(s_v.h, vnh)
                s_ax.inc(sync.dma_start(out=dr_flat[0:1, 0:2048], in_=hi_t[:, 0:C]), 16)
                s_ax.inc(sync.dma_start(out=dr_flat[0:1, 2048:4096], in_=hi_t[:, C:2 * C]), 16)
                s_ax.inc(sync.dma_start(out=dr_flat[1:2, 0:2048], in_=lo_t[:, 0:C]), 16)
                s_ax.inc(sync.dma_start(out=dr_flat[1:2, 2048:4096], in_=lo_t[:, C:2 * C]), 16)
            SC(flh)
            axn2 = s_ax.n
            def flr0(sync, axn2=axn2):
                sync.wait_ge(s_ax.h, axn2)
                s_ax.inc(sync.dma_start(out=eind_hi[32:33, :], in_=dr_flat[0:1, 0:2048]), 16)
                s_ax.inc(sync.dma_start(out=eind_lo[32:33, :], in_=dr_flat[1:2, 0:2048]), 16)
            SC(flr0)
            def prep_half1():
                pv1 = s_pe.n
                def ab1(v, pv1=pv1):
                    v.wait_ge(s_pe.h, pv1)
                    v.tensor_scalar_mul(ab_hi[0:16, :], hi_t[:, C:2 * C], -1.0)
                    s_v.inc(v.tensor_scalar_mul(ab_lo[0:16, :], lo_t[:, C:2 * C], -1.0))
                V(ab1)
                def flr1(sync, pv1=pv1):
                    sync.wait_ge(s_pe.h, pv1)
                    s_ax.inc(sync.dma_start(out=eind_hi[32:33, :], in_=dr_flat[0:1, 2048:4096]), 16)
                    s_ax.inc(sync.dma_start(out=eind_lo[32:33, :], in_=dr_flat[1:2, 2048:4096]), 16)
                SC(flr1)
            emit_acum.prep_half1 = prep_half1
        vn2 = s_v.n
        def sdec(s, vn2=vn2, is_pre=is_pre):
            s.wait_ge(s_v.h, vn2)
            s.activation(dec_bf[:, :], acum_f[:, :], AF.Exp, bias=acum_f[:, C - 1:C], scale=-1.0)
            i = s.activation(expac_f[:, :], acum_f[:, :], AF.Exp)
            s_s.inc(i)
        SC(sdec)
        sn = s_s.n
        def vdec(v, sn=sn, is_pre=is_pre, c=c):
            v.wait_ge(s_s.h, sn)
            if is_pre:
                i = v.tensor_tensor(dtdec_bf[:, :], dtb_bf[:, c * C:(c + 1) * C], dec_bf[:, :], ALU.mult)
            else:
                i = v.tensor_copy(expac_bf[:, :], expac_f[:, :])
            s_v.inc(i)
        V(vdec)
        if d_acum is not None and not is_pre:
            vn3 = s_v.n
            def dbg_ac(sync, b=b, c=c, vn3=vn3):
                sync.wait_ge(s_v.h, vn3)
                s_st.inc(sync.dma_start(out=d_acum[b * NCB + c], in_=acum_f[:, :]), 16)
            SY(dbg_ac)

    def emit_xh_tm(b, c):
        sv = s_s.n
        vv = s_v.n
        def trx(t, c=c, sv=sv, vv=vv):
            t.wait_ge(s_s.h, sv)
            t.wait_ge(s_v.h, vv)
            P3b = P3[:].bitcast(BF16)
            for ftile in range(NFIN):
                t.matmul(P3b[:, ftile * 128:(ftile + 1) * 128],
                         xcv_bf[:, ftile * BLK + c * C: ftile * BLK + (c + 1) * C],
                         ident_s[:], is_transpose=True, start=True, stop=True)
            P2b = P2[:].bitcast(BF16)
            t.matmul(P2b[:, 256:256 + 128],
                     xcv_bf[:, 16 * BLK + c * C: 16 * BLK + (c + 1) * C],
                     ident_s[:], is_transpose=True, start=True, stop=True)
            pe_done(t)
        T(trx)
        pn = s_pe.n
        def evx(v, pn=pn):
            v.wait_ge(s_pe.h, pn)
            P3b = P3[:].bitcast(BF16)
            P2b = P2[:].bitcast(BF16)
            v.tensor_copy(xh_tm[:], P3b[:, 0:DIN])
            s_v.inc(v.tensor_copy(b_tm[:], P2b[:, 256:256 + 128]))
        V(evx)

    def emit_scores(b, c):
        sv = s_s.n
        vv0 = s_v.n
        def mms(t, c=c, sv=sv, vv0=vv0):
            t.wait_ge(s_s.h, sv)
            t.wait_ge(s_v.h, vv0)
            t.matmul(P2[:, 512:512 + 128],
                     xcv_bf[:, 16 * BLK + c * C:16 * BLK + (c + 1) * C],
                     xcv_bf[:, 17 * BLK + c * C:17 * BLK + (c + 1) * C],
                     start=True, stop=True)
            pe_done(t)
        T(mms)
        pn = s_pe.n
        def evs(v, pn=pn):
            v.wait_ge(s_pe.h, pn)
            s_v.inc(v.tensor_tensor(mskst[:], P2[:, 512:512 + 128], triu_fs[:], ALU.mult))
        V(evs)

    def emit_uh(b, c, is_pre):
        vv = s_v.n
        def mmdt(t, c=c, vv=vv, is_pre=is_pre):
            t.wait_ge(s_v.h, vv)
            lhs = dtdec_bf[:, :] if is_pre else dtb_bf[:, c * C:(c + 1) * C]
            for half in range(2):
                PZ = PS[half]
                for q in range(2):
                    t.matmul(PZ[:, q * 512:(q + 1) * 512], lhs,
                             eind2_s[:, half * 1024 + q * 512: half * 1024 + (q + 1) * 512],
                             start=True, stop=True)
            pe_done(t)
        T(mmdt)
        pn = s_pe.n
        def evuh(v, pn=pn, is_pre=is_pre):
            v.wait_ge(s_pe.h, pn)
            dst = uh2_bf if is_pre else uh_bf
            v.tensor_tensor(dst[:, 0:1024], xh_tm[:, 0:1024], P0[:], ALU.mult)
            s_v.inc(v.tensor_tensor(dst[:, 1024:2048], xh_tm[:, 1024:2048], P1[:], ALU.mult))
        V(evuh)
        if not is_pre:
            vv2 = s_v.n
            def mmdec(t, c=c, vv2=vv2):
                t.wait_ge(s_v.h, vv2)
                for half in range(2):
                    PZ = PS[half]
                    for q in range(2):
                        t.matmul(PZ[:, q * 512:(q + 1) * 512], dec_bf[:, :],
                                 eind2_s[:, half * 1024 + q * 512: half * 1024 + (q + 1) * 512],
                                 start=True, stop=True)
                pe_done(t)
            T(mmdec)
            pn2 = s_pe.n
            def evuh2(v, pn2=pn2):
                v.wait_ge(s_pe.h, pn2)
                v.tensor_tensor(uh2_bf[:, 0:1024], uh_bf[:, 0:1024], P0[:], ALU.mult)
                s_v.inc(v.tensor_tensor(uh2_bf[:, 1024:2048], uh_bf[:, 1024:2048], P1[:], ALU.mult))
            V(evuh2)

    def emit_ea(b, c):
        vv = s_v.n
        def mmea(t, vv=vv):
            t.wait_ge(s_v.h, vv)
            for half in range(2):
                PZ = PS[half]
                for q in range(2):
                    t.matmul(PZ[:, q * 512:(q + 1) * 512], expac_bf[:, :],
                             eind2_s[:, half * 1024 + q * 512: half * 1024 + (q + 1) * 512],
                             start=True, stop=True)
            pe_done(t)
        T(mmea)
        pn = s_pe.n
        def evea(s, pn=pn):
            s.wait_ge(s_pe.h, pn)
            s.activation(ea_bf[:, 0:1024], P0[:], AF.Copy)
            s_s.inc(s.activation(ea_bf[:, 1024:2048], P1[:], AF.Copy))
        SC(evea)

    def emit_ssd_quadratic(b, c):
        vv = s_v.n
        sv = s_s.n
        def mmdsk(t, c=c, sv=sv, vv=vv):
            t.wait_ge(s_s.h, sv)
            t.wait_ge(s_v.h, vv)
            for ftile in range(NFIN):
                PZ = P2 if ftile < 8 else P3
                off = (ftile % 8) * 128
                t.matmul(PZ[:, off:off + 128],
                         xcv_bf[:, ftile * BLK + c * C: ftile * BLK + (c + 1) * C],
                         dskd_s[:, ftile * 128:(ftile + 1) * 128],
                         start=((ftile % 8) % 4 == 0), stop=False)
            pe_done(t)
        T(mmdsk)
        if d_eind is not None and b == 0 and c == 0:
            axq = s_ax.n
            def dbg_ei(sync, axq=axq):
                sync.wait_ge(s_ax.h, axq)
                s_st.inc(sync.dma_start(out=d_eind, in_=eind_hi[:]), 16)
            SY(dbg_ei)
        for half in range(2):
            axn = s_ax.n
            vv2 = s_v.n
            sv2 = s_s.n
            def mmr(t, half=half, axn=axn, vv2=vv2, sv2=sv2):
                t.wait_ge(s_ax.h, axn)
                t.wait_ge(s_v.h, vv2)
                t.wait_ge(s_s.h, sv2)
                for q in range(4):
                    PZ = PS[q // 2]
                    off = (q % 2) * 512
                    t.matmul(PZ[:, off:off + 512], ab_hi[0:33, :],
                             eind_hi[:, q * 512:(q + 1) * 512], start=True, stop=False)
                    t.matmul(PZ[:, off:off + 512], ab_lo[0:33, :],
                             eind_lo[:, q * 512:(q + 1) * 512], start=False, stop=False)
                for hl in range(16):
                    PZ = PS[hl // 8]
                    off = (hl % 8) * 128
                    t.matmul(PZ[:, off:off + 128], ident_s[:], maskneg_s[:],
                             start=False, stop=True)
                pe_done(t)
            T(mmr)
            if half == 0:
                emit_acum.prep_half1()
            pn = s_pe.n
            def eve(s, pn=pn):
                s.wait_ge(s_pe.h, pn)
                s.activation(egt[:, 0:1024], P0[:], AF.Exp)
                s_s.inc(s.activation(egt[:, 1024:2048], P1[:], AF.Exp))
            SC(eve)
            sn = s_s.n
            def gtm(v, sn=sn):
                v.wait_ge(s_s.h, sn)
                src = mskst[:].unsqueeze(1).broadcast_to([128, 16, C])
                s_v.inc(v.tensor_tensor(egt[:].rearrange("p (h l) -> p h l", h=16),
                                        egt[:].rearrange("p (h l) -> p h l", h=16), src, ALU.mult))
            V(gtm)
            vn = s_v.n
            def mmyd(t, half=half, vn=vn):
                t.wait_ge(s_v.h, vn)
                PZ = P2 if half == 0 else P3
                for hl in range(16):
                    h = half * 16 + hl
                    t.matmul(PZ[:, hl * 64:hl * 64 + 64],
                             egt[:, hl * C:(hl + 1) * C],
                             uh_bf[:, h * PH:(h + 1) * PH],
                             start=False, stop=(hl == 15))
                pe_done(t)
            T(mmyd)

    def emit_yoff_and_y(b, c):
        vv = s_v.n
        sn0 = s_s.n
        def cpsb(v, vv=vv):
            s_v.inc(v.tensor_copy(state_bf[:], state[:]))
        V(cpsb)
        vn0 = s_v.n
        def mmyo(t, c=c, vn0=vn0, sn0=sn0):
            t.wait_ge(s_v.h, vn0)
            t.wait_ge(s_s.h, sn0)
            for h in range(NH):
                PZ = P0 if h < 16 else P1
                t.matmul(PZ[:, (h % 16) * 64:(h % 16) * 64 + 64],
                         xcv_bf[:, 17 * BLK + c * C:17 * BLK + (c + 1) * C],
                         state_bf[:, h * PH:(h + 1) * PH],
                         start=(h % 8 == 0), stop=(h % 16 == 15))
            pe_done(t)
        T(mmyo)
        pn = s_pe.n
        def ya(v, pn=pn, c=c):
            v.wait_ge(s_pe.h, pn)
            v.tensor_tensor(y_f32[:, 0:1024], P0[:], ea_bf[:, 0:1024], ALU.mult)
            v.tensor_tensor(y_f32[:, 1024:2048], P1[:], ea_bf[:, 1024:2048], ALU.mult)
            v.tensor_tensor(y_f32[:, 0:1024], y_f32[:, 0:1024], P2[:], ALU.add)
            s_v.inc(v.tensor_tensor(y_f32[:, 1024:2048], y_f32[:, 1024:2048], P3[:], ALU.add))
        V(ya)
        if d_y is not None:
            vn = s_v.n
            def dbg_y(sync, b=b, c=c, vn=vn):
                sync.wait_ge(s_v.h, vn)
                s_st.inc(sync.dma_start(out=d_y[b * NCB + c], in_=y_f32[:]), 16)
            SY(dbg_y)
        stn_y = s_st.n
        def yz(v, c=c, stn_y=stn_y):
            if stn_y:
                v.wait_ge(s_st.h, stn_y)
            s_v.inc(v.tensor_tensor(y_f32[:], y_f32[:], z_bf[:, c * DIN:(c + 1) * DIN], ALU.mult))
        V(yz)
        vn2 = s_v.n
        def gn(s, vn2=vn2):
            s.wait_ge(s_v.h, vn2)
            s.activation(sc128[:, 0:DIN], y_f32[:], AF.Square, accum_out=smalls[:, SM_TMP:SM_TMP + 1])
            s_s.inc(s.activation(smalls[:, SM_TMP:SM_TMP + 1], smalls[:, SM_TMP:SM_TMP + 1], AF.Sqrt,
                                 bias=epsc[:, 3:4], scale=1.0 / DIN))
        SC(gn)
        sn = s_s.n
        def gv(v, sn=sn):
            v.wait_ge(s_s.h, sn)
            v.reciprocal(smalls[:, SM_TMP2:SM_TMP2 + 1], smalls[:, SM_TMP:SM_TMP + 1])
            s_v.inc(v.tensor_scalar_mul(uh_bf[:], y_f32[:], smalls[:, SM_TMP2:SM_TMP2 + 1]))
        V(gv)
        vn3 = s_v.n
        def trg(t, c=c, vn3=vn3):
            t.wait_ge(s_v.h, vn3)
            P3b = P3[:].bitcast(BF16)
            for ftile in range(NFIN):
                t.matmul(P3b[:, ftile * 128:(ftile + 1) * 128],
                         uh_bf[:, ftile * 128:(ftile + 1) * 128],
                         ident_s[:], is_transpose=True, start=True, stop=True)
            pe_done(t)
        T(trg)
        pn2 = s_pe.n
        def evg(v, c=c, pn2=pn2, b=b):
            v.wait_ge(s_pe.h, pn2)
            if b == 0 and c == 0:
                v.wait_ge(s_ws.h, preload.stream_done)
            P3b = P3[:].bitcast(BF16)
            dst = g_blk[:].rearrange("p (f b) -> p f b", f=NFIN)[:, :, c * C:(c + 1) * C]
            s_v.inc(v.tensor_copy(dst, P3b.rearrange("p (f l) -> p f l", f=NFIN)))
        V(evg)

    def emit_scan2(b, c, is_pre, last_pre=False, first_main=False):
        vv = s_v.n
        def mmst(t, vv=vv):
            t.wait_ge(s_v.h, vv)
            for h in range(NH):
                PZ = P0 if h < 16 else P1
                t.matmul(PZ[:, (h % 16) * 64:(h % 16) * 64 + 64],
                         b_tm[:],
                         uh2_bf[:, h * PH:(h + 1) * PH],
                         start=(h % 8 == 0), stop=(h % 16 == 15))
            pe_done(t)
        T(mmst)
        def cdr(v):
            src = expac_f[:, C - 1:C].broadcast_to([NH, 128])
            s_v.inc(v.tensor_copy(cdrep[:, :], src))
        V(cdr)
        vn = s_v.n
        def mmcd(t, vn=vn):
            t.wait_ge(s_v.h, vn)
            for q in range(2):
                t.matmul(P2[:, q * 512:(q + 1) * 512], cdrep[:, :],
                         eind2_s[:, q * 512:(q + 1) * 512], start=True, stop=True)
                t.matmul(P3[:, q * 512:(q + 1) * 512], cdrep[:, :],
                         eind2_s[:, 1024 + q * 512: 1024 + (q + 1) * 512], start=True, stop=True)
            pe_done(t)
        T(mmcd)
        pn = s_pe.n
        def upd(v, pn=pn, first_main=first_main):
            v.wait_ge(s_pe.h, pn)
            if first_main:
                pass
            v.tensor_tensor(state[:, 0:1024], state[:, 0:1024], P2[:], ALU.mult)
            v.tensor_tensor(state[:, 1024:2048], state[:, 1024:2048], P3[:], ALU.mult)
            v.tensor_tensor(state[:, 0:1024], state[:, 0:1024], P0[:], ALU.add)
            s_v.inc(v.tensor_tensor(state[:, 1024:2048], state[:, 1024:2048], P1[:], ALU.add))
        V(upd)
        if d_state is not None and not is_pre:
            vn2 = s_v.n
            def dbg_s(sync, b=b, c=c, vn2=vn2):
                sync.wait_ge(s_v.h, vn2)
                s_st.inc(sync.dma_start(out=d_state[b * NCB + c], in_=state[:]), 16)
            SY(dbg_s)

    def emit_flag():
        def fl(v):
            v.tensor_scalar_mul(state[:], state[:], smalls[:, SM_FLAG:SM_FLAG + 1])
            s_v.inc(v.tensor_scalar_mul(tails[:], tails[:], smalls[:, SM_FLAG:SM_FLAG + 1]))
        V(fl)
        if d_pstate is not None:
            vn = s_v.n
            def dbg_p(sync, vn=vn):
                sync.wait_ge(s_v.h, vn)
                s_st.inc(sync.dma_start(out=d_pstate, in_=state[:]), 16)
            SY(dbg_p)

    def emit_outproj_h(b):
        vv = s_v.n
        for ftile in range(NFIN):
            par = ftile % 2
            pv = s_pe.n
            def ldw(sync, ftile=ftile, par=par, pv=pv):
                if pv:
                    sync.wait_ge(s_pe.h, pv)
                sync.wait_ge(s_ws.h, preload.wready["wout"])
                s_ld.inc(sync.dma_start(out=wo_rot[:, par * D:(par + 1) * D],
                                        in_=dr_woutbf[128 * ftile:128 * (ftile + 1), :]), 16)
            SY(ldw)
            ln = s_ld.n
            def fw(v, ftile=ftile, par=par, ln=ln):
                v.wait_ge(s_ld.h, ln)
                s_v.inc(v.tensor_scalar_mul(wo_rot[:, par * D:(par + 1) * D],
                                            wo_rot[:, par * D:(par + 1) * D],
                                            smalls[:, 8 + ftile:9 + ftile]))
            V(fw)
            vvf = s_v.n
            def mmo(t, ftile=ftile, par=par, ln=ln, vv=vv, vvf=vvf):
                t.wait_ge(s_ld.h, ln)
                t.wait_ge(s_v.h, max(vv, vvf))
                for tt in range(NTT):
                    PZ = PS[tt]
                    for half in range(2):
                        t.matmul(PZ[:, half * 512:(half + 1) * 512],
                                 g_blk[:, ftile * BLK + tt * 128:ftile * BLK + (tt + 1) * 128],
                                 wo_rot[:, par * D + half * 512: par * D + (half + 1) * 512],
                                 start=(ftile == 0), stop=(ftile == NFIN - 1))
                if ftile == NFIN - 1:
                    pe_done(t)
                else:
                    s_pe.inc(t.nop())
            T(mmo)
        if d_g is not None:
            def dbg_g(sync, b=b, vv=vv):
                sync.wait_ge(s_v.h, vv)
                s_st.inc(sync.dma_start(out=d_g[b], in_=g_blk[:]), 16)
            SY(dbg_g)
        pn = s_pe.n
        for tt in range(NTT):
            s0 = 6 * tt
            par = tt % 2
            stn = s_st.n
            def he(v, tt=tt, par=par, pn=pn, s0=s0, stn=stn):
                v.wait_ge(s_pe.h, pn)
                if stn:
                    v.wait_ge(s_st.h, stn)
                PZ = PS[tt]
                hbuf = h_rot[:, par * D:(par + 1) * D]
                v.tensor_scalar_mul(sc128[:, 1024:1024 + D], u_tm[:, tt * D:(tt + 1) * D], stats[:, s0 + 4:s0 + 5])
                s_v.inc(v.tensor_tensor(hbuf, sc128[:, 1024:1024 + D], PZ[:], ALU.add))
            V(he)
            vn = s_v.n
            def hs(sync, b=b, tt=tt, par=par, vn=vn):
                sync.wait_ge(s_v.h, vn)
                s_st.inc(sync.dma_start(out=dr_h[b * BLK + tt * 128: b * BLK + (tt + 1) * 128, :],
                                        in_=h_rot[:, par * D:(par + 1) * D]), 16)
                if d_h is not None:
                    s_st.inc(sync.dma_start(out=d_h[b * BLK + tt * 128: b * BLK + (tt + 1) * 128, :],
                                            in_=h_rot[:, par * D:(par + 1) * D]), 16)
            SY(hs)
            def hns(s, tt=tt, par=par, s0=s0, vn=vn):
                s.wait_ge(s_v.h, vn)
                s.activation(sc128[:, 0:D], h_rot[:, par * D:(par + 1) * D], AF.Square,
                             accum_out=stats[:, s0 + 5:s0 + 6])
                s_s.inc(s.activation(stats[:, s0 + 5:s0 + 6], stats[:, s0 + 5:s0 + 6], AF.Sqrt,
                                     bias=epsc[:, 1:2], scale=1.0 / D))
            SC(hns)
            sn = s_s.n
            def hnv(v, tt=tt, par=par, s0=s0, sn=sn):
                v.wait_ge(s_s.h, sn)
                v.reciprocal(stats[:, s0 + 5:s0 + 6], stats[:, s0 + 5:s0 + 6])
                s_v.inc(v.tensor_scalar_mul(hn_rot[:, par * D:(par + 1) * D],
                                            h_rot[:, par * D:(par + 1) * D], stats[:, s0 + 5:s0 + 6]))
            V(hnv)
            vn2 = s_v.n
            def trh(t, tt=tt, par=par, vn2=vn2):
                t.wait_ge(s_v.h, vn2)
                PZb = PS[tt][:].bitcast(BF16)
                for k in range(NKD):
                    t.matmul(PZb[:, k * 128:(k + 1) * 128],
                             hn_rot[:, par * D + k * 128: par * D + (k + 1) * 128],
                             ident_s[:], is_transpose=True, start=True, stop=True)
                pe_done(t)
            T(trh)
            pn2 = s_pe.n
            def evh(v, tt=tt, pn2=pn2):
                v.wait_ge(s_pe.h, pn2)
                PZb = PS[tt][:].bitcast(BF16)
                par2 = tt % 2
                s_v.inc(v.tensor_copy(hnf_rot[:, par2 * D:(par2 + 1) * D], PZb[:, 0:D]))
            V(evh)
            vn3 = s_v.n
            def hnst(sync, b=b, tt=tt, vn3=vn3):
                sync.wait_ge(s_v.h, vn3)
                par2 = tt % 2
                for k in range(NKD):
                    s_st.inc(sync.dma_start(
                        out=dr_hn[k * 128:(k + 1) * 128, b * BLK + tt * 128: b * BLK + (tt + 1) * 128],
                        in_=hnf_rot[:, par2 * D + k * 128: par2 * D + (k + 1) * 128]), 16)
                    if d_hn is not None:
                        s_st.inc(sync.dma_start(
                            out=d_hn[k * 128:(k + 1) * 128, b * BLK + tt * 128: b * BLK + (tt + 1) * 128],
                            in_=hnf_rot[:, par2 * D + k * 128: par2 * D + (k + 1) * 128]), 16)
            SY(hnst)

    def emit_m2():
        barrier()
        es1.close()
        es2 = contextlib.ExitStack()
        sb2 = lambda n, sh, dt_: es2.enter_context(nc.sbuf_tensor(n, sh, dt_))
        g1_bf = sb2("g1_bf", [128, NFF * BLK], BF16)
        relu_s = sb2("relu_s", [128, 4 * BLK], BF16)
        hn_s = sb2("hn_s", [128, NKD * BLK], BF16)
        x1_s = sb2("x1_s", [128, 2 * D], F32)
        wproj_rot = sb2("wproj_rot", [128, 11 * D], BF16)
        pm2_s = sb2("pm2_s", [128, 2 * D], F32) if d_pm2 is not None else None

        esw = contextlib.ExitStack()
        wbig = esw.enter_context(nc.sbuf_tensor("wbig", [128, NKD * FF], BF16))

        vnw = s_v.n
        def ldwfc(sync, vnw=vnw):
            sync.wait_ge(s_v.h, vnw)
            sync.wait_ge(s_ws.h, preload.wready["wfc"])
            for k in range(NKD):
                s_ld.inc(sync.dma_start(out=wbig[:, k * FF:(k + 1) * FF],
                                        in_=dr_wfcbf[128 * k:128 * (k + 1), :]), 16)
        SY(ldwfc)
        wfc_ld = s_ld.n


        mlp_state = {"last_v": 0, "last_s": 0, "m1_pe_end": 0, "tail_free_v": 0,
                     "tail_free_s": 0, "pending": [], "ys4_free": {}}

        def mlp_tb(tb, proj_src, is_ffn, rhs_ld, pair_tail, rhs_v=0):
            sq2_at = {}
            rl_at = {}
            work = mlp_state["pending"]
            mlp_state["pending"] = []
            p01v, p01s = mlp_state["last_v"], mlp_state["last_s"]
            for f0 in range(0, NFF, 4):
                pns = {}
                for f in range(f0, f0 + 4):
                    PZ = PS[(f % 4) // 2]
                    off = ((f % 4) % 2) * 512
                    def mmf(t, f=f, PZ=PZ, off=off, rhs_ld=rhs_ld, p01v=p01v, p01s=p01s, rhs_v=rhs_v):
                        if f == 0:
                            t.wait_ge(s_ld.h, wfc_ld)
                            if rhs_ld:
                                t.wait_ge(s_ld.h, rhs_ld)
                        if f < 4:
                            if rhs_v:
                                t.wait_ge(s_v.h, rhs_v)
                            if p01v:
                                t.wait_ge(s_v.h, p01v)
                            if p01s:
                                t.wait_ge(s_s.h, p01s)
                            if EXPT_SERIAL and mlp_state["tail_free_v"]:
                                t.wait_ge(s_v.h, mlp_state["tail_free_v"])
                        else:
                            t.wait_ge(s_v.h, sq2_at[f - 4])
                            t.wait_ge(s_s.h, rl_at[f - 4])
                        for k in range(NKD):
                            t.matmul(PZ[:, off:off + 512],
                                     wbig[:, k * FF + f * 128:k * FF + (f + 1) * 128],
                                     hn_s[:, k * BLK:(k + 1) * BLK],
                                     start=(k == 0), stop=(k == NKD - 1))
                        pe_done(t)
                    T(mmf)
                    pns[f] = s_pe.n
                def rl(sc, f0=f0, pns=dict(pns)):
                    for f in range(f0, f0 + 4):
                        PZ = PS[(f % 4) // 2]
                        off = ((f % 4) % 2) * 512
                        sc.wait_ge(s_pe.h, pns[f])
                        s_s.inc(sc.activation(relu_s[:, (f % 4) * BLK:(f % 4 + 1) * BLK],
                                              PZ[:, off:off + 512], AF.Relu))
                SC(rl)
                for f in range(f0, f0 + 4):
                    rl_at[f] = s_s.n - (f0 + 3 - f)
                for f in range(f0, f0 + 4):
                    PZ = PS[(f % 4) // 2]
                    off = ((f % 4) % 2) * 512
                    sn = rl_at[f]
                    def sq2(v, f=f, PZ=PZ, off=off, sn=sn, dst=mlp_state.get("last_st", 0)):
                        if f == 0 and dst:
                            v.wait_ge(s_st.h, dst)
                        v.wait_ge(s_s.h, sn)
                        s_v.inc(v.tensor_tensor(g1_bf[:, f * BLK:(f + 1) * BLK],
                                                relu_s[:, (f % 4) * BLK:(f % 4 + 1) * BLK], PZ[:, off:off + 512], ALU.mult))
                    V(sq2)
                    sq2_at[f] = s_v.n
                if work and f0 in (12, 28):
                    work.pop(0)()
            mlp_state["last_v"] = s_v.n
            mlp_state["last_s"] = s_s.n
            mlp_state["m1_pe_end"] = s_pe.n
            g1_done = s_v.n
            if d_g1 is not None and not is_ffn:
                def dbg_g1(sync, tb=tb, g1_done=g1_done):
                    sync.wait_ge(s_v.h, g1_done)
                    s_st.inc(sync.dma_start(out=d_g1[tb], in_=g1_bf[:]), 16)
                SY(dbg_g1)
                mlp_state["last_st"] = s_st.n
            tfv, tfs = mlp_state["tail_free_v"], mlp_state["tail_free_s"]
            mmp_at = {}
            lns = {}

            def emit_ldwp(pair, f, pv):
                slot = f % 11
                def ldwp(sync, f=f, slot=slot, pv=pv, is_ffn=is_ffn, proj_src=proj_src):
                    if pv:
                        sync.wait_ge(s_pe.h, pv)
                    sync.wait_ge(s_ws.h, preload.wready["Wproj" if is_ffn else "wproj"])
                    s_wp.inc(sync.dma_start(out=wproj_rot[:, slot * D:(slot + 1) * D],
                                            in_=proj_src[f * 128:(f + 1) * 128, :]), 16)
                SY(ldwp)
                lns[(pair, f)] = s_wp.n

            for pair in range(2):
                for j in range(7):
                    if pair == 1:
                        pv = mmp_at[(0, min(j + 25, NFF - 1))]
                    else:
                        pv = mlp_state.get("prev_tail7", [0] * 7)[j]
                    emit_ldwp(pair, j, pv)
                for f in range(NFF):
                    slot = f % 11
                    ln = lns[(pair, min(f + 6, NFF - 1))]
                    def mmp(t, f=f, pair=pair, slot=slot, ln=ln, g1_done=g1_done, tfv=tfv, tfs=tfs):
                        t.wait_ge(s_wp.h, ln)
                        if f == 0:
                            t.wait_ge(s_v.h, g1_done if pair == 0 else mlp_state["pt_v"])
                            if pair == 0:
                                if tfv:
                                    t.wait_ge(s_v.h, tfv)
                                if tfs:
                                    t.wait_ge(s_s.h, tfs)
                            else:
                                if mlp_state["pt_s"]:
                                    t.wait_ge(s_s.h, mlp_state["pt_s"])
                        for ti in range(2):
                            tt = 2 * pair + ti
                            for hf in range(2):
                                t.matmul(PS[2 + ti][:, hf * 512:(hf + 1) * 512],
                                         g1_bf[:, f * BLK + tt * 128:f * BLK + (tt + 1) * 128],
                                         wproj_rot[:, slot * D + hf * 512:slot * D + (hf + 1) * 512],
                                         start=(f == 0), stop=(f == NFF - 1))
                        if f == NFF - 1:
                            pe_done(t)
                        else:
                            s_pe.inc(t.nop())
                    T(mmp)
                    mmp_at[(pair, f)] = s_pe.n
                    if f + 7 < NFF:
                        emit_ldwp(pair, f + 7, mmp_at[(pair, f)])
                if pair == 1:
                    mlp_state["prev_tail7"] = [mmp_at[(1, min(j + 25, NFF - 1))] for j in range(7)]
                if pair == 0:
                    if work:
                        work.pop(0)()
                else:
                    while work:
                        work.pop(0)()
                tv, ts = pair_tail(tb, pair, s_pe.n)
                if pair == 0:
                    mlp_state["pt_v"], mlp_state["pt_s"] = tv, ts
                else:
                    mlp_state["tail_free_v"], mlp_state["tail_free_s"] = tv, ts

        def m2a_tail(tb, pair, pe_cnt):
            lastv = 0
            dumped = 0
            if d_pm2 is not None:
                stp = s_st.n
                def dmpv(v, pe_cnt=pe_cnt, stp=stp):
                    v.wait_ge(s_pe.h, pe_cnt)
                    if stp:
                        v.wait_ge(s_st.h, stp)
                    v.tensor_copy(pm2_s[:, 0:D], PS[2][:])
                    s_v.inc(v.tensor_copy(pm2_s[:, D:2 * D], PS[3][:]))
                V(dmpv)
                vq = s_v.n
                def dmp(sync, tb=tb, pair=pair, vq=vq):
                    sync.wait_ge(s_v.h, vq)
                    s_st.inc(sync.dma_start(out=d_pm2[tb, pair, 0], in_=pm2_s[:, 0:D]), 16)
                    s_st.inc(sync.dma_start(out=d_pm2[tb, pair, 1], in_=pm2_s[:, D:2 * D]), 16)
                SY(dmp)
                dumped = s_st.n
            for ti in range(2):
                tt = 2 * pair + ti
                par = tt % 2
                vv = s_v.n
                sv = s_s.n
                stn = s_st.n
                def ldxh(sync, tb=tb, tt=tt, par=par, vv=vv, sv=sv):
                    if vv:
                        sync.wait_ge(s_v.h, vv)
                    if sv:
                        sync.wait_ge(s_s.h, sv)
                    s_ld.inc(sync.dma_start(out=x0_rot[:, par * D:(par + 1) * D],
                                            in_=xm[tb * BLK + tt * 128:tb * BLK + (tt + 1) * 128, :]), 16)
                    s_ld.inc(sync.dma_start(out=h_rot[:, par * D:(par + 1) * D],
                                            in_=dr_h[tb * BLK + tt * 128:tb * BLK + (tt + 1) * 128, :]), 16)
                SY(ldxh)
                ln = s_ld.n
                def x1a(v, ti=ti, tt=tt, par=par, ln=ln, pe_cnt=pe_cnt, stn=stn, dumped=dumped):
                    v.wait_ge(s_ld.h, ln)
                    v.wait_ge(s_pe.h, pe_cnt)
                    if dumped:
                        v.wait_ge(s_st.h, dumped)
                    if stn:
                        v.wait_ge(s_st.h, stn)
                    v.tensor_tensor(sc128[:, 0:D], x0_rot[:, par * D:(par + 1) * D],
                                    h_rot[:, par * D:(par + 1) * D], ALU.add)
                    s_v.inc(v.tensor_tensor(x1_s[:, par * D:(par + 1) * D], sc128[:, 0:D],
                                            PS[2 + ti][:], ALU.add))
                V(x1a)
                lastv = s_v.n
                vn = s_v.n
                def stx1(sync, tb=tb, tt=tt, par=par, vn=vn):
                    sync.wait_ge(s_v.h, vn)
                    s_st.inc(sync.dma_start(out=dr_x1[tb * BLK + tt * 128:tb * BLK + (tt + 1) * 128, :],
                                            in_=x1_s[:, par * D:(par + 1) * D]), 16)
                    if d_x1 is not None:
                        s_st.inc(sync.dma_start(out=d_x1[tb * BLK + tt * 128:tb * BLK + (tt + 1) * 128, :],
                                                in_=x1_s[:, par * D:(par + 1) * D]), 16)
                SY(stx1)
            return lastv, 0

        for tb in range(NBLK):
            pv0 = mlp_state["m1_pe_end"] if tb > 0 else 0
            def ldhn(sync, tb=tb, pv0=pv0):
                if pv0:
                    sync.wait_ge(s_pe.h, pv0)
                for k in range(NKD):
                    s_ld.inc(sync.dma_start(out=hn_s[:, k * BLK:(k + 1) * BLK],
                                            in_=dr_hn[k * 128:(k + 1) * 128, tb * BLK:(tb + 1) * BLK]), 16)
            SY(ldhn)
            hnld = s_ld.n
            mlp_tb(tb, dr_wprojbf, False, hnld, m2a_tail)

        barrier()
        esw.close()
        esw2 = contextlib.ExitStack()
        wbig2 = esw2.enter_context(nc.sbuf_tensor("wbig2", [128, NKD * FF], BF16))
        ys4 = esw2.enter_context(nc.sbuf_tensor("ys4", [128, 4 * D], F32))
        mem_g = esw2.enter_context(nc.sbuf_tensor("mem_g", [128, D], F32))
        spk_g = esw2.enter_context(nc.sbuf_tensor("spk_g", [128, D], F32))
        spk2_g = esw2.enter_context(nc.sbuf_tensor("spk2_g", [128, D], F32))
        spks_g = esw2.enter_context(nc.sbuf_tensor("spks_g", [128, D], F32))
        def ldWfcT(sync):
            sync.wait_ge(s_ws.h, preload.wready["Wfc"])
            for k in range(NKD):
                s_ld.inc(sync.dma_start(out=wbig2[:, k * FF:(k + 1) * FF],
                                        in_=dr_Wfcbf[128 * k:128 * (k + 1), :]), 16)
        SY(ldWfcT)
        WfcT_ld = s_ld.n

        def m2b_tail(tb, pair, pe_cnt):
            lasts = 0
            for ti in range(2):
                tt = 2 * pair + ti
                vv0, gv0 = mlp_state["ys4_free"].get(tt, (0, 0))
                def evy(sc, ti=ti, tt=tt, pe_cnt=pe_cnt, gv0=gv0, vv0=vv0):
                    sc.wait_ge(s_pe.h, pe_cnt)
                    if gv0:
                        sc.wait_ge(s_g.h, gv0)
                    if vv0:
                        sc.wait_ge(s_v.h, vv0)
                    s_s.inc(sc.activation(ys4[:, tt * D:(tt + 1) * D], PS[2 + ti][:], AF.Copy))
                SC(evy)
                lasts = s_s.n
                mlp_state["pending"].append(
                    lambda tb=tb, tt=tt, sn=s_s.n: emit_piece(tb, tt, sn))
            return 0, lasts

        def emit_piece(tb, tt, sn):
            par = tt % 2
            vvx = s_v.n
            def ldx1b(g, tb=tb, tt=tt, par=par, vvx=vvx):
                if vvx:
                    g.wait_ge(s_v.h, vvx)
                s_px.inc(g.dma_start(out=h_rot[:, par * D:(par + 1) * D],
                                     in_=dr_x1[tb * BLK + tt * 128:tb * BLK + (tt + 1) * 128, :]), 16)
            G(ldx1b)
            lnb = s_px.n
            ys = ys4[:, tt * D:(tt + 1) * D]
            spks = [spk_g, spk2_g]
            gw0 = s_g.n
            def snn_v1(v, sn=sn, ys=ys, gw0=gw0):
                v.wait_ge(s_s.h, sn)
                if gw0:
                    v.wait_ge(s_g.h, gw0)
                v.tensor_single_scalar(spk_g[:], ys, 1.0, ALU.is_gt)
                s_v.inc(v.scalar_tensor_tensor(mem_g[:], ys, 1.9, spk_g[:], ALU.mult, ALU.subtract))
            V(snn_v1)
            vq = s_v.n
            def snn_g1(g, vq=vq):
                g.wait_ge(s_v.h, vq)
                s_g.inc(g.tensor_copy(spks_g[:], spk_g[:]))
            G(snn_g1)
            for step in (3, 4, 5, 6):
                buf = spks[(step - 3 + 1) % 2]
                gq = s_g.n
                def snn_vs(v, step=step, buf=buf, ys=ys, gq=gq):
                    v.wait_ge(s_g.h, gq)
                    v.tensor_single_scalar(buf[:], mem_g[:], 1.0, ALU.is_gt)
                    if step < 6:
                        v.scalar_tensor_tensor(mem_g[:], mem_g[:], 0.9, buf[:], ALU.mult, ALU.subtract)
                        i = v.tensor_tensor(mem_g[:], mem_g[:], ys, ALU.add)
                    else:
                        i = v.nop()
                    s_v.inc(i)
                V(snn_vs)
                vq2 = s_v.n
                def snn_gs(g, buf=buf, vq2=vq2):
                    g.wait_ge(s_v.h, vq2)
                    s_g.inc(g.tensor_add(spks_g[:], spks_g[:], buf[:]))
                G(snn_gs)
            gn2 = s_g.n
            stn = s_st.n
            def outa(v, tt=tt, par=par, gn2=gn2, lnb=lnb, stn=stn):
                v.wait_ge(s_g.h, gn2)
                v.wait_ge(s_px.h, lnb)
                if stn:
                    v.wait_ge(s_st.h, stn)
                s_v.inc(v.scalar_tensor_tensor(x1_s[:, par * D:(par + 1) * D], spks_g[:], 0.2,
                                               h_rot[:, par * D:(par + 1) * D], ALU.mult, ALU.add))
            V(outa)
            vn = s_v.n
            def sto(g, tb=tb, tt=tt, par=par, vn=vn):
                g.wait_ge(s_v.h, vn)
                s_st.inc(g.dma_start(out=out[tb * BLK + tt * 128:tb * BLK + (tt + 1) * 128, :],
                                     in_=x1_s[:, par * D:(par + 1) * D]), 16)
            G(sto)
            mlp_state["ys4_free"][tt] = (s_v.n, s_g.n)

        mlp_state["stats_free"] = (0, 0)
        x0f = {}
        for tb in range(NBLK):
            for tt in range(NTT):
                par = tt % 2
                vv, sv = mlp_state["stats_free"]
                fv, fs = x0f.get(par, (0, 0))
                vv = max(vv, fv)
                sv = max(sv, fs)
                def ldx1(sync, tb=tb, tt=tt, par=par, vv=vv, sv=sv):
                    if vv:
                        sync.wait_ge(s_v.h, vv)
                    if sv:
                        sync.wait_ge(s_s.h, sv)
                    s_ld.inc(sync.dma_start(out=x0_rot[:, par * D:(par + 1) * D],
                                            in_=dr_x1[tb * BLK + tt * 128:tb * BLK + (tt + 1) * 128, :]), 16)
                SY(ldx1)
                ln = s_ld.n
                s0 = 6 * tt
                xa = x0_rot[:, par * D:(par + 1) * D]
                def sq(sc, ln=ln, s0=s0, xa=xa):
                    sc.wait_ge(s_ld.h, ln)
                    s_s.inc(sc.activation(sc128[:, 0:D], xa, AF.Square, accum_out=stats[:, s0 + 1:s0 + 2]))
                SC(sq)
                sn = s_s.n
                def v1(v, ln=ln, sn=sn, s0=s0, xa=xa):
                    v.wait_ge(s_ld.h, ln)
                    v.wait_ge(s_s.h, sn)
                    v.reduce_sum(stats[:, s0:s0 + 1], xa, axis=AX)
                    v.tensor_scalar_mul(stats[:, s0:s0 + 1], stats[:, s0:s0 + 1], 1.0 / D)
                    v.tensor_scalar_mul(stats[:, s0 + 1:s0 + 2], stats[:, s0 + 1:s0 + 2], 1.0 / D)
                    v.scalar_tensor_tensor(stats[:, s0 + 1:s0 + 2], stats[:, s0:s0 + 1], stats[:, s0:s0 + 1],
                                           stats[:, s0 + 1:s0 + 2], ALU.mult, ALU.subtract)
                    s_v.inc(v.tensor_scalar_mul(stats[:, s0 + 1:s0 + 2], stats[:, s0 + 1:s0 + 2], -1.0))
                V(v1)
                vn1 = s_v.n
                def s1(sc, vn1=vn1, s0=s0):
                    sc.wait_ge(s_v.h, vn1)
                    s_s.inc(sc.activation(stats[:, s0 + 2:s0 + 3], stats[:, s0 + 1:s0 + 2], AF.Sqrt, bias=epsc[:, 0:1]))
                SC(s1)
                sn1 = s_s.n
                pvh = s_pe.n
                def v2(v, sn1=sn1, s0=s0, xa=xa, par=par, pvh=pvh):
                    v.wait_ge(s_s.h, sn1)
                    if pvh:
                        v.wait_ge(s_pe.h, pvh)
                    v.reciprocal(stats[:, s0 + 2:s0 + 3], stats[:, s0 + 2:s0 + 3])
                    s_v.inc(v.tensor_scalar(hn_rot[:, par * D:(par + 1) * D], xa,
                                            stats[:, s0:s0 + 1], stats[:, s0 + 2:s0 + 3],
                                            ALU.subtract, ALU.mult))
                V(v2)
                vn2 = s_v.n
                lv = mlp_state["last_v"]
                def trt(t, tt=tt, par=par, vn2=vn2, lv=lv):
                    t.wait_ge(s_v.h, max(vn2, lv))
                    P0b = P0[:].bitcast(BF16)
                    for k in range(NKD):
                        t.matmul(P0b[:, k * 128:(k + 1) * 128],
                                 hn_rot[:, par * D + k * 128:par * D + (k + 1) * 128],
                                 ident_s[:], is_transpose=True, start=True, stop=True)
                    pe_done(t)
                T(trt)
                pn = s_pe.n
                def evt(v, tt=tt, pn=pn):
                    v.wait_ge(s_pe.h, pn)
                    P0b = P0[:].bitcast(BF16)
                    dst = hn_s[:].rearrange("p (k b) -> p k b", k=NKD)[:, :, tt * 128:(tt + 1) * 128]
                    s_v.inc(v.tensor_copy(dst, P0b[:, 0:1024].rearrange("p (k l) -> p k l", k=NKD)))
                V(evt)
                x0f[par] = (s_v.n, s_s.n)
            mlp_state["stats_free"] = (s_v.n, s_s.n)
            if tb == 0:
                def wWT(t):
                    t.wait_ge(s_ld.h, WfcT_ld)
                T(wWT)
            mlp_tb(tb, dr_Wprojbf, True, 0, m2b_tail, rhs_v=s_v.n)
        for th in mlp_state["pending"]:
            th()
        mlp_state["pending"] = []
        esw2.close()
        es2.close()

    preload()
    SA = stop_after or "all"
    def go(tag):
        order = ["preload", "prefront0", "pre0a", "pre0b", "pre0c", "pre0d", "pre0", "pre", "m1front0", "m1acum", "m1xh", "m1scores", "m1uh", "m1ea", "m1quad", "m1yoff", "m1c0", "m1b0", "m1", "all"]
        return order.index(SA) >= order.index(tag)
    if go("prefront0"):
        for b in range(NBLK):
            emit_block_front(xp, b, is_pre=True)
            if SA == "prefront0":
                break
            if go("pre0a"):
                for c in range(NCB):
                    emit_acum(b, c, is_pre=True)
                    if go("pre0b"):
                        emit_xh_tm(b, c)
                    if go("pre0c"):
                        emit_uh(b, c, is_pre=True)
                    if go("pre0d"):
                        emit_scan2(b, c, is_pre=True)
                    if not go("pre0"):
                        break
            if not go("pre"):
                break
        if go("pre"):
            emit_flag()
    if go("m1front0"):
        for b in range(NBLK):
            emit_block_front(xm, b, is_pre=False)
            if SA == "m1front0":
                break
            for c in range(NCB):
                emit_acum(b, c, is_pre=False)
                if go("m1xh"):
                    emit_xh_tm(b, c)
                if go("m1scores"):
                    emit_scores(b, c)
                if go("m1uh"):
                    emit_uh(b, c, is_pre=False)
                if go("m1ea"):
                    emit_ea(b, c)
                if go("m1quad"):
                    emit_ssd_quadratic(b, c)
                if go("m1yoff"):
                    emit_yoff_and_y(b, c)
                if go("m1c0"):
                    emit_scan2(b, c, is_pre=False)
                if not go("m1b0"):
                    break
            if not go("m1b0"):
                break
            emit_outproj_h(b)
            if SA == "m1b0":
                break
        if go("all"):
            emit_m2()

    stn = s_st.n
    axn = s_ax.n
    def fin(sync, stn=stn, axn=axn):
        if stn:
            sync.wait_ge(s_st.h, stn)
        if axn:
            sync.wait_ge(s_ax.h, axn)
    SY(fin)
    barrier()

    es.close()
    return nc, dbg_outs


def _consts():
    sl = np.arange(128)
    triu = (sl[:, None] <= sl[None, :])
    c = {}
    c["triu_f"] = triu.astype(np.float32)
    c["triu_bf"] = triu.astype(ml_dtypes.bfloat16)
    c["maskneg"] = np.where(~triu, np.float32(MASKVAL), np.float32(0)).astype(ml_dtypes.bfloat16)
    c["ident_in"] = np.eye(128, dtype=ml_dtypes.bfloat16)
    c["onesrow_in"] = np.ones((1, 128), dtype=ml_dtypes.bfloat16)
    e = np.zeros((32, 16, 128), dtype=ml_dtypes.bfloat16)
    for hp in range(16):
        e[hp, hp, :] = 1
    c["eind_in"] = e.reshape(32, 2048)
    e2 = np.zeros((NH, NH, PH), dtype=ml_dtypes.bfloat16)
    for h in range(NH):
        e2[h, h, :] = 1
    c["eind2_in"] = e2.reshape(NH, DIN)
    return c


def kernel(L0_attn_x, L0_attn_w_in, L0_attn_conv_w, L0_attn_conv_b, L0_attn_dt_bias,
           L0_attn_a_log, L0_attn_d_skip, L0_attn_norm_w, L0_attn_w_out, L0_attn_w_fc,
           L0_attn_w_proj, L0_ffn_W_fc, L0_ffn_W_proj, _dbg=(), _stop=None, _nc_cache={}):
    x = np.ascontiguousarray(L0_attn_x)
    B, L, _ = x.shape
    key = (tuple(sorted(_dbg)), _stop)
    if key not in _nc_cache:
        _nc_cache[key] = build_program(dbg=_dbg, stop_after=_stop)
    nc, dbg_outs = _nc_cache[key]

    consts = _consts()
    dsk = np.asarray(L0_attn_d_skip, np.float32)
    dskd = np.zeros((128, NFIN * 128), np.float32)
    for t in range(NFIN):
        vals = np.concatenate([np.repeat(dsk[2 * t], 64), np.repeat(dsk[2 * t + 1], 64)])
        blockd = np.zeros((128, 128), np.float32)
        np.fill_diagonal(blockd, vals)
        dskd[:, t * 128:(t + 1) * 128] = blockd

    base = dict(
        w_in=np.ascontiguousarray(L0_attn_w_in, np.float32),
        conv_w=np.ascontiguousarray(L0_attn_conv_w, np.float32),
        conv_b=np.ascontiguousarray(L0_attn_conv_b, np.float32).reshape(CONVD, 1),
        dt_bias=np.ascontiguousarray(L0_attn_dt_bias, np.float32).reshape(NH, 1),
        a_log=np.ascontiguousarray(L0_attn_a_log, np.float32).reshape(NH, 1),
        d_skip_diag=dskd,
        norm_w=np.ascontiguousarray(L0_attn_norm_w, np.float32).reshape(DIN, 1),
        w_out=np.ascontiguousarray(L0_attn_w_out, np.float32),
        w_fc=np.ascontiguousarray(L0_attn_w_fc, np.float32),
        w_proj=np.ascontiguousarray(L0_attn_w_proj, np.float32),
        Wfc=np.ascontiguousarray(np.asarray(L0_ffn_W_fc, np.float32).T),
        Wproj=np.ascontiguousarray(np.asarray(L0_ffn_W_proj, np.float32).T),
        **consts,
    )
    in_maps = []
    for core in range(8):
        b, half = core // 2, core % 2
        m = dict(base)
        m["xm"] = np.ascontiguousarray(x[b, half * TOK:(half + 1) * TOK, :], np.float32)
        m["xp"] = np.ascontiguousarray(x[b, 0:TOK, :], np.float32)
        m["flag_in"] = np.full((128, 1), float(half), np.float32)
        in_maps.append(m)

    res = run_bass_kernel_spmd(nc, in_maps, core_ids=list(range(8)))
    outs = np.zeros((B, L, D), np.float32)
    for core in range(8):
        b, half = core // 2, core % 2
        outs[b, half * TOK:(half + 1) * TOK, :] = res.results[core]["out"]
    if _dbg:
        return outs, [res.results[c] for c in range(8)]
    return outs


if __name__ == "__main__":
    nc, _ = build_program()
    print("built")
